# version 24
# baseline (speedup 1.0000x reference)
import contextlib

import numpy as np
import ml_dtypes

import concourse.bass as bass
import concourse.mybir as mybir
from concourse.bass_utils import run_bass_kernel_spmd

F32 = mybir.dt.float32
BF16 = mybir.dt.bfloat16
I32 = mybir.dt.int32
AF = mybir.ActivationFunctionType
ALU = mybir.AluOpType
AX = mybir.AxisListType

P = 128
R = 4096
NT = R // P
D = 1024
DC = D // P
HALO = 2176
NHT = HALO // P

N_CORES = 8


class Ch:

    def __init__(self, handle):
        self.h = handle
        self.tot = 0

    def base(self):
        return self.tot


def _inc(instr, ch, k=1):
    instr.then_inc(ch.h, k)
    ch.tot += k
    return instr


class Ctx:
    def __init__(self):
        import functools
        self.nc = bass.Bass("TRN2", target_bir_lowering=False, debug=False)
        self.nc.sbuf_tensor = functools.partial(self.nc.sbuf_tensor,
                                                allow_name_mangling=True)
        self.nc.psum_tensor = functools.partial(self.nc.psum_tensor,
                                                allow_name_mangling=True)
        self.es = contextlib.ExitStack()
        self._chs = {}

    def sb(self, name, shape, dt):
        return self.es.enter_context(self.nc.sbuf_tensor(name, shape, dt))

    def ch(self, name):
        if name not in self._chs:
            h = self.es.enter_context(self.nc.semaphore(name))
            self._chs[name] = Ch(h)
        return self._chs[name]

    def chn(self, name, n):
        return [self.ch(f"{name}_{j}") for j in range(n)]

    def close(self):
        self.es.close()


def emit_tern_chain(vector, pairs, bufs, scale_ap, ch, final_ch):
    W = bufs["W"]
    y, p, n = bufs["y"], bufs["c1"], bufs["c2"]
    hv = [slice(0, W), slice(W, 2 * W)]
    tok = {}
    n_ops = len(pairs) * 2 * 4
    cnt = [0]

    def op(emit, out_key, deps):
        v = max([tok[d] for d in deps if d in tok], default=0)
        if v:
            vector.wait_ge(ch.h, v)
        ins = emit()
        cnt[0] += 1
        if cnt[0] == n_ops:
            _inc(ins, final_ch, 1)
            tok[out_key] = 0
        else:
            _inc(ins, ch, 1)
            tok[out_key] = ch.tot
        return ins

    for (srcA, dstA), (srcB, dstB) in pairs:
        src_ = [srcA, srcB]
        dst = [dstA, dstB]
        for s in range(2):
            op(lambda s=s: vector.tensor_scalar(y[:, hv[s]], src_[s], scale_ap, None,
                                                op0=ALU.mult),
               ("y", s), [("o", 0), ("o", 1)])
        for s in range(2):
            op(lambda s=s: vector.tensor_scalar(p[:, hv[s]], y[:, hv[s]], 0.5, None,
                                                op0=ALU.is_gt),
               ("p", s), [("y", s)])
        for s in range(2):
            op(lambda s=s: vector.tensor_scalar(n[:, hv[s]], y[:, hv[s]], -0.5, None,
                                                op0=ALU.is_lt),
               ("n", s), [("y", s)])
        for s in range(2):
            op(lambda s=s: vector.tensor_tensor(dst[s], p[:, hv[s]], n[:, hv[s]],
                                                op=ALU.subtract),
               ("o", s), [("p", s), ("n", s), ("o", 1 - s)])
    return


def emit_consts(ctx):
    nc = ctx.nc
    c = {}
    c["eps"] = ctx.sb("c_eps", [P, 1], F32)
    c["ones_col"] = ctx.sb("c_ones_col", [P, 1], F32)
    c["ones_row"] = ctx.sb("c_ones_row", [1, P], F32)
    with nc.Block(no_gpsimd_drain=True) as block:
        @block.vector
        def _(vector):
            vector.memset(c["eps"][:], 1e-5)
            vector.memset(c["ones_col"][:], 1.0)
            vector.memset(c["ones_row"][:], 1.0)
    return c


def emit_conv_int_weight(ctx, w_h, O, I, wT, ident, tag, src_dt=I32, stash_ap=None):
    nc = ctx.nc
    OT = O // P
    KC = I // P
    lw = ctx.chn(f"lw{tag}", 2)
    cw, twr, wev = ctx.ch(f"cw{tag}"), ctx.ch(f"twr{tag}"), ctx.ch(f"wev{tag}")
    b_lw = [c.base() for c in lw]
    b_cw, b_twr, b_wev = cw.base(), twr.base(), wev.base()
    with contextlib.ExitStack() as es2:
        wint = [es2.enter_context(nc.sbuf_tensor(f"wint{tag}{j}", [P, I], src_dt)) for j in range(2)]
        wbf = [es2.enter_context(nc.sbuf_tensor(f"wbf{tag}{j}", [P, I], BF16)) for j in range(2)]
        trsl = [es2.enter_context(nc.psum_tensor(f"pstr{tag}{j}", [P, P], BF16)) for j in range(4)]
        with nc.Block(no_gpsimd_drain=True) as block:
            @block.sync
            def _(sync):
                for j in range(OT):
                    if j >= 2:
                        sync.wait_ge(cw.h, b_cw + j - 1)
                    _inc(sync.dma_start(out=wint[j % 2][:], in_=w_h[j * P:(j + 1) * P, :]),
                         lw[j % 2], 16)

            @block.scalar
            def _(scalar):
                for j in range(OT):
                    scalar.wait_ge(lw[j % 2].h, b_lw[j % 2] + 16 * (j // 2 + 1))
                    if j >= 2:
                        scalar.wait_ge(twr.h, b_twr + (j - 1) * KC)
                    _inc(scalar.activation(wbf[j % 2][:], wint[j % 2][:], AF.Copy), cw, 1)

            @block.tensor
            def _(tensor):
                for j in range(OT):
                    tensor.wait_ge(cw.h, b_cw + j + 1)
                    for k in range(KC):
                        g = j * KC + k
                        if g >= 4:
                            tensor.wait_ge(wev.h, b_wev + g - 3)
                        ins = tensor.transpose(trsl[g % 4][:],
                                               wbf[j % 2][:, k * P:(k + 1) * P], ident[:])
                        _inc(ins, twr, 1)

            @block.vector
            def _(vector):
                for j in range(OT):
                    for k in range(KC):
                        g = j * KC + k
                        vector.wait_ge(twr.h, b_twr + g + 1)
                        ins = vector.tensor_copy(wT[:, k, j * P:(j + 1) * P], trsl[g % 4][:])
                        _inc(ins, wev, 1)

            @block.gpsimd
            def _(gpsimd):
                if stash_ap is not None:
                    sst = ctx.chn("qssqw", 2)[0]
                    gpsimd.wait_ge(wev.h, b_wev + OT * KC)
                    _inc(gpsimd.dma_start(out=stash_ap, in_=wT[:]), sst, 16)
                    gpsimd.wait_ge(sst.h, sst.tot)


def emit_quant_l1_weight(ctx, w_h, O, I, wT, ident, consts, m_b, tag, stash_ap=None):
    nc = ctx.nc
    OT = O // P
    KC = I // P
    QC = I // D
    lw = ctx.chn(f"qlw{tag}", 2)
    ar, cw, twr, wev = (ctx.ch(f"qar{tag}"), ctx.ch(f"qcw{tag}"),
                        ctx.ch(f"qtw{tag}"), ctx.ch(f"qwe{tag}"))
    sst2 = ctx.chn(f"qss{tag}", 2)
    b_lw = [c.base() for c in lw]
    b_ar, b_cw = ar.base(), cw.base()
    b_twr, b_wev = twr.base(), wev.base()
    b_sst2 = [c.base() for c in sst2]
    to_stash = wT is None
    with contextlib.ExitStack() as es2:
        sb = lambda n, s, d: es2.enter_context(nc.sbuf_tensor(n, s, d))
        wfl = [sb(f"qwf{tag}{j}", [P, I], F32) for j in range(2)]
        absr = [sb(f"qab{tag}{j}", [P, 1], F32) for j in range(2)]
        acc = sb(f"qac{tag}", [P, 1], F32)
        tot = sb(f"qto{tag}", [P, 1], F32)
        sw_b = sb(f"qsw{tag}", [P, 1], F32)
        qy = sb(f"qqy{tag}", [P, D], F32)
        qc1 = sb(f"qq1{tag}", [P, D], F32)
        qc2 = sb(f"qq2{tag}", [P, D], F32)
        wbf = [sb(f"qwb{tag}{j}", [P, I], BF16) for j in range(2)]
        if to_stash:
            stg = [sb(f"qsg{tag}{j}", [P, KC, P], BF16) for j in range(2)]
        trsl = [es2.enter_context(nc.psum_tensor(f"qpt{tag}{j}", [P, P], BF16)) for j in range(4)]
        psum1 = es2.enter_context(nc.psum_tensor(f"qp1{tag}", [P, 1], F32))
        psumb = es2.enter_context(nc.psum_tensor(f"qpb{tag}", [P, 1], F32))

        with nc.Block(no_gpsimd_drain=True) as block:
            @block.sync
            def _(sync):
                for j in range(OT):
                    if j >= 2:
                        sync.wait_ge(ar.h, b_ar + j - 1)
                    _inc(sync.dma_start(out=wfl[j % 2][:], in_=w_h[j * P:(j + 1) * P, :]),
                         lw[j % 2], 16)

            @block.vector
            def _(vector):
                vector.memset(acc[:], 0.0)
                for j in range(OT):
                    vector.wait_ge(lw[j % 2].h, b_lw[j % 2] + 16 * (j // 2 + 1))
                    ins = vector.tensor_reduce(absr[j % 2][:], wfl[j % 2][:], axis=AX.X,
                                               op=ALU.add, apply_absolute_value=True)
                    _inc(ins, ar, 1)
                    vector.wait_ge(ar.h, ar.tot)
                    vector.tensor_tensor(acc[:], acc[:], absr[j % 2][:], op=ALU.add)

        with nc.Block(no_gpsimd_drain=True) as block:
            @block.tensor
            def _(tensor):
                _inc(tensor.matmul(psum1[:1, :], acc[:], consts["ones_col"][:],
                                   start=True, stop=True), twr, 1)

            @block.vector
            def _(vector):
                vector.wait_ge(twr.h, b_twr + 1)
                _inc(vector.tensor_scalar(tot[:1, :], psum1[:1, :], 1.0 / float(O * I), None,
                                          op0=ALU.mult), ar, 1)
                vector.wait_ge(ar.h, ar.tot)
                _inc(vector.tensor_scalar(tot[:1, :], tot[:1, :], 1e-5, None, op0=ALU.max),
                     ar, 1)
                vector.wait_ge(ar.h, ar.tot)
                _inc(vector.reciprocal(sw_b[:1, :], tot[:1, :]), ar, 1)

        with nc.Block(no_gpsimd_drain=True) as block:
            @block.tensor
            def _(tensor):
                tensor.matmul(psum1[:], consts["ones_row"][:], tot[:1, :], start=True, stop=True)
                _inc(tensor.matmul(psumb[:], consts["ones_row"][:], sw_b[:1, :],
                                   start=True, stop=True), twr, 1)

            @block.vector
            def _(vector):
                vector.wait_ge(twr.h, b_twr + 2)
                vector.tensor_copy(m_b[:], psum1[:])
                vector.tensor_copy(sw_b[:], psumb[:])

        b_lw2 = [c.base() for c in lw]
        with nc.Block() as block:
            @block.sync
            def _(sync):
                for j in range(OT):
                    if j >= 2:
                        sync.wait_ge(cw.h, b_cw + j - 1)
                    _inc(sync.dma_start(out=wfl[j % 2][:], in_=w_h[j * P:(j + 1) * P, :]),
                         lw[j % 2], 16)

            @block.vector
            def _(vector):
                qbufs = {"W": 512, "y": qy, "c1": qc1, "c2": qc2}
                for j in range(OT):
                    vector.wait_ge(lw[j % 2].h, b_lw2[j % 2] + 16 * (j // 2 + 1))
                    if j >= 2:
                        vector.wait_ge(twr.h, b_twr + 2 + (j - 1) * KC)
                    if j >= 1:
                        vector.wait_ge(cw.h, b_cw + j)
                    pairs = []
                    for cc in range(QC):
                        s0 = slice(cc * D, cc * D + 512)
                        s1 = slice(cc * D + 512, (cc + 1) * D)
                        pairs.append(((wfl[j % 2][:, s0], wbf[j % 2][:, s0]),
                                      (wfl[j % 2][:, s1], wbf[j % 2][:, s1])))
                    emit_tern_chain(vector, pairs, qbufs, sw_b[:], ar, cw)

            @block.tensor
            def _(tensor):
                for j in range(OT):
                    tensor.wait_ge(cw.h, b_cw + j + 1)
                    for k in range(KC):
                        g = j * KC + k
                        if g >= 4:
                            tensor.wait_ge(wev.h, b_wev + g - 3)
                        ins = tensor.transpose(trsl[g % 4][:],
                                               wbf[j % 2][:, k * P:(k + 1) * P], ident[:])
                        _inc(ins, twr, 1)

            @block.scalar
            def _(scalar):
                for j in range(OT):
                    if to_stash and j >= 2:
                        scalar.wait_ge(sst2[j % 2].h, b_sst2[j % 2] + 16 * (j // 2))
                    for k in range(KC):
                        g = j * KC + k
                        scalar.wait_ge(twr.h, b_twr + 2 + g + 1)
                        dst = stg[j % 2][:, k, :] if to_stash else wT[:, k, j * P:(j + 1) * P]
                        ins = scalar.activation(dst, trsl[g % 4][:], AF.Copy)
                        _inc(ins, wev, 1)

            @block.gpsimd
            def _(gpsimd):
                if to_stash:
                    for j in range(OT):
                        gpsimd.wait_ge(wev.h, b_wev + (j + 1) * KC)
                        _inc(gpsimd.dma_start(out=stash_ap[:, :, j * P:(j + 1) * P],
                                              in_=stg[j % 2][:]), sst2[j % 2], 16)
                    for q in range(2):
                        gpsimd.wait_ge(sst2[q].h, sst2[q].tot)
                elif stash_ap is not None:
                    gpsimd.wait_ge(wev.h, b_wev + OT * KC)
                    _inc(gpsimd.dma_start(out=stash_ap, in_=wT[:]), sst2[0], 16)
                    gpsimd.wait_ge(sst2[0].h, sst2[0].tot)


def emit_ffn_passA(ctx, src_tiles, h_scr, wgT, wuT, H, consts, quant, qd, tag):
    ntot = len(src_tiles)
    nc = ctx.nc
    NB = H // 512
    NHC = H // 512
    NCC = 2 + NHC
    ident = consts["ident"]
    ld3 = ctx.chn(f"ldA{tag}", 3)
    xq = ctx.ch(f"xqA{tag}")
    sq = ctx.ch(f"sqA{tag}")
    agg = ctx.ch(f"aggA{tag}")
    trr = ctx.ch(f"trrA{tag}")
    tev = ctx.ch(f"tevA{tag}")
    mg = ctx.ch(f"mgA{tag}")
    mu = ctx.ch(f"muA{tag}")
    sil = ctx.ch(f"silA{tag}")
    hh = ctx.ch(f"hhA{tag}")
    qv = ctx.ch(f"hqA{tag}")
    qa = ctx.ch(f"qaA{tag}")
    qp = ctx.ch(f"qpA{tag}")
    sth2 = ctx.chn(f"sthA{tag}", 2)
    vr = ctx.ch(f"vrA{tag}")
    cs = ctx.ch(f"csA{tag}")
    b_ld3 = [c.base() for c in ld3]
    b_xq, b_sq, b_agg = xq.base(), sq.base(), agg.base()
    b_trr, b_tev, b_mg, b_mu = trr.base(), tev.base(), mg.base(), mu.base()
    b_sil, b_hh = sil.base(), hh.base()
    b_qv, b_qa, b_qp = qv.base(), qa.base(), qp.base()
    b_sth2 = [c.base() for c in sth2]
    b_cs = cs.base()
    HV = [slice(0, 512), slice(512, 1024), slice(1024, 1536)]
    NSL = 3
    qidx = {}
    marks = {}
    cctr = [0]

    with contextlib.ExitStack() as es2:
        sb = lambda n, s, d: es2.enter_context(nc.sbuf_tensor(n, s, d))
        ps = lambda n, s, d: es2.enter_context(nc.psum_tensor(n, s, d))
        xbuf = [sb(f"xA{tag}{j}", [P, D], F32) for j in range(3)]
        stats = [sb(f"stA{tag}{j}", [P, 12], F32) for j in range(2)]
        mv = [sb(f"mvA{tag}{j}", [P, 2], F32) for j in range(2)]
        sqv = [sb(f"sqvA{tag}{j}", [P, 1], F32) for j in range(2)]
        rstd = [sb(f"rsA{tag}{j}", [P, 1], F32) for j in range(2)]
        xop = [sb(f"xoA{tag}{j}", [P, D], BF16) for j in range(2)]
        xT = [sb(f"xTA{tag}{j}", [P, DC, P], BF16) for j in range(2)]
        sg = [sb(f"sgA{tag}{j}", [P, 512], F32) for j in range(2)]
        hsb = [sb(f"hsA{tag}{j}", [P, H], BF16) for j in range(2)]
        if quant:
            xn = [sb(f"xnA{tag}{j}", [P, D], F32) for j in range(2)]
            am = [sb(f"amA{tag}{j}", [P, 1], F32) for j in range(2)]
            sa = [sb(f"saA{tag}{j}", [P, 1], F32) for j in range(2)]
            csg = [sb(f"csgA{tag}{j}", [P, 1], F32) for j in range(2)]
            csu_all = sb(f"csuA{tag}", [P, ntot], F32)
            ham = [sb(f"hamA{tag}{j}", [P, 1], F32) for j in range(2)]
            hsa = [sb(f"hsaA{tag}{j}", [P, 1], F32) for j in range(2)]
            qy = sb(f"qyA{tag}", [P, 3 * 512], F32)
            qyi = sb(f"qyiA{tag}", [P, 3 * 512], I32)
            qc1 = sb(f"qc1A{tag}", [P, 3 * 512], F32)
            qc2 = sb(f"qc2A{tag}", [P, 3 * 512], F32)
        trsl = [ps(f"ptA{tag}{j}", [P, P], BF16) for j in range(4)]
        psg = [ps(f"pgA{tag}{j}", [P, 512], F32) for j in range(2)]
        psu = [ps(f"puA{tag}{j}", [P, 512], F32) for j in range(2)]

        def chain_head(vector, src_half, scale_ap, vrk):
            c = cctr[0]
            cctr[0] += 1
            qidx[(c, 'src')] = src_half
            qidx[(c, 'scl')] = scale_ap
            qidx[(c, 'vrk')] = vrk
            return c

        def chain_r(vector, c):
            h = HV[c % NSL]
            vector.wait_ge(qa.h, b_qa + c + 1)
            _inc(vector.scalar_tensor_tensor(qy[:, h], qidx[(c, 'src')],
                                             qidx[(c, 'scl')], qyi[:, h],
                                             op0=ALU.mult, op1=ALU.subtract), qv, 1)
            qidx[(c, 'r')] = qv.tot

        def chain_ccc(vector, c, dst_half):
            h = HV[c % NSL]
            if c >= NSL:
                vector.wait_ge(qp.h, b_qp + (c - NSL + 1))
            vector.wait_ge(qv.h, qidx[(c, 'r')])
            _inc(vector.tensor_scalar(qc1[:, h], qy[:, h], 0.5, None, op0=ALU.is_gt),
                 qv, 1)
            _inc(vector.tensor_scalar(qc2[:, h], qy[:, h], -0.5, None, op0=ALU.is_lt),
                 qv, 1)
            qidx[(c, 'cc')] = qv.tot
            vector.wait_ge(qv.h, qidx[(c, 'cc')])
            _inc(vector.tensor_tensor(qc1[:, h], qyi[:, h], qc1[:, h], op=ALU.add), qv, 1)
            qidx[(c, 't2a')] = qv.tot
            qidx[(c, 'dst')] = dst_half

        def emit_ln(vector, i):
            pr = i % 2
            vector.wait_ge(ld3[i % 3].h, b_ld3[i % 3] + 16 * (i // 3 + 1))
            if i >= 2:
                vector.wait_ge(sq.h, b_sq + i - 1)
                vector.wait_ge(trr.h, b_trr + DC * (i - 1))
                if quant:
                    vector.wait_ge(sil.h, b_sil + (i - 1) * NB)
            vector.bn_stats(stats[pr][:, 0:6], xbuf[i % 3][:, 0:512])
            _inc(vector.bn_stats(stats[pr][:, 6:12], xbuf[i % 3][:, 512:1024]), vr, 1)
            vector.wait_ge(vr.h, vr.tot)
            _inc(vector.bn_aggr(mv[pr][:], stats[pr][:]), agg, 1)
            vector.wait_ge(agg.h, agg.tot)
            vector.wait_ge(sq.h, b_sq + i + 1)
            _inc(vector.reciprocal(rstd[pr][:], sqv[pr][:]), vr, 1)
            vector.wait_ge(vr.h, vr.tot)
            if not quant:
                ins = vector.tensor_scalar(xop[pr][:], xbuf[i % 3][:], mv[pr][:, 0:1],
                                           rstd[pr][:], op0=ALU.subtract, op1=ALU.mult)
                _inc(ins, xq, 1)
            else:
                if i >= 2:
                    vector.wait_ge(qa.h, b_qa + marks[i - 2]['xc'][1] + 1)
                _inc(vector.tensor_scalar(xn[pr][:], xbuf[i % 3][:], mv[pr][:, 0:1],
                                          rstd[pr][:], op0=ALU.subtract, op1=ALU.mult), vr, 1)
                vector.wait_ge(vr.h, vr.tot)
                _inc(vector.tensor_reduce(am[pr][:], xn[pr][:], axis=AX.X, op=ALU.max,
                                          apply_absolute_value=True), vr, 1)
                vector.wait_ge(vr.h, vr.tot)
                _inc(vector.tensor_scalar(am[pr][:], am[pr][:], 1e-5, None, op0=ALU.max),
                     vr, 1)
                vector.wait_ge(vr.h, vr.tot)
                _inc(vector.reciprocal(sa[pr][:], am[pr][:]), vr, 1)
                vector.wait_ge(vr.h, vr.tot)
                _inc(vector.tensor_scalar(sa[pr][:], sa[pr][:], 127.0, None, op0=ALU.mult),
                     vr, 1)
                _inc(vector.tensor_scalar(csg[pr][:], am[pr][:], 1.0 / 127.0, qd["m_g"][:],
                                          op0=ALU.mult, op1=ALU.mult), cs, 1)
                _inc(vector.tensor_scalar(csu_all[:, i:i + 1], am[pr][:], 1.0 / 127.0,
                                          qd["m_u"][:], op0=ALU.mult, op1=ALU.mult), cs, 1)
                vector.wait_ge(vr.h, vr.tot)
                vrk = vr.tot
                ca = chain_head(vector, xn[pr][:, HV[0]], sa[pr][:], vrk)
                cb = chain_head(vector, xn[pr][:, HV[1]], sa[pr][:], vrk)
                chain_r(vector, ca)
                chain_r(vector, cb)
                chain_ccc(vector, ca, xop[pr][:, HV[0]])
                chain_ccc(vector, cb, xop[pr][:, HV[1]])
                marks[i] = {'xc': [ca, cb], 'hc': []}

        def emit_hquant_prologue(vector, i):
            pr = i % 2
            vector.wait_ge(hh.h, b_hh + (i + 1) * NB)
            if i >= 2:
                vector.wait_ge(qa.h, b_qa + marks[i - 2]['hc'][-1] + 1)
            _inc(vector.tensor_reduce(ham[pr][:], hsb[pr][:], axis=AX.X, op=ALU.max,
                                      apply_absolute_value=True), vr, 1)
            vector.wait_ge(vr.h, vr.tot)
            _inc(vector.tensor_scalar(ham[pr][:], ham[pr][:], 1e-5, None, op0=ALU.max), vr, 1)
            vector.wait_ge(vr.h, vr.tot)
            _inc(vector.reciprocal(hsa[pr][:], ham[pr][:]), vr, 1)
            vector.wait_ge(vr.h, vr.tot)
            _inc(vector.tensor_scalar(hsa[pr][:], hsa[pr][:], 127.0, None, op0=ALU.mult), vr, 1)
            _inc(vector.tensor_scalar(qd["inv_sh_all"][:, i:i + 1], ham[pr][:], 1.0 / 127.0,
                                      csu_all[:, i:i + 1], op0=ALU.mult, op1=ALU.mult), vr, 1)
            vector.wait_ge(vr.h, vr.tot)
            marks[i]['hvrk'] = vr.tot
            marks[i]['hc'] = []

        def emit_hquant_step(vector, i, hc):
            pr = i % 2
            cs_list = marks[i]['hc']
            half = hsb[pr][:, hc * 512:(hc + 1) * 512]
            cs_list.append(chain_head(vector, half, hsa[pr][:], marks[i]['hvrk']))
            if hc >= 1:
                chain_r(vector, cs_list[hc - 1])
            if hc >= 2:
                j = hc - 2
                chain_ccc(vector, cs_list[j], hsb[pr][:, j * 512:(j + 1) * 512])
            if hc == NHC - 1:
                chain_r(vector, cs_list[NHC - 1])
                for j in (NHC - 2, NHC - 1):
                    chain_ccc(vector, cs_list[j], hsb[pr][:, j * 512:(j + 1) * 512])

        with nc.Block() as block:
            @block.vector
            def _(vector):
                emit_ln(vector, 0)
                for i in range(ntot):
                    pr = i % 2
                    if i >= 2:
                        vector.wait_ge(mg.h, b_mg + (i - 1) * NB)
                        vector.wait_ge(mu.h, b_mu + (i - 1) * NB)
                    for k in range(DC):
                        g = i * DC + k
                        vector.wait_ge(trr.h, b_trr + g + 1)
                        ins = vector.tensor_copy(xT[pr][:, k, :], trsl[g % 4][:])
                        _inc(ins, tev, 1)
                    if quant and i >= 1:
                        emit_hquant_prologue(vector, i - 1)
                    if i + 1 < ntot:
                        emit_ln(vector, i + 1)
                    if i >= 2:
                        vector.wait_ge(sth2[i % 2].h, b_sth2[i % 2] + 16 * (i // 2))
                    for nb in range(NB):
                        G = i * NB + nb
                        vector.wait_ge(sil.h, b_sil + G + 1)
                        vector.wait_ge(mu.h, b_mu + G + 1)
                        sl = slice(nb * 512, (nb + 1) * 512)
                        ins = vector.tensor_tensor(hsb[pr][:, sl], sg[nb % 2][:],
                                                   psu[nb % 2][:], op=ALU.mult)
                        _inc(ins, hh, 1)
                        if quant and i >= 1 and nb < NHC:
                            emit_hquant_step(vector, i - 1, nb)
                if quant:
                    emit_hquant_prologue(vector, ntot - 1)
                    for hc in range(NHC):
                        emit_hquant_step(vector, ntot - 1, hc)

            @block.sync
            def _(sync):
                for i in range(ntot):
                    if i >= 3:
                        if quant:
                            sync.wait_ge(qp.h, b_qp + marks[i - 3]['xc'][1] + 1)
                        else:
                            sync.wait_ge(xq.h, b_xq + i - 2)
                    _inc(sync.dma_start(out=xbuf[i % 3][:], in_=src_tiles[i]), ld3[i % 3], 16)

            @block.scalar
            def _(scalar):
                def emit_sqrt(i):
                    pr = i % 2
                    scalar.wait_ge(agg.h, b_agg + i + 1)
                    if i >= 2:
                        if quant:
                            scalar.wait_ge(qp.h, b_qp + marks[i - 2]['xc'][1] + 1)
                        else:
                            scalar.wait_ge(xq.h, b_xq + i - 1)
                    _inc(scalar.activation(sqv[pr][:], mv[pr][:, 1:2], AF.Sqrt,
                                           bias=consts["eps"][:]), sq, 1)

                def act_chunk(c):
                    h = HV[c % NSL]
                    scalar.wait_ge(vr.h, qidx[(c, 'vrk')])
                    if c >= NSL:
                        scalar.wait_ge(qv.h, qidx[(c - NSL, 't2a')])
                    _inc(scalar.activation(qyi[:, h], qidx[(c, 'src')], AF.Copy,
                                           scale=qidx[(c, 'scl')]), qa, 1)

                emit_sqrt(0)
                if quant:
                    for c in marks[0]['xc']:
                        act_chunk(c)
                for i in range(ntot):
                    pr = i % 2
                    if i + 1 < ntot:
                        emit_sqrt(i + 1)
                        if quant:
                            for c in marks[i + 1]['xc']:
                                act_chunk(c)
                    if quant:
                        scalar.wait_ge(cs.h, b_cs + 2 * i + 1)
                    for nb in range(NB):
                        G = i * NB + nb
                        scalar.wait_ge(mg.h, b_mg + G + 1)
                        if quant:
                            ins = scalar.activation(sg[nb % 2][:], psg[nb % 2][:], AF.Silu,
                                                    scale=csg[pr][:])
                        else:
                            ins = scalar.activation(sg[nb % 2][:], psg[nb % 2][:], AF.Silu,
                                                    scale=1.0 / 127.0)
                        _inc(ins, sil, 1)
                        if quant and i >= 1 and nb < NHC:
                            act_chunk(marks[i - 1]['hc'][nb])
                if quant:
                    for c in marks[ntot - 1]['hc']:
                        act_chunk(c)

            @block.tensor
            def _(tensor):
                for i in range(ntot):
                    pr = i % 2
                    if quant:
                        tensor.wait_ge(qp.h, b_qp + marks[i]['xc'][1] + 1)
                    else:
                        tensor.wait_ge(xq.h, b_xq + i + 1)
                    for k in range(DC):
                        g = i * DC + k
                        if g >= 4:
                            tensor.wait_ge(tev.h, b_tev + g - 3)
                        ins = tensor.transpose(trsl[g % 4][:],
                                               xop[pr][:, k * P:(k + 1) * P], ident[:])
                        _inc(ins, trr, 1)
                    tensor.wait_ge(tev.h, b_tev + DC * (i + 1))
                    for nb in range(NB):
                        G = i * NB + nb
                        if G >= 2:
                            tensor.wait_ge(hh.h, b_hh + G - 1)
                        for k in range(DC):
                            ins = tensor.matmul(psg[nb % 2][:], xT[pr][:, k, :],
                                                wgT[:, k, nb * 512:(nb + 1) * 512],
                                                start=(k == 0), stop=(k == DC - 1))
                            if k == DC - 1:
                                _inc(ins, mg, 1)
                        for k in range(DC):
                            ins = tensor.matmul(psu[nb % 2][:], xT[pr][:, k, :],
                                                wuT[:, k, nb * 512:(nb + 1) * 512],
                                                start=(k == 0), stop=(k == DC - 1))
                            if k == DC - 1:
                                _inc(ins, mu, 1)

            @block.gpsimd
            def _(gpsimd):
                def pool_chunk(c):
                    h = HV[c % NSL]
                    gpsimd.wait_ge(qv.h, qidx[(c, 't2a')])
                    _inc(gpsimd.tensor_tensor(qidx[(c, 'dst')], qc1[:, h], qc2[:, h],
                                              op=ALU.subtract), qp, 1)

                if quant:
                    for c in marks[0]['xc']:
                        pool_chunk(c)

                def store_tile(i):
                    pr = i % 2
                    if quant:
                        gpsimd.wait_ge(qp.h, b_qp + marks[i]['hc'][-1] + 1)
                    gpsimd.wait_ge(hh.h, b_hh + (i + 1) * NB)
                    _inc(gpsimd.dma_start(out=h_scr[i * P:(i + 1) * P, :], in_=hsb[pr][:]),
                         sth2[i % 2], 16)

                for i in range(ntot):
                    if quant and i + 1 < ntot:
                        for c in marks[i + 1]['xc']:
                            pool_chunk(c)
                    if quant and i >= 1:
                        for c in marks[i - 1]['hc']:
                            pool_chunk(c)
                        store_tile(i - 1)
                    if not quant:
                        store_tile(i)
                if quant:
                    for c in marks[ntot - 1]['hc']:
                        pool_chunk(c)
                    store_tile(ntot - 1)
                for q in range(2):
                    gpsimd.wait_ge(sth2[q].h, sth2[q].tot)


def emit_ffn_passB(ctx, xd_tiles, h_scr, wdT, H, consts, quant, qd, tag,
                  bf16_copy_ap=None):
    ntot = len(xd_tiles)
    nc = ctx.nc
    HC = H // P
    NBAT = (HC + 7) // 8
    ident = consts["ident"]
    ldh2 = ctx.chn(f"ldhB{tag}", 2)
    ldx2 = ctx.chn(f"ldxB{tag}", 2)
    trr = ctx.ch(f"trrB{tag}")
    tev = ctx.ch(f"tevB{tag}")
    md = ctx.ch(f"mdB{tag}")
    epi = ctx.ch(f"epiB{tag}")
    cbf = ctx.ch(f"cbfB{tag}")
    st2 = ctx.chn(f"stB{tag}", 2)
    stb2 = ctx.chn(f"stbB{tag}", 2)
    vrB = ctx.ch(f"vrB{tag}")
    b_vrB = vrB.base()
    b_ldh2 = [c.base() for c in ldh2]
    b_ldx2 = [c.base() for c in ldx2]
    b_trr, b_tev, b_md = trr.base(), tev.base(), md.base()
    b_epi, b_cbf = epi.base(), cbf.base()
    b_st2 = [c.base() for c in st2]
    b_stb2 = [c.base() for c in stb2]

    with contextlib.ExitStack() as es2:
        sb = lambda n, s, d: es2.enter_context(nc.sbuf_tensor(n, s, d))
        ps = lambda n, s, d: es2.enter_context(nc.psum_tensor(n, s, d))
        hbuf = [sb(f"hB{tag}{j}", [P, H], BF16) for j in range(2)]
        xbuf = [sb(f"xB{tag}{j}", [P, D], F32) for j in range(2)]
        hT = [sb(f"hTB{tag}{j}", [P, HC, P], BF16) for j in range(2)]
        obuf = [sb(f"oB{tag}{j}", [P, D], F32) for j in range(2)]
        obf = [sb(f"obB{tag}{j}", [P, D], BF16) for j in range(2)] if bf16_copy_ap is not None else None
        csd = [sb(f"csdB{tag}{j}", [P, 1], F32) for j in range(2)] if quant else None
        trsl = [ps(f"ptB{tag}{j}", [P, P], BF16) for j in range(4)]
        psd = [ps(f"pdB{tag}{j}", [P, 512], F32) for j in range(4)]

        with nc.Block() as block:
            @block.sync
            def _(sync):
                for i in range(ntot):
                    if i >= 2:
                        sync.wait_ge(trr.h, b_trr + (i - 1) * HC)
                        sync.wait_ge(epi.h, b_epi + 2 * (i - 1))
                    _inc(sync.dma_start(out=hbuf[i % 2][:], in_=h_scr[i * P:(i + 1) * P, :]),
                         ldh2[i % 2], 16)
                    _inc(sync.dma_start(out=xbuf[i % 2][:], in_=xd_tiles[i][0]),
                         ldx2[i % 2], 16)

            @block.tensor
            def _(tensor):
                for i in range(ntot):
                    pr = i % 2
                    tensor.wait_ge(ldh2[pr].h, b_ldh2[pr] + 16 * (i // 2 + 1))
                    for k in range(HC):
                        g = i * HC + k
                        if g >= 4:
                            tensor.wait_ge(tev.h, b_tev + g - 3)
                        ins = tensor.transpose(trsl[g % 4][:],
                                               hbuf[pr][:, k * P:(k + 1) * P], ident[:])
                        _inc(ins, trr, 1)
                    tensor.wait_ge(tev.h, b_tev + (i + 1) * HC)
                    for nb in range(2):
                        if i >= 2:
                            tensor.wait_ge(epi.h, b_epi + 2 * (i - 1))
                        for k in range(HC):
                            ins = tensor.matmul(psd[2 * pr + nb][:], hT[pr][:, k, :],
                                                wdT[:, k, nb * 512:(nb + 1) * 512],
                                                start=(k == 0), stop=(k == HC - 1))
                            if k == HC - 1:
                                _inc(ins, md, 1)

            @block.vector
            def _(vector):
                for i in range(ntot):
                    pr = i % 2
                    if quant:
                        if i >= 2:
                            vector.wait_ge(epi.h, b_epi + 2 * (i - 1))
                        _inc(vector.tensor_scalar(csd[pr][:], qd["inv_sh_all"][:, i:i + 1], 0.5,
                                                  qd["m_d"][:], op0=ALU.mult, op1=ALU.mult),
                             vrB, 1)
                    if i >= 2:
                        vector.wait_ge(md.h, b_md + 2 * (i - 1))
                    for k in range(HC):
                        g = i * HC + k
                        vector.wait_ge(trr.h, b_trr + g + 1)
                        ins = vector.tensor_copy(hT[pr][:, k, :], trsl[g % 4][:])
                        _inc(ins, tev, 1)
                    if i >= 2:
                        vector.wait_ge(st2[pr].h, b_st2[pr] + 16 * (i // 2))
                    vector.wait_ge(ldx2[pr].h, b_ldx2[pr] + 16 * (i // 2 + 1))
                    if quant:
                        vector.wait_ge(vrB.h, b_vrB + i + 1)
                    for nb in range(2):
                        vector.wait_ge(md.h, b_md + 2 * i + nb + 1)
                        sl = slice(nb * 512, (nb + 1) * 512)
                        if quant:
                            ins = vector.scalar_tensor_tensor(obuf[pr][:, sl],
                                                              psd[2 * pr + nb][:], csd[pr][:],
                                                              xbuf[pr][:, sl],
                                                              op0=ALU.mult, op1=ALU.add)
                        else:
                            ins = vector.scalar_tensor_tensor(obuf[pr][:, sl],
                                                              psd[2 * pr + nb][:],
                                                              0.5 / (127.0 * 127.0),
                                                              xbuf[pr][:, sl],
                                                              op0=ALU.mult, op1=ALU.add)
                        _inc(ins, epi, 1)

            @block.scalar
            def _(scalar):
                if bf16_copy_ap is not None:
                    for i in range(ntot):
                        pr = i % 2
                        scalar.wait_ge(epi.h, b_epi + 2 * (i + 1))
                        if i >= 2:
                            scalar.wait_ge(stb2[pr].h, b_stb2[pr] + 16 * (i // 2))
                        _inc(scalar.activation(obf[pr][:], obuf[pr][:], AF.Copy), cbf, 1)

            @block.gpsimd
            def _(gpsimd):
                for i in range(ntot):
                    pr = i % 2
                    gpsimd.wait_ge(epi.h, b_epi + 2 * (i + 1))
                    _inc(gpsimd.dma_start(out=xd_tiles[i][1], in_=obuf[pr][:]),
                         st2[pr], 16)
                    if bf16_copy_ap is not None:
                        gpsimd.wait_ge(cbf.h, b_cbf + i + 1)
                        _inc(gpsimd.dma_start(out=bf16_copy_ap[i * P:(i + 1) * P, :],
                                              in_=obf[pr][:]), stb2[pr], 16)
                for q in range(2):
                    gpsimd.wait_ge(st2[q].h, st2[q].tot)
                    if bf16_copy_ap is not None:
                        gpsimd.wait_ge(stb2[q].h, stb2[q].tot)




def emit_attn0(ctx, src_ap, dst_ap, halo_ap, halo_dt, cos_h, sin_h, cosk_h, sink_h,
               Wq1_h, Wq2_h, Wk1_h, Wk2_h, Wv_h, Wout_h, bout_h,
               lq1_h, lk1_h, lq2_h, lk2_h, consts, tag="a0"):
    nc = ctx.nc
    ident = consts["ident"]
    NW = 16
    U = NW * 2 * 16
    es0 = contextlib.ExitStack()
    sb = lambda n, s, d: es0.enter_context(nc.sbuf_tensor(n, s, d))

    q1T = sb(f"q1T{tag}", [P, DC, HALO], BF16)
    q2T = sb(f"q2T{tag}", [P, DC, HALO], BF16)
    k1T = sb(f"k1T{tag}", [P, HALO], BF16)
    k2T = sb(f"k2T{tag}", [P, HALO], BF16)
    v_sb = sb(f"v{tag}", [P, NHT, P], BF16)
    neglam = sb(f"nlam{tag}", [P, 1], F32)

    with contextlib.ExitStack() as esw:
        wsb = lambda n, s, d: esw.enter_context(nc.sbuf_tensor(n, s, d))
        wps = lambda n, s, d: esw.enter_context(nc.psum_tensor(n, s, d))
        Wq1 = wsb(f"Wq1{tag}", [P, DC, D], BF16)
        Wq2 = wsb(f"Wq2{tag}", [P, DC, D], BF16)
        Wkv = wsb(f"Wkv{tag}", [P, DC, 384], BF16)
        wf = [wsb(f"wf{tag}{j}", [P, D], F32) for j in range(2)]
        lvec = wsb(f"lv{tag}", [1, 256], F32)
        lsc = wsb(f"lsc{tag}", [1, 8], F32)
        lw2 = ctx.chn(f"lw{tag}", 2)
        cwc = ctx.ch(f"cw{tag}")
        lml = ctx.ch(f"lml{tag}")
        lmv = ctx.ch(f"lmv{tag}")
        lme = ctx.ch(f"lme{tag}")
        b_lw2 = [c.base() for c in lw2]
        b_cw, b_lml = cwc.base(), lml.base()
        b_lmv, b_lme = lmv.base(), lme.base()
        with nc.Block(no_gpsimd_drain=True) as block:
            @block.sync
            def _(sync):
                j = 0
                for src_h in (Wq1_h, Wq2_h):
                    for k in range(DC):
                        if j >= 2:
                            sync.wait_ge(cwc.h, b_cw + j - 1)
                        _inc(sync.dma_start(out=wf[j % 2][:],
                                            in_=src_h[k * P:(k + 1) * P, :]), lw2[j % 2], 16)
                        j += 1
                for k in range(DC):
                    if j >= 2:
                        sync.wait_ge(cwc.h, b_cw + j - 1)
                    _inc(sync.dma_start(out=wf[j % 2][:, 0:128],
                                        in_=Wk1_h[k * P:(k + 1) * P, :]), lw2[j % 2], 16)
                    _inc(sync.dma_start(out=wf[j % 2][:, 128:256],
                                        in_=Wk2_h[k * P:(k + 1) * P, :]), lw2[j % 2], 16)
                    _inc(sync.dma_start(out=wf[j % 2][:, 256:384],
                                        in_=Wv_h[k * P:(k + 1) * P, :]), lw2[j % 2], 16)
                    j += 1
                for li, lh in enumerate((lq1_h, lk1_h, lq2_h, lk2_h)):
                    _inc(sync.dma_start(out=lvec[:, li * 64:(li + 1) * 64],
                                        in_=lh.ap().rearrange("(one d) -> one d", one=1)),
                         lml, 16)

            @block.scalar
            def _(scalar):
                j = 0
                nld = [0, 0]
                for dst_T in (Wq1, Wq2):
                    for k in range(DC):
                        nld[j % 2] += 16
                        scalar.wait_ge(lw2[j % 2].h, b_lw2[j % 2] + nld[j % 2])
                        if j >= 4:
                            pass
                        _inc(scalar.activation(dst_T[:, k, :], wf[j % 2][:], AF.Copy), cwc, 1)
                        j += 1
                for k in range(DC):
                    nld[j % 2] += 48
                    scalar.wait_ge(lw2[j % 2].h, b_lw2[j % 2] + nld[j % 2])
                    _inc(scalar.activation(Wkv[:, k, :], wf[j % 2][:, 0:384], AF.Copy), cwc, 1)
                    j += 1
                scalar.wait_ge(lmv.h, b_lmv + 4)
                _inc(scalar.activation(lsc[:, 2:3], lsc[:, 0:1], AF.Exp), lme, 1)
                _inc(scalar.activation(lsc[:, 3:4], lsc[:, 1:2], AF.Exp), lme, 1)

            @block.vector
            def _(vector):
                vector.wait_ge(lml.h, b_lml + 64)
                _inc(vector.tensor_tensor(lvec[:, 0:64], lvec[:, 0:64], lvec[:, 64:128],
                                          op=ALU.mult), lmv, 1)
                vector.wait_ge(lmv.h, lmv.tot)
                _inc(vector.tensor_reduce(lsc[:, 0:1], lvec[:, 0:64], axis=AX.X, op=ALU.add),
                     lmv, 1)
                vector.wait_ge(lmv.h, lmv.tot)
                _inc(vector.tensor_tensor(lvec[:, 128:192], lvec[:, 128:192],
                                          lvec[:, 192:256], op=ALU.mult), lmv, 1)
                vector.wait_ge(lmv.h, lmv.tot)
                _inc(vector.tensor_reduce(lsc[:, 1:2], lvec[:, 128:192], axis=AX.X,
                                          op=ALU.add), lmv, 1)
                vector.wait_ge(lme.h, b_lme + 2)
                _inc(vector.tensor_tensor(lsc[:, 4:5], lsc[:, 2:3], lsc[:, 3:4],
                                          op=ALU.subtract), lmv, 1)
                vector.wait_ge(lmv.h, lmv.tot)
                _inc(vector.tensor_scalar(lsc[:, 4:5], lsc[:, 4:5], 0.8, None, op0=ALU.add),
                     lmv, 1)
                vector.wait_ge(lmv.h, lmv.tot)
                _inc(vector.tensor_scalar(lsc[:, 4:5], lsc[:, 4:5], 0.9, None, op0=ALU.min),
                     lmv, 1)
                vector.wait_ge(lmv.h, lmv.tot)
                _inc(vector.tensor_scalar(lsc[:, 4:5], lsc[:, 4:5], 0.1, None, op0=ALU.max),
                     lmv, 1)
                vector.wait_ge(lmv.h, lmv.tot)
                _inc(vector.tensor_scalar(lsc[:, 5:6], lsc[:, 4:5], -1.0, None,
                                          op0=ALU.mult), lmv, 1)

        with contextlib.ExitStack() as esl:
            psl = esl.enter_context(nc.psum_tensor(f"psl{tag}", [P, 1], F32))
            with nc.Block(no_gpsimd_drain=True) as block:
                @block.tensor
                def _(tensor):
                    _inc(tensor.matmul(psl[:], consts["ones_row"][:], lsc[:, 5:6],
                                       start=True, stop=True), lme, 1)

                @block.vector
                def _(vector):
                    vector.wait_ge(lme.h, b_lme + 3)
                    vector.tensor_copy(neglam[:], psl[:])

        xhb = [wsb(f"xh{tag}{j}", [P, D], halo_dt) for j in range(3)]
        cosb = [wsb(f"cos{tag}{j}", [P, 512], F32) for j in range(2)]
        sinb = [wsb(f"sin{tag}{j}", [P, 512], F32) for j in range(2)]
        coskb = [wsb(f"cosk{tag}{j}", [P, 128], F32) for j in range(2)]
        sinkb = [wsb(f"sink{tag}{j}", [P, 128], F32) for j in range(2)]
        stats = [wsb(f"st{tag}{j}", [P, 12], F32) for j in range(2)]
        mvt = [wsb(f"mv{tag}{j}", [P, 2], F32) for j in range(2)]
        sqvt = [wsb(f"sqv{tag}{j}", [P, 1], F32) for j in range(2)]
        rstd = [wsb(f"rstd{tag}{j}", [P, 1], F32) for j in range(2)]
        xnb = [wsb(f"xn{tag}{j}", [P, D], BF16) for j in range(2)]
        lnT = [wsb(f"lnT{tag}{j}", [P, DC, P], BF16) for j in range(2)]
        q1r = [wsb(f"q1r{tag}{j}", [P, D], BF16) for j in range(2)]
        q2r = [wsb(f"q2r{tag}{j}", [P, D], BF16) for j in range(2)]
        krb = [wsb(f"kr{tag}{j}", [P, 256], BF16) for j in range(2)]
        rt1 = wsb(f"rt1{tag}", [P, 512], F32)
        rt2 = wsb(f"rt2{tag}", [P, 512], F32)
        trsl = [wps(f"tr{tag}{j}", [P, P], BF16) for j in range(3)]
        psq1 = [wps(f"pq1{tag}{j}", [P, 512], F32) for j in range(2)]
        psq2 = [wps(f"pq2{tag}{j}", [P, 512], F32) for j in range(2)]
        pskv = wps(f"pkv{tag}", [P, 384], F32)

        ldx3 = ctx.chn(f"ldx{tag}", 3)
        ldc2 = ctx.chn(f"ldc{tag}", 2)
        sq = ctx.ch(f"sq{tag}")
        agg = ctx.ch(f"agg{tag}")
        vr = ctx.ch(f"vr{tag}")
        xq = ctx.ch(f"xq{tag}")
        trr = ctx.ch(f"trr{tag}")
        tev = ctx.ch(f"tev{tag}")
        rte = ctx.ch(f"rte{tag}")
        mq1 = ctx.ch(f"mq1{tag}")
        mq2 = ctx.ch(f"mq2{tag}")
        mkv = ctx.ch(f"mkv{tag}")
        ropd = ctx.ch(f"ropd{tag}")
        vev = ctx.ch(f"vev{tag}")
        b_ldx3 = [c.base() for c in ldx3]
        b_ldc2 = [c.base() for c in ldc2]
        b_sq, b_agg, b_xq = sq.base(), agg.base(), xq.base()
        b_trr, b_tev, b_rte = trr.base(), tev.base(), rte.base()
        b_mq1, b_mq2, b_mkv = mq1.base(), mq2.base(), mkv.base()
        b_ropd, b_vev = ropd.base(), vev.base()

        def slot_guard(tensor_eng, t, j):
            gt = t * 26 + j
            if gt < 3:
                return
            pt, pj = (gt - 3) // 26, (gt - 3) % 26
            if pj < 8:
                tensor_eng.wait_ge(tev.h, b_tev + pt * 8 + pj + 1)
            else:
                tensor_eng.wait_ge(rte.h, b_rte + pt * 18 + (pj - 8) + 1)

        with nc.Block() as block:
            @block.sync
            def _(sync):
                for t in range(NHT):
                    if t >= 3:
                        sync.wait_ge(xq.h, b_xq + t - 2)
                    _inc(sync.dma_start(out=xhb[t % 3][:],
                                        in_=halo_ap[t * P:(t + 1) * P, :]), ldx3[t % 3], 16)
                    if t >= 2:
                        sync.wait_ge(ropd.h, b_ropd + 5 * (t - 1))
                    _inc(sync.dma_start(out=cosb[t % 2][:],
                                        in_=cos_h[t * P:(t + 1) * P, :]), ldc2[t % 2], 16)
                    _inc(sync.dma_start(out=sinb[t % 2][:],
                                        in_=sin_h[t * P:(t + 1) * P, :]), ldc2[t % 2], 16)
                    _inc(sync.dma_start(out=coskb[t % 2][:],
                                        in_=cosk_h[t * P:(t + 1) * P, :]), ldc2[t % 2], 16)
                    _inc(sync.dma_start(out=sinkb[t % 2][:],
                                        in_=sink_h[t * P:(t + 1) * P, :]), ldc2[t % 2], 16)

            @block.vector
            def _(vector):
                def ln(t):
                    pr = t % 2
                    vector.wait_ge(ldx3[t % 3].h, b_ldx3[t % 3] + 16 * (t // 3 + 1))
                    if t >= 2:
                        vector.wait_ge(sq.h, b_sq + t - 1)
                        vector.wait_ge(trr.h, b_trr + 26 * (t - 2) + 8)
                    vector.bn_stats(stats[pr][:, 0:6], xhb[t % 3][:, 0:512])
                    _inc(vector.bn_stats(stats[pr][:, 6:12], xhb[t % 3][:, 512:1024]), vr, 1)
                    vector.wait_ge(vr.h, vr.tot)
                    _inc(vector.bn_aggr(mvt[pr][:], stats[pr][:]), agg, 1)
                    vector.wait_ge(agg.h, agg.tot)
                    vector.wait_ge(sq.h, b_sq + t + 1)
                    _inc(vector.reciprocal(rstd[pr][:], sqvt[pr][:]), vr, 1)
                    vector.wait_ge(vr.h, vr.tot)
                    ins = vector.tensor_scalar(xnb[pr][:], xhb[t % 3][:], mvt[pr][:, 0:1],
                                               rstd[pr][:], op0=ALU.subtract, op1=ALU.mult)
                    _inc(ins, xq, 1)

                def rope(t):
                    pr = t % 2
                    vector.wait_ge(ldc2[t % 2].h, b_ldc2[t % 2] + 64 * (t // 2 + 1))
                    if t >= 2:
                        vector.wait_ge(trr.h, b_trr + 26 * (t - 1))
                    jobs = [(psq1[0], q1r[pr][:, 0:512], 8, mq1, b_mq1, 2, 1),
                            (psq1[1], q1r[pr][:, 512:1024], 8, mq1, b_mq1, 2, 2),
                            (psq2[0], q2r[pr][:, 0:512], 8, mq2, b_mq2, 2, 1),
                            (psq2[1], q2r[pr][:, 512:1024], 8, mq2, b_mq2, 2, 2),
                            (pskv[:, 0:256], krb[pr][:, 0:256], 4, mkv, b_mkv, 1, 1)]
                    for ji, (bnk, dstslice, nh, mch, mbase, per, mcnt) in enumerate(jobs):
                        vector.wait_ge(mch.h, mbase + per * t + mcnt)
                        vector.wait_ge(ropd.h, b_ropd + 5 * t + ji)
                        W2 = nh * 32
                        bv = bnk if ji == 4 else bnk[:]
                        src_e = bv.rearrange("p (h d two) -> p h d two", h=nh, d=32,
                                             two=2)[:, :, :, 0:1]
                        src_o = bv.rearrange("p (h d two) -> p h d two", h=nh, d=32,
                                             two=2)[:, :, :, 1:2]
                        dst_h0 = dstslice.rearrange("p (h two d) -> p h two d", h=nh, two=2,
                                                    d=32)[:, :, 0:1, :]
                        dst_h1 = dstslice.rearrange("p (h two d) -> p h two d", h=nh, two=2,
                                                    d=32)[:, :, 1:2, :]
                        if mch is mkv:
                            cosv = coskb[pr][:, 0:W2]
                            sinv = sinkb[pr][:, 0:W2]
                        else:
                            cosv = cosb[pr][:, 0:W2]
                            sinv = sinb[pr][:, 0:W2]
                        _inc(vector.tensor_tensor(rt1[:, 0:W2], src_e, cosv, op=ALU.mult),
                             vr, 1)
                        _inc(vector.tensor_tensor(rt2[:, 0:W2], src_o, sinv, op=ALU.mult),
                             vr, 1)
                        vector.wait_ge(vr.h, vr.tot)
                        _inc(vector.tensor_tensor(dst_h0, rt1[:, 0:W2], rt2[:, 0:W2],
                                                  op=ALU.subtract), vr, 1)
                        vector.wait_ge(vr.h, vr.tot)
                        _inc(vector.tensor_tensor(rt1[:, 0:W2], src_e, sinv, op=ALU.mult),
                             vr, 1)
                        _inc(vector.tensor_tensor(rt2[:, 0:W2], src_o, cosv, op=ALU.mult),
                             vr, 1)
                        vector.wait_ge(vr.h, vr.tot)
                        _inc(vector.tensor_tensor(dst_h1, rt1[:, 0:W2], rt2[:, 0:W2],
                                                  op=ALU.add), ropd, 1)

                ln(0)
                for t in range(NHT):
                    if t + 1 < NHT:
                        ln(t + 1)
                    pr = t % 2
                    if t >= 2:
                        vector.wait_ge(mkv.h, b_mkv + t - 1)
                    for k in range(DC):
                        vector.wait_ge(trr.h, b_trr + t * 26 + k + 1)
                        ins = vector.tensor_copy(lnT[pr][:, k, :], trsl[(t * 26 + k) % 3][:])
                        _inc(ins, tev, 1)
                    rope(t)

            @block.scalar
            def _(scalar):
                def sqrt_t(t):
                    pr = t % 2
                    scalar.wait_ge(agg.h, b_agg + t + 1)
                    if t >= 2:
                        scalar.wait_ge(xq.h, b_xq + t - 1)
                    _inc(scalar.activation(sqvt[pr][:], mvt[pr][:, 1:2], AF.Sqrt,
                                           bias=consts["eps"][:]), sq, 1)
                sqrt_t(0)
                for t in range(NHT):
                    if t + 1 < NHT:
                        sqrt_t(t + 1)
                    scalar.wait_ge(mkv.h, b_mkv + t + 1)
                    if t >= 1:
                        scalar.wait_ge(ropd.h, b_ropd + 5 * t)
                    _inc(scalar.activation(v_sb[:, t, :], pskv[:, 256:384], AF.Copy), vev, 1)
                    for j in range(18):
                        gt = t * 26 + 8 + j
                        scalar.wait_ge(trr.h, b_trr + gt + 1)
                        if j < 8:
                            dst = q1T[:, j, t * P:(t + 1) * P]
                        elif j < 16:
                            dst = q2T[:, j - 8, t * P:(t + 1) * P]
                        elif j == 16:
                            dst = k1T[:, t * P:(t + 1) * P]
                        else:
                            dst = k2T[:, t * P:(t + 1) * P]
                        _inc(scalar.activation(dst, trsl[gt % 3][:], AF.Copy), rte, 1)

            @block.tensor
            def _(tensor):
                for t in range(NHT):
                    pr = t % 2
                    tensor.wait_ge(xq.h, b_xq + t + 1)
                    for k in range(DC):
                        slot_guard(tensor, t, k)
                        ins = tensor.transpose(trsl[(t * 26 + k) % 3][:],
                                               xnb[pr][:, k * P:(k + 1) * P], ident[:])
                        _inc(ins, trr, 1)
                    tensor.wait_ge(tev.h, b_tev + 8 * (t + 1))
                    if t >= 1:
                        tensor.wait_ge(ropd.h, b_ropd + 5 * t)
                        tensor.wait_ge(vev.h, b_vev + t)
                    for bq in range(2):
                        for k in range(DC):
                            ins = tensor.matmul(psq1[bq][:], lnT[pr][:, k, :],
                                                Wq1[:, k, bq * 512:(bq + 1) * 512],
                                                start=(k == 0), stop=(k == DC - 1))
                        _inc(ins, mq1, 1)
                    for bq in range(2):
                        for k in range(DC):
                            ins = tensor.matmul(psq2[bq][:], lnT[pr][:, k, :],
                                                Wq2[:, k, bq * 512:(bq + 1) * 512],
                                                start=(k == 0), stop=(k == DC - 1))
                        _inc(ins, mq2, 1)
                    for k in range(DC):
                        ins = tensor.matmul(pskv[:], lnT[pr][:, k, :], Wkv[:, k, :],
                                            start=(k == 0), stop=(k == DC - 1))
                    _inc(ins, mkv, 1)
                    tensor.wait_ge(ropd.h, b_ropd + 5 * (t + 1))
                    for j in range(18):
                        slot_guard(tensor, t, 8 + j)
                        if j < 8:
                            srcv = q1r[pr][:, j * P:(j + 1) * P]
                        elif j < 16:
                            srcv = q2r[pr][:, (j - 8) * P:(j - 7) * P]
                        elif j == 16:
                            srcv = krb[pr][:, 0:128]
                        else:
                            srcv = krb[pr][:, 128:256]
                        ins = tensor.transpose(trsl[(t * 26 + 8 + j) % 3][:], srcv, ident[:])
                        _inc(ins, trr, 1)

    A_sb = sb(f"A{tag}", [P, 32, D], BF16)
    with contextlib.ExitStack() as esb:
        bsb = lambda n, s, d: esb.enter_context(nc.sbuf_tensor(n, s, d))
        bps = lambda n, s, d: esb.enter_context(nc.psum_tensor(n, s, d))
        p_sb2 = [bsb(f"p2{tag}{j}", [P, 2, 256], BF16) for j in range(2)]
        pt_sb = bsb(f"pt{tag}", [P, 4, P], BF16)
        nm1 = [bsb(f"nm1{tag}{j}", [P, 1], F32) for j in range(2)]
        nm2 = [bsb(f"nm2{tag}{j}", [P, 1], F32) for j in range(2)]
        bb1 = [bsb(f"bb1{tag}{j}", [P, 1], F32) for j in range(2)]
        bb2 = [bsb(f"bb2{tag}{j}", [P, 1], F32) for j in range(2)]
        ss1 = [bsb(f"ss1{tag}{j}", [P, 1], F32) for j in range(2)]
        ss2 = [bsb(f"ss2{tag}{j}", [P, 1], F32) for j in range(2)]
        rr1 = [bsb(f"rr1{tag}{j}", [P, 1], F32) for j in range(2)]
        rr2 = [bsb(f"rr2{tag}{j}", [P, 1], F32) for j in range(2)]
        tmpa = bsb(f"tmpa{tag}", [P, 64], F32)
        tmpb = bsb(f"tmpb{tag}", [P, 64], F32)
        pss1 = [bps(f"ps1{tag}{j}", [P, 256], F32) for j in range(2)]
        pss2 = [bps(f"ps2{tag}{j}", [P, 256], F32) for j in range(2)]
        pts = [bps(f"pts{tag}{j}", [P, P], BF16) for j in range(2)]
        psa1 = bps(f"pa1{tag}", [P, 64], F32)
        psa2 = bps(f"pa2{tag}", [P, 64], F32)

        mqk = ctx.ch(f"mqk{tag}")
        nmb = ctx.ch(f"nmb{tag}")
        expd = ctx.ch(f"expd{tag}")
        rch = ctx.ch(f"rch{tag}")
        ptrc = ctx.ch(f"ptr{tag}")
        ptev = ctx.ch(f"ptev{tag}")
        mpv = ctx.ch(f"mpv{tag}")
        cmb = ctx.ch(f"cmb{tag}")
        ptv2 = ctx.ch(f"ptv2{tag}")
        b_mqk, b_nmb, b_expd, b_rch = mqk.base(), nmb.base(), expd.base(), rch.base()
        b_ptr, b_ptev, b_mpv, b_cmb = ptrc.base(), ptev.base(), mpv.base(), cmb.base()
        b_ptv2 = ptv2.base()

        def unit(u):
            a, h = u // 16, u % 16
            w, qh = a // 2, a % 2
            return w, qh, h, a

        with nc.Block() as block:
            @block.tensor
            def _(tensor):
                for s in range(U + 2):
                    if s < U:
                        u = s
                        w, qh, h, a = unit(u)
                        g = h % 2
                        qcol = 128 * (w + qh)
                        kcol = 128 * w
                        if u >= 2:
                            tensor.wait_ge(expd.h, b_expd + 2 * (u - 1))
                        ins = tensor.matmul(pss1[u % 2][:],
                                            q1T[64 * g:64 * g + 64, h // 2, qcol:qcol + 128],
                                            k1T[64 * g:64 * g + 64, kcol:kcol + 256],
                                            start=True, stop=True)
                        _inc(ins, mqk, 1)
                        ins = tensor.matmul(pss2[u % 2][:],
                                            q2T[64 * g:64 * g + 64, h // 2, qcol:qcol + 128],
                                            k2T[64 * g:64 * g + 64, kcol:kcol + 256],
                                            start=True, stop=True)
                        _inc(ins, mqk, 1)
                    if 2 <= s:
                        u = s - 2
                        w, qh, h, a = unit(u)
                        g = h % 2
                        tensor.wait_ge(ptev.h, b_ptev + 2 * (u + 1))
                        tensor.wait_ge(ptv2.h, b_ptv2 + 2 * (u + 1))
                        if u >= 1:
                            tensor.wait_ge(cmb.h, b_cmb + 3 * u)
                        for br, dstp in ((0, psa1), (1, psa2)):
                            for j in range(2):
                                rv = v_sb[:, w + j, 64 * g:64 * g + 64]
                                ins = tensor.matmul(dstp[:], pt_sb[:, 2 * br + j, :], rv,
                                                    start=(j == 0), stop=(j == 1))
                            _inc(ins, mpv, 1)
                    if 1 <= s <= U:
                        u = s - 1
                        tensor.wait_ge(expd.h, b_expd + 2 * (u + 1))
                        for j in range(4):
                            gj = 4 * u + j
                            if gj >= 2:
                                if j < 2:
                                    tensor.wait_ge(ptv2.h, b_ptv2 + 2 * (u - 1) + j + 1)
                                else:
                                    tensor.wait_ge(ptev.h, b_ptev + 2 * u + j - 1)
                            br, half = j // 2, j % 2
                            srcv = p_sb2[u % 2][:, br, half * 128:(half + 1) * 128]
                            ins = tensor.transpose(pts[gj % 2][:], srcv, ident[:])
                            _inc(ins, ptrc, 1)

            @block.vector
            def _(vector):
                for s in range(U + 2):
                    if s < U:
                        u = s
                        pr = u % 2
                        if u >= 2:
                            vector.wait_ge(expd.h, b_expd + 2 * (u - 1))
                        vector.wait_ge(mqk.h, b_mqk + 2 * u + 1)
                        _inc(vector.tensor_reduce(nm1[pr][:], pss1[pr][:], axis=AX.X,
                                                  op=ALU.max, negate=True), nmb, 1)
                        vector.wait_ge(mqk.h, b_mqk + 2 * u + 2)
                        _inc(vector.tensor_reduce(nm2[pr][:], pss2[pr][:], axis=AX.X,
                                                  op=ALU.max, negate=True), nmb, 1)
                    if 1 <= s <= U:
                        u = s - 1
                        pr = u % 2
                        vector.wait_ge(expd.h, b_expd + 2 * (u + 1))
                        if u >= 2:
                            vector.wait_ge(cmb.h, b_cmb + 3 * (u - 1))
                        _inc(vector.reciprocal(rr1[pr][:], ss1[pr][:]), rch, 1)
                        _inc(vector.reciprocal(rr2[pr][:], ss2[pr][:]), rch, 1)
                    if 2 <= s:
                        v = s - 2
                        if v >= 1:
                            vector.wait_ge(mpv.h, b_mpv + 2 * v)
                        for j in (2, 3):
                            gj = 4 * v + j
                            vector.wait_ge(ptrc.h, b_ptr + gj + 1)
                            _inc(vector.tensor_copy(pt_sb[:, j, :], pts[gj % 2][:]), ptv2, 1)
                    if 2 <= s:
                        u = s - 2
                        pr = u % 2
                        w, qh, h, a = unit(u)
                        vector.wait_ge(mpv.h, b_mpv + 2 * (u + 1))
                        vector.wait_ge(rch.h, b_rch + 2 * (u + 1))
                        _inc(vector.tensor_scalar(tmpa[:], psa1[:], rr1[pr][:], None,
                                                  op0=ALU.mult), cmb, 1)
                        _inc(vector.tensor_scalar(tmpb[:], psa2[:], rr2[pr][:], neglam[:],
                                                  op0=ALU.mult, op1=ALU.mult), cmb, 1)
                        vector.wait_ge(cmb.h, cmb.tot)
                        ins = vector.tensor_tensor(A_sb[:, a, 64 * h:64 * h + 64],
                                                   tmpa[:], tmpb[:], op=ALU.add)
                        _inc(ins, cmb, 1)

            @block.scalar
            def _(scalar):
                for u in range(U):
                    pr = u % 2
                    scalar.wait_ge(nmb.h, b_nmb + 2 * u + 1)
                    if u >= 2:
                        scalar.wait_ge(ptrc.h, b_ptr + 4 * (u - 1))
                        scalar.wait_ge(rch.h, b_rch + 2 * (u - 1))
                    _inc(scalar.activation(p_sb2[pr][:, 0, :], pss1[pr][:], AF.Exp,
                                           bias=nm1[pr][:],
                                           accum_out=ss1[pr][:]), expd, 1)
                    scalar.wait_ge(nmb.h, b_nmb + 2 * u + 2)
                    _inc(scalar.activation(p_sb2[pr][:, 1, :], pss2[pr][:], AF.Exp,
                                           bias=nm2[pr][:],
                                           accum_out=ss2[pr][:]), expd, 1)
                    if u >= 1:
                        v = u - 1
                        scalar.wait_ge(mpv.h, b_mpv + 2 * v)
                        for j in range(2):
                            gj = 4 * v + j
                            scalar.wait_ge(ptrc.h, b_ptr + gj + 1)
                            _inc(scalar.activation(pt_sb[:, j, :], pts[gj % 2][:], AF.Copy),
                                 ptev, 1)
                for v in (U - 1,):
                    scalar.wait_ge(mpv.h, b_mpv + 2 * v)
                    for j in range(2):
                        gj = 4 * v + j
                        scalar.wait_ge(ptrc.h, b_ptr + gj + 1)
                        _inc(scalar.activation(pt_sb[:, j, :], pts[gj % 2][:], AF.Copy),
                             ptev, 1)

    with contextlib.ExitStack() as esc:
        csb = lambda n, s, d: esc.enter_context(nc.sbuf_tensor(n, s, d))
        cps = lambda n, s, d: esc.enter_context(nc.psum_tensor(n, s, d))
        Wout = csb(f"Wout{tag}", [P, DC, D], BF16)
        bout_row = csb(f"bor{tag}", [1, D], F32)
        bout_b = csb(f"bob{tag}", [P, D], F32)
        wf2 = [csb(f"wf2{tag}{j}", [P, D], F32) for j in range(2)]
        lwo2 = ctx.chn(f"lw{tag}", 2)
        cwo = ctx.ch(f"cw{tag}")
        mbo = ctx.ch(f"lme{tag}")
        ldbo = ctx.chn(f"ldc{tag}", 2)[0]
        b_lwo2 = [c.base() for c in lwo2]
        b_cwo, b_mbo = cwo.base(), mbo.base()
        b_ldbo = ldbo.base()
        poC = [cps(f"po{tag}{j}", [P, 512], F32) for j in range(2)]
        with nc.Block(no_gpsimd_drain=True) as block:
            @block.sync
            def _(sync):
                for k in range(DC):
                    if k >= 2:
                        sync.wait_ge(cwo.h, b_cwo + k - 1)
                    _inc(sync.dma_start(out=wf2[k % 2][:],
                                        in_=Wout_h[k * P:(k + 1) * P, :]), lwo2[k % 2], 16)
                _inc(sync.dma_start(out=bout_row[:],
                                    in_=bout_h.ap().rearrange("(one d) -> one d", one=1)),
                     ldbo, 16)

            @block.scalar
            def _(scalar):
                for k in range(DC):
                    scalar.wait_ge(lwo2[k % 2].h, b_lwo2[k % 2] + 16 * (k // 2 + 1))
                    _inc(scalar.activation(Wout[:, k, :], wf2[k % 2][:], AF.Copy), cwo, 1)

            @block.tensor
            def _(tensor):
                tensor.wait_ge(ldbo.h, b_ldbo + 16)
                _inc(tensor.matmul(poC[0][:], consts["ones_row"][:], bout_row[:, 0:512],
                                   start=True, stop=True), mbo, 1)
                _inc(tensor.matmul(poC[1][:], consts["ones_row"][:], bout_row[:, 512:1024],
                                   start=True, stop=True), mbo, 1)

            @block.vector
            def _(vector):
                vector.wait_ge(mbo.h, b_mbo + 2)
                vector.tensor_copy(bout_b[:, 0:512], poC[0][:])
                vector.tensor_copy(bout_b[:, 512:1024], poC[1][:])

        xbC = [csb(f"xbC{tag}{j}", [P, D], F32) for j in range(2)]
        AT = [csb(f"AT{tag}{j}", [P, DC, P], BF16) for j in range(2)]
        obuf = [csb(f"obC{tag}{j}", [P, D], F32) for j in range(2)]
        trslC = [cps(f"trC{tag}{j}", [P, P], BF16) for j in range(4)]
        ldxC = ctx.chn(f"ldx{tag}", 2)
        trrC = ctx.ch(f"trr{tag}")
        tevC = ctx.ch(f"tev{tag}")
        moC = ctx.ch(f"mq1{tag}")
        vrC = ctx.ch(f"vr{tag}")
        epoC = ctx.ch(f"xq{tag}")
        stoC = ctx.chn("qssqw", 2)
        b_ldxC = [c.base() for c in ldxC]
        b_trrC, b_tevC, b_moC = trrC.base(), tevC.base(), moC.base()
        b_vrC, b_epoC = vrC.base(), epoC.base()
        b_stoC = [c.base() for c in stoC]
        with nc.Block() as block:
            @block.sync
            def _(sync):
                for a in range(32):
                    if a >= 2:
                        sync.wait_ge(epoC.h, b_epoC + 2 * (a - 1))
                    _inc(sync.dma_start(out=xbC[a % 2][:],
                                        in_=src_ap[a * P:(a + 1) * P, :]), ldxC[a % 2], 16)

            @block.tensor
            def _(tensor):
                for a in range(32):
                    pr = a % 2
                    for k in range(DC):
                        g = a * DC + k
                        if g >= 4:
                            tensor.wait_ge(tevC.h, b_tevC + g - 3)
                        ins = tensor.transpose(trslC[g % 4][:], A_sb[:, a, k * P:(k + 1) * P],
                                               ident[:])
                        _inc(ins, trrC, 1)
                    tensor.wait_ge(tevC.h, b_tevC + DC * (a + 1))
                    if a >= 1:
                        tensor.wait_ge(epoC.h, b_epoC + 2 * a)
                    for nb in range(2):
                        for k in range(DC):
                            ins = tensor.matmul(poC[nb][:], AT[pr][:, k, :],
                                                Wout[:, k, nb * 512:(nb + 1) * 512],
                                                start=(k == 0), stop=(k == DC - 1))
                        _inc(ins, moC, 1)

            @block.vector
            def _(vector):
                for a in range(32):
                    pr = a % 2
                    if a >= 2:
                        vector.wait_ge(moC.h, b_moC + 2 * (a - 1))
                    for k in range(DC):
                        g = a * DC + k
                        vector.wait_ge(trrC.h, b_trrC + g + 1)
                        ins = vector.tensor_copy(AT[pr][:, k, :], trslC[g % 4][:])
                        _inc(ins, tevC, 1)
                    vector.wait_ge(ldxC[pr].h, b_ldxC[pr] + 16 * (a // 2 + 1))
                    if a >= 2:
                        vector.wait_ge(stoC[pr].h, b_stoC[pr] + 16 * (a // 2))
                    for nb in range(2):
                        vector.wait_ge(moC.h, b_moC + 2 * a + nb + 1)
                        sl = slice(nb * 512, (nb + 1) * 512)
                        _inc(vector.tensor_tensor(obuf[pr][:, sl], poC[nb][:],
                                                  xbC[pr][:, sl], op=ALU.add), vrC, 1)
                        vector.wait_ge(vrC.h, vrC.tot)
                        _inc(vector.tensor_tensor(obuf[pr][:, sl], obuf[pr][:, sl],
                                                  bout_b[:, sl], op=ALU.add), epoC, 1)

            @block.gpsimd
            def _(gpsimd):
                for a in range(32):
                    pr = a % 2
                    gpsimd.wait_ge(epoC.h, b_epoC + 2 * (a + 1))
                    _inc(gpsimd.dma_start(out=dst_ap[a * P:(a + 1) * P, :], in_=obuf[pr][:]),
                         stoC[pr], 16)
                for q in range(2):
                    gpsimd.wait_ge(stoC[q].h, stoC[q].tot)
    es0.close()


def emit_attn1(ctx, src_ap, dst_ap, Wq_h, bq_h, Wk_h, bk_h, Wv_h, bv_h, consts,
               tag="a1"):
    nc = ctx.nc
    ident = consts["ident"]
    with contextlib.ExitStack() as es2:
        sb = lambda n, s, d: es2.enter_context(nc.sbuf_tensor(n, s, d))
        ps = lambda n, s, d: es2.enter_context(nc.psum_tensor(n, s, d))
        WqT = sb(f"WqT{tag}", [P, DC, D], BF16)
        WkT = sb(f"WkT{tag}", [P, DC, D], BF16)
        WvT = sb(f"WvT{tag}", [P, DC, D], BF16)
        emit_conv_int_weight(ctx, Wq_h, D, D, WqT, ident, "cv", src_dt=F32)
        emit_conv_int_weight(ctx, Wk_h, D, D, WkT, ident, "cv", src_dt=F32)
        emit_conv_int_weight(ctx, Wv_h, D, D, WvT, ident, "cv", src_dt=F32)
        bq_sb = sb(f"bq{tag}", [P, DC], F32)
        bk_sb = sb(f"bk{tag}", [P, DC], F32)
        bv_row = sb(f"bvr{tag}", [1, D], F32)
        bv_b = sb(f"bvb{tag}", [P, D], F32)
        es_b = contextlib.ExitStack()
        psb = [es_b.enter_context(nc.psum_tensor(f"psb{tag}{j}", [P, 512], F32))
               for j in range(2)]
        ldb = ctx.chn(f"ldx{tag}", 3)[0]
        mmb = ctx.ch(f"lme{tag}")
        b_ldb, b_mmb = ldb.base(), mmb.base()
        with nc.Block(no_gpsimd_drain=True) as block:
            @block.sync
            def _(sync):
                with nc.allow_non_contiguous_dma(reason="tiny bias rearrange"):
                    _inc(sync.dma_start(out=bq_sb[:], in_=bq_h.ap().rearrange("(k p) -> p k", p=P),
                                        allow_slow_non_contiguous=True), ldb, 16)
                    _inc(sync.dma_start(out=bk_sb[:], in_=bk_h.ap().rearrange("(k p) -> p k", p=P),
                                        allow_slow_non_contiguous=True), ldb, 16)
                _inc(sync.dma_start(out=bv_row[:], in_=bv_h.ap().rearrange("(one d) -> one d", one=1)),
                     ldb, 16)
                sync.wait_ge(ldb.h, b_ldb + 48)

            @block.tensor
            def _(tensor):
                tensor.wait_ge(ldb.h, b_ldb + 48)
                _inc(tensor.matmul(psb[0][:], consts["ones_row"][:], bv_row[:, 0:512],
                                   start=True, stop=True), mmb, 1)
                _inc(tensor.matmul(psb[1][:], consts["ones_row"][:], bv_row[:, 512:1024],
                                   start=True, stop=True), mmb, 1)

            @block.vector
            def _(vector):
                vector.wait_ge(mmb.h, b_mmb + 2)
                vector.tensor_copy(bv_b[:, 0:512], psb[0][:])
                vector.tensor_copy(bv_b[:, 512:1024], psb[1][:])
        es_b.close()

        lnT = sb(f"lnT{tag}", [P, DC, 1024], BF16)
        qT = sb(f"qT{tag}", [P, DC, 1024], BF16)
        kT = sb(f"kT{tag}", [P, DC, 1024], BF16)
        v_sb = sb(f"v{tag}", [P, DC, 1024], BF16)
        p_sb = [sb(f"p{tag}{j}", [P, 1024], BF16) for j in range(2)]
        pt_sb = sb(f"pt{tag}", [P, DC, P], BF16)
        xbuf = [sb(f"xa{tag}{j}", [P, D], F32) for j in range(3)]
        xn = [sb(f"xn{tag}{j}", [P, D], BF16) for j in range(2)]
        stats = [sb(f"st{tag}{j}", [P, 12], F32) for j in range(2)]
        mv = [sb(f"mv{tag}{j}", [P, 2], F32) for j in range(2)]
        sqv = [sb(f"sq{tag}{j}", [P, 1], F32) for j in range(2)]
        rstd = [sb(f"rs{tag}{j}", [P, 1], F32) for j in range(2)]
        nmt = [sb(f"nm{tag}{j}", [P, 1], F32) for j in range(3)]
        ssm = [sb(f"ss{tag}{j}", [P, 1], F32) for j in range(3)]
        rr = sb(f"rr{tag}", [P, 1], F32)
        xb2 = [sb(f"xb2{tag}{j}", [P, D], F32) for j in range(2)]
        obuf = [sb(f"ob{tag}{j}", [P, D], F32) for j in range(2)]
        trsl = [ps(f"tr{tag}{j}", [P, P], BF16) for j in range(4)]
        psqk = [ps(f"pqk{tag}{j}", [P, 512], F32) for j in range(2)]
        psv = psqk
        pss = [ps(f"pss{tag}{j}", [P, 512], F32) for j in range(2)]
        pso = psqk

        for chn in range(4):
            base = chn * 1024
            ld3 = ctx.chn(f"ldx{tag}", 3)
            sq = ctx.ch(f"sq{tag}")
            agg = ctx.ch(f"agg{tag}")
            vr = ctx.ch(f"vr{tag}")
            xq = ctx.ch(f"xq{tag}")
            trr = ctx.ch(f"trr{tag}")
            tev = ctx.ch(f"tev{tag}")
            b_ld3 = [c.base() for c in ld3]
            b_sq, b_agg, b_xq = sq.base(), agg.base(), xq.base()
            b_trr, b_tev = trr.base(), tev.base()
            with nc.Block() as block:
                @block.sync
                def _(sync):
                    for t in range(8):
                        if t >= 3:
                            sync.wait_ge(xq.h, b_xq + t - 2)
                        _inc(sync.dma_start(out=xbuf[t % 3][:],
                                            in_=src_ap[base + t * P:base + (t + 1) * P, :]),
                             ld3[t % 3], 16)

                @block.vector
                def _(vector):
                    def ln(t):
                        pr = t % 2
                        vector.wait_ge(ld3[t % 3].h, b_ld3[t % 3] + 16 * (t // 3 + 1))
                        if t >= 2:
                            vector.wait_ge(sq.h, b_sq + t - 1)
                            vector.wait_ge(trr.h, b_trr + DC * (t - 1))
                        vector.bn_stats(stats[pr][:, 0:6], xbuf[t % 3][:, 0:512])
                        _inc(vector.bn_stats(stats[pr][:, 6:12], xbuf[t % 3][:, 512:1024]), vr, 1)
                        vector.wait_ge(vr.h, vr.tot)
                        _inc(vector.bn_aggr(mv[pr][:], stats[pr][:]), agg, 1)
                        vector.wait_ge(agg.h, agg.tot)
                        vector.wait_ge(sq.h, b_sq + t + 1)
                        _inc(vector.reciprocal(rstd[pr][:], sqv[pr][:]), vr, 1)
                        vector.wait_ge(vr.h, vr.tot)
                        ins = vector.tensor_scalar(xn[pr][:], xbuf[t % 3][:], mv[pr][:, 0:1],
                                                   rstd[pr][:], op0=ALU.subtract, op1=ALU.mult)
                        _inc(ins, xq, 1)
                    ln(0)
                    for t in range(8):
                        if t + 1 < 8:
                            ln(t + 1)
                        for k in range(DC):
                            g = t * DC + k
                            vector.wait_ge(trr.h, b_trr + g + 1)
                            ins = vector.tensor_copy(lnT[:, k, t * P:(t + 1) * P], trsl[g % 4][:])
                            _inc(ins, tev, 1)

                @block.scalar
                def _(scalar):
                    for t in range(8):
                        pr = t % 2
                        scalar.wait_ge(agg.h, b_agg + t + 1)
                        if t >= 2:
                            scalar.wait_ge(xq.h, b_xq + t - 1)
                        _inc(scalar.activation(sqv[pr][:], mv[pr][:, 1:2], AF.Sqrt,
                                               bias=consts["eps"][:]), sq, 1)

                @block.tensor
                def _(tensor):
                    for t in range(8):
                        pr = t % 2
                        tensor.wait_ge(xq.h, b_xq + t + 1)
                        for k in range(DC):
                            g = t * DC + k
                            if g >= 4:
                                tensor.wait_ge(tev.h, b_tev + g - 3)
                            ins = tensor.transpose(trsl[g % 4][:],
                                                   xn[pr][:, k * P:(k + 1) * P], ident[:])
                            _inc(ins, trr, 1)

            mqk = ctx.ch(f"mqk{tag}")
            eqk = ctx.ch(f"lmv{tag}")
            b_mqk, b_eqk = mqk.base(), eqk.base()
            with nc.Block() as block:
                @block.tensor
                def _(tensor):
                    g = 0
                    for (WT,) in ((WqT,), (WkT,)):
                        for ec in range(DC):
                            for tg in range(2):
                                if g >= 2:
                                    tensor.wait_ge(eqk.h, b_eqk + g - 1)
                                for k in range(DC):
                                    ins = tensor.matmul(psqk[g % 2][:],
                                                        WT[:, k, ec * P:(ec + 1) * P],
                                                        lnT[:, k, tg * 512:(tg + 1) * 512],
                                                        start=(k == 0), stop=(k == DC - 1))
                                _inc(ins, mqk, 1)
                                g += 1
                    for t in range(8):
                        for nb in range(2):
                            if g >= 2:
                                tensor.wait_ge(eqk.h, b_eqk + g - 1)
                            for k in range(DC):
                                ins = tensor.matmul(psqk[g % 2][:],
                                                    lnT[:, k, t * P:(t + 1) * P],
                                                    WvT[:, k, nb * 512:(nb + 1) * 512],
                                                    start=(k == 0), stop=(k == DC - 1))
                            _inc(ins, mqk, 1)
                            g += 1

                @block.vector
                def _(vector):
                    g = 0
                    for wi, (dstT, bias_sb) in enumerate(((qT, bq_sb), (kT, bk_sb))):
                        for ec in range(DC):
                            for tg in range(2):
                                vector.wait_ge(mqk.h, b_mqk + g + 1)
                                ins = vector.tensor_scalar(
                                    dstT[:, ec, tg * 512:(tg + 1) * 512], psqk[g % 2][:],
                                    bias_sb[:, ec:ec + 1], None, op0=ALU.add)
                                _inc(ins, eqk, 1)
                                g += 1
                    for t in range(8):
                        for nb in range(2):
                            vector.wait_ge(mqk.h, b_mqk + g + 1)
                            ins = vector.tensor_tensor(v_sb[:, t, nb * 512:(nb + 1) * 512],
                                                       psqk[g % 2][:],
                                                       bv_b[:, nb * 512:(nb + 1) * 512],
                                                       op=ALU.add)
                            _inc(ins, eqk, 1)
                            g += 1

            ldx2 = ctx.chn(f"ldx{tag}", 2)
            msc = ctx.ch(f"mq2{tag}")
            vrc = ctx.ch(f"mkv{tag}")
            expd = ctx.ch(f"expd{tag}")
            ptr = ctx.ch(f"ptr{tag}")
            ptev = ctx.ch(f"ptev{tag}")
            mpv = ctx.ch(f"mpv{tag}")
            epo = ctx.ch(f"ropd{tag}")
            cbf = ctx.ch(f"vev{tag}")
            sto2 = ctx.chn("qssqw", 2)
            b_ldx2 = [c.base() for c in ldx2]
            b_msc, b_vrc, b_expd = msc.base(), vrc.base(), expd.base()
            b_ptr, b_ptev, b_mpv, b_epo = ptr.base(), ptev.base(), mpv.base(), epo.base()
            b_cbf = cbf.base()
            b_sto2 = [c.base() for c in sto2]
            with nc.Block() as block:
                @block.sync
                def _(sync):
                    for qt in range(8):
                        if qt >= 2:
                            sync.wait_ge(epo.h, b_epo + 2 * (qt - 1))
                        _inc(sync.dma_start(out=xb2[qt % 2][:],
                                            in_=src_ap[base + qt * P:base + (qt + 1) * P, :]),
                             ldx2[qt % 2], 16)

                @block.tensor
                def _(tensor):
                    for qt in range(8):
                        if qt >= 1:
                            tensor.wait_ge(expd.h, b_expd + 2 * qt)
                        for nb in range(2):
                            for k in range(DC):
                                ins = tensor.matmul(pss[nb][:],
                                                    qT[:, k, qt * P:(qt + 1) * P],
                                                    kT[:, k, nb * 512:(nb + 1) * 512],
                                                    start=(k == 0), stop=(k == DC - 1))
                            _inc(ins, msc, 1)
                        tensor.wait_ge(expd.h, b_expd + 2 * (qt + 1))
                        for k in range(DC):
                            g = qt * DC + k
                            if g >= 4:
                                tensor.wait_ge(ptev.h, b_ptev + g - 3)
                            ins = tensor.transpose(trsl[g % 4][:],
                                                   p_sb[qt % 2][:, k * P:(k + 1) * P], ident[:])
                            _inc(ins, ptr, 1)
                        tensor.wait_ge(ptev.h, b_ptev + DC * (qt + 1))
                        if qt >= 1:
                            tensor.wait_ge(epo.h, b_epo + 2 * qt)
                        for nb in range(2):
                            for k in range(DC):
                                ins = tensor.matmul(pso[nb][:],
                                                    pt_sb[:, k, :],
                                                    v_sb[:, k, nb * 512:(nb + 1) * 512],
                                                    start=(k == 0), stop=(k == DC - 1))
                            _inc(ins, mpv, 1)

                @block.vector
                def _(vector):
                    for qt in range(8):
                        if qt >= 1:
                            vector.wait_ge(epo.h, b_epo + 2 * qt)
                        vector.wait_ge(msc.h, b_msc + 2 * qt + 1)
                        _inc(vector.tensor_reduce(nmt[0][:], pss[0][:], axis=AX.X, op=ALU.max,
                                                  negate=True), vrc, 1)
                        vector.wait_ge(msc.h, b_msc + 2 * qt + 2)
                        _inc(vector.tensor_reduce(nmt[1][:], pss[1][:], axis=AX.X, op=ALU.max,
                                                  negate=True), vrc, 1)
                        vector.wait_ge(vrc.h, vrc.tot)
                        _inc(vector.tensor_tensor(nmt[2][:], nmt[0][:], nmt[1][:], op=ALU.min),
                             vrc, 1)
                        vector.wait_ge(expd.h, b_expd + 2 * (qt + 1))
                        _inc(vector.tensor_tensor(ssm[2][:], ssm[0][:], ssm[1][:], op=ALU.add),
                             vrc, 1)
                        vector.wait_ge(vrc.h, vrc.tot)
                        _inc(vector.reciprocal(rr[:], ssm[2][:]), vrc, 1)
                        if qt >= 1:
                            vector.wait_ge(mpv.h, b_mpv + 2 * qt)
                        for k in range(DC):
                            g = qt * DC + k
                            vector.wait_ge(ptr.h, b_ptr + g + 1)
                            ins = vector.tensor_copy(pt_sb[:, k, :], trsl[g % 4][:])
                            _inc(ins, ptev, 1)
                        vector.wait_ge(ldx2[qt % 2].h, b_ldx2[qt % 2] + 16 * (qt // 2 + 1))
                        if qt >= 2:
                            vector.wait_ge(sto2[qt % 2].h, b_sto2[qt % 2] + 16 * (qt // 2))
                        vector.wait_ge(vrc.h, vrc.tot)
                        for nb in range(2):
                            vector.wait_ge(mpv.h, b_mpv + 2 * qt + nb + 1)
                            sl = slice(nb * 512, (nb + 1) * 512)
                            ins = vector.scalar_tensor_tensor(obuf[qt % 2][:, sl], pso[nb][:],
                                                              rr[:], xb2[qt % 2][:, sl],
                                                              op0=ALU.mult, op1=ALU.add)
                            _inc(ins, epo, 1)

                @block.scalar
                def _(scalar):
                    for qt in range(8):
                        pr = qt % 2
                        scalar.wait_ge(vrc.h, b_vrc + 5 * qt + 3)
                        if qt >= 2:
                            scalar.wait_ge(ptr.h, b_ptr + DC * (qt - 1))
                        _inc(scalar.activation(p_sb[pr][:, 0:512], pss[0][:], AF.Exp,
                                               bias=nmt[2][:], accum_out=ssm[0][:]), expd, 1)
                        _inc(scalar.activation(p_sb[pr][:, 512:1024], pss[1][:], AF.Exp,
                                               bias=nmt[2][:], accum_out=ssm[1][:]), expd, 1)

                @block.gpsimd
                def _(gpsimd):
                    for qt in range(8):
                        gpsimd.wait_ge(epo.h, b_epo + 2 * (qt + 1))
                        _inc(gpsimd.dma_start(out=dst_ap[base + qt * P:base + (qt + 1) * P, :],
                                              in_=obuf[qt % 2][:]), sto2[qt % 2], 16)
                    for q in range(2):
                        gpsimd.wait_ge(sto2[q].h, sto2[q].tot)


def emit_ident(ctx, ident_h):
    nc = ctx.nc
    ident = ctx.sb("identsb", [P, P], BF16)
    ch = ctx.ch("identld")
    b = ch.base()
    with nc.Block(no_gpsimd_drain=True) as block:
        @block.sync
        def _(sync):
            _inc(sync.dma_start(out=ident[:], in_=ident_h[:, :]), ch, 16)
            sync.wait_ge(ch.h, b + 16)
    return ident


_PROG_CACHE = {}


def build_ffn0(parts=("conv", "passA", "passB")):
    ctx = Ctx()
    nc = ctx.nc
    H = 3072
    xin = nc.dram_tensor("xin", [R, D], F32, kind="ExternalInput")
    wg = nc.dram_tensor("wg", [H, D], I32, kind="ExternalInput")
    wu = nc.dram_tensor("wu", [H, D], I32, kind="ExternalInput")
    wd = nc.dram_tensor("wd", [D, H], I32, kind="ExternalInput")
    ident_h = nc.dram_tensor("ident", [P, P], BF16, kind="ExternalInput")
    xout = nc.dram_tensor("xout", [R, D], F32, kind="ExternalOutput")
    hscr = nc.dram_tensor("hscr", [R, H], BF16)
    consts = emit_consts(ctx)
    consts["ident"] = emit_ident(ctx, ident_h)
    wgT = ctx.sb("wgT", [P, DC, H], BF16)
    wuT = ctx.sb("wuT", [P, DC, H], BF16)
    wdT = ctx.sb("wdT", [P, H // P, D], BF16)
    if "conv" in parts:
        emit_conv_int_weight(ctx, wg, H, D, wgT, consts["ident"], "g0")
        emit_conv_int_weight(ctx, wu, H, D, wuT, consts["ident"], "u0")
        emit_conv_int_weight(ctx, wd, D, H, wdT, consts["ident"], "d0")
    tiles = [xin.ap()[i * P:(i + 1) * P, :] for i in range(NT)]
    xd = [(xin.ap()[i * P:(i + 1) * P, :], xout.ap()[i * P:(i + 1) * P, :]) for i in range(NT)]
    if "passA" in parts:
        emit_ffn_passA(ctx, tiles, hscr.ap(), wgT, wuT, H, consts, False, None, "f0")
    if "passB" in parts:
        emit_ffn_passB(ctx, xd, hscr.ap(), wdT, H, consts, False, None, "f0")
    else:
        chz = ctx.ch("dbgst")
        bz = chz.base()
        with nc.Block(no_gpsimd_drain=True) as block:
            @block.gpsimd
            def _(gpsimd):
                _inc(gpsimd.dma_start(out=xout[0:P, :], in_=wgT[:, 0, 0:D]), chz, 16)
                gpsimd.wait_ge(chz.h, bz + 16)
    ctx.close()
    return nc


def build_ffn1():
    ctx = Ctx()
    nc = ctx.nc
    H = 4096
    xin = nc.dram_tensor("xin", [R, D], F32, kind="ExternalInput")
    wg = nc.dram_tensor("wg", [H, D], F32, kind="ExternalInput")
    wu = nc.dram_tensor("wu", [H, D], F32, kind="ExternalInput")
    wd = nc.dram_tensor("wd", [D, H], F32, kind="ExternalInput")
    ident_h = nc.dram_tensor("ident", [P, P], BF16, kind="ExternalInput")
    xout = nc.dram_tensor("xout", [R, D], F32, kind="ExternalOutput")
    hscr = nc.dram_tensor("hscr", [R, H], BF16)
    wdT_s = nc.dram_tensor("wdTs", [P, H // P, D], BF16)
    consts = emit_consts(ctx)
    consts["ident"] = emit_ident(ctx, ident_h)
    qd = {
        "m_g": ctx.sb("m_g", [P, 1], F32),
        "m_u": ctx.sb("m_u", [P, 1], F32),
        "m_d": ctx.sb("m_d", [P, 1], F32),
        "inv_sh_all": ctx.sb("inv_sh_all", [P, NT], F32),
    }
    emit_quant_l1_weight(ctx, wd, D, H, None, consts["ident"], consts, qd["m_d"], "qd",
                         stash_ap=wdT_s.ap())
    es_gu = contextlib.ExitStack()
    ternGT = es_gu.enter_context(nc.sbuf_tensor("ternGT", [P, DC, H], BF16))
    ternUT = es_gu.enter_context(nc.sbuf_tensor("ternUT", [P, DC, H], BF16))
    emit_quant_l1_weight(ctx, wg, H, D, ternGT, consts["ident"], consts, qd["m_g"], "qg")
    emit_quant_l1_weight(ctx, wu, H, D, ternUT, consts["ident"], consts, qd["m_u"], "qu")
    tiles = [xin.ap()[i * P:(i + 1) * P, :] for i in range(NT)]
    emit_ffn_passA(ctx, tiles, hscr.ap(), ternGT, ternUT, H, consts, True, qd, "f1")
    es_gu.close()
    ternDT = ctx.sb("ternDT", [P, H // P, D], BF16)
    ch = ctx.ch("ldDT")
    bldt = ch.base()
    with ctx.nc.Block(no_gpsimd_drain=True) as block:
        @block.sync
        def _(sync):
            _inc(sync.dma_start(out=ternDT[:], in_=wdT_s[:, :, :]), ch, 16)
            sync.wait_ge(ch.h, bldt + 16)
    xd = [(xin.ap()[i * P:(i + 1) * P, :], xout.ap()[i * P:(i + 1) * P, :]) for i in range(NT)]
    emit_ffn_passB(ctx, xd, hscr.ap(), ternDT, H, consts, True, qd, "f1")
    ctx.close()
    return nc


def _ident_np():
    return np.eye(P, dtype=ml_dtypes.bfloat16)


def _x_shards(x):
    out = []
    for c in range(N_CORES):
        b, s = c // 4, (c % 4) * R
        out.append(np.ascontiguousarray(x[b, s:s + R, :], dtype=np.float32))
    return out


def _launch(name, builder, in_maps):
    if name not in _PROG_CACHE:
        _PROG_CACHE[name] = builder()
    nc = _PROG_CACHE[name]
    return run_bass_kernel_spmd(nc, in_maps, core_ids=list(range(N_CORES)))


def run_ffn0(x_shards, wg, wu, wd):
    ident = _ident_np()
    maps = [{"xin": xs, "wg": wg, "wu": wu, "wd": wd, "ident": ident} for xs in x_shards]
    res = _launch("ffn0", build_ffn0, maps)
    return [res.results[c]["xout"] for c in range(N_CORES)]


def run_ffn1(x_shards, wg, wu, wd):
    ident = _ident_np()
    wg = np.ascontiguousarray(wg, dtype=np.float32)
    wu = np.ascontiguousarray(wu, dtype=np.float32)
    wd = np.ascontiguousarray(wd, dtype=np.float32)
    maps = [{"xin": xs, "wg": wg, "wu": wu, "wd": wd, "ident": ident} for xs in x_shards]
    res = _launch("ffn1", build_ffn1, maps)
    return [res.results[c]["xout"] for c in range(N_CORES)]


def build_attn1():
    ctx = Ctx()
    nc = ctx.nc
    xin = nc.dram_tensor("xin", [R, D], F32, kind="ExternalInput")
    Wq = nc.dram_tensor("Wq", [D, D], F32, kind="ExternalInput")
    bq = nc.dram_tensor("bq", [D], F32, kind="ExternalInput")
    Wk = nc.dram_tensor("Wk", [D, D], F32, kind="ExternalInput")
    bk = nc.dram_tensor("bk", [D], F32, kind="ExternalInput")
    Wv = nc.dram_tensor("Wv", [D, D], F32, kind="ExternalInput")
    bv = nc.dram_tensor("bv", [D], F32, kind="ExternalInput")
    ident_h = nc.dram_tensor("ident", [P, P], BF16, kind="ExternalInput")
    xout = nc.dram_tensor("xout", [R, D], F32, kind="ExternalOutput")
    consts = emit_consts(ctx)
    consts["ident"] = emit_ident(ctx, ident_h)
    emit_attn1(ctx, xin.ap(), xout.ap(), Wq, bq, Wk, bk, Wv, bv, consts)
    ctx.close()
    return nc


def run_attn1(x_shards, Wq, bq, Wk, bk, Wv, bv):
    ident = _ident_np()
    maps = [{"xin": xs, "Wq": Wq, "bq": bq, "Wk": Wk, "bk": bk, "Wv": Wv, "bv": bv,
             "ident": ident} for xs in x_shards]
    res = _launch("attn1", build_attn1, maps)
    return [res.results[c]["xout"] for c in range(N_CORES)]


def build_attn0():
    ctx = Ctx()
    nc = ctx.nc
    xin = nc.dram_tensor("xin", [R, D], F32, kind="ExternalInput")
    halo = nc.dram_tensor("halo", [HALO, D], F32, kind="ExternalInput")
    cosr = nc.dram_tensor("cosr", [HALO, 512], F32, kind="ExternalInput")
    sinr = nc.dram_tensor("sinr", [HALO, 512], F32, kind="ExternalInput")
    coskr = nc.dram_tensor("coskr", [HALO, 128], F32, kind="ExternalInput")
    sinkr = nc.dram_tensor("sinkr", [HALO, 128], F32, kind="ExternalInput")
    Wq1 = nc.dram_tensor("Wq1", [D, D], F32, kind="ExternalInput")
    Wq2 = nc.dram_tensor("Wq2", [D, D], F32, kind="ExternalInput")
    Wk1 = nc.dram_tensor("Wk1", [D, 128], F32, kind="ExternalInput")
    Wk2 = nc.dram_tensor("Wk2", [D, 128], F32, kind="ExternalInput")
    Wv = nc.dram_tensor("Wv", [D, 128], F32, kind="ExternalInput")
    Wout = nc.dram_tensor("Wout", [D, D], F32, kind="ExternalInput")
    bout = nc.dram_tensor("bout", [D], F32, kind="ExternalInput")
    lq1 = nc.dram_tensor("lq1", [64], F32, kind="ExternalInput")
    lk1 = nc.dram_tensor("lk1", [64], F32, kind="ExternalInput")
    lq2 = nc.dram_tensor("lq2", [64], F32, kind="ExternalInput")
    lk2 = nc.dram_tensor("lk2", [64], F32, kind="ExternalInput")
    ident_h = nc.dram_tensor("ident", [P, P], BF16, kind="ExternalInput")
    xout = nc.dram_tensor("xout", [R, D], F32, kind="ExternalOutput")
    consts = emit_consts(ctx)
    consts["ident"] = emit_ident(ctx, ident_h)
    emit_attn0(ctx, xin.ap(), xout.ap(), halo.ap(), F32, cosr.ap(), sinr.ap(),
               coskr.ap(), sinkr.ap(),
               Wq1, Wq2, Wk1, Wk2, Wv, Wout, bout, lq1, lk1, lq2, lk2, consts)
    ctx.close()
    return nc


def _rope_tables(cp):
    pos = (2048 * cp + np.arange(HALO, dtype=np.float64))[:, None]
    div = np.exp(np.arange(0, 64, 2, dtype=np.float64) * (-np.log(10000.0) / 64.0))
    cos = np.cos(pos * div).astype(np.float32)
    sin = np.sin(pos * div).astype(np.float32)
    cosq = (np.tile(cos, (1, 16)) * np.float32(0.125)).copy()
    sinq = (np.tile(sin, (1, 16)) * np.float32(0.125)).copy()
    cosk = np.tile(cos, (1, 4)).copy()
    sink = np.tile(sin, (1, 4)).copy()
    return cosq, sinq, cosk, sink


def run_attn0(x_shards, x_full_per_batch, ins):
    ident = _ident_np()
    maps = []
    for c in range(N_CORES):
        b, cp = c // 4, c % 4
        halo = np.ascontiguousarray(x_full_per_batch[b][2048 * cp:2048 * cp + HALO, :],
                                    dtype=np.float32)
        cosr, sinr, coskr, sinkr = _rope_tables(cp)
        maps.append({"xin": x_shards[c], "halo": halo, "cosr": cosr, "sinr": sinr,
                     "coskr": coskr, "sinkr": sinkr,
                     "Wq1": ins["L0_attn_Wq1"], "Wq2": ins["L0_attn_Wq2"],
                     "Wk1": ins["L0_attn_Wk1"], "Wk2": ins["L0_attn_Wk2"],
                     "Wv": ins["L0_attn_Wv"], "Wout": ins["L0_attn_Wout"],
                     "bout": ins["L0_attn_bout"], "lq1": ins["L0_attn_lq1"],
                     "lk1": ins["L0_attn_lk1"], "lq2": ins["L0_attn_lq2"],
                     "lk2": ins["L0_attn_lk2"], "ident": ident})
    res = _launch("attn0", build_attn0, maps)
    return [res.results[c]["xout"] for c in range(N_CORES)]


def kernel(**inputs):
    if not _UNFUSED:
        return kernel_fused(inputs)
    return _kernel_unfused(**inputs)


_UNFUSED = False


def _kernel_unfused(**inputs):
    x = np.ascontiguousarray(np.asarray(inputs["L0_attn_x"]), dtype=np.float32)
    shards = _x_shards(x)
    wg0 = np.ascontiguousarray(np.asarray(inputs["L0_ffn_w_gate"]), dtype=np.int32)
    wu0 = np.ascontiguousarray(np.asarray(inputs["L0_ffn_w_up"]), dtype=np.int32)
    wd0 = np.ascontiguousarray(np.asarray(inputs["L0_ffn_w_down"]), dtype=np.int32)

    s1 = run_ffn0(shards, wg0, wu0, wd0)
    x1b = [np.concatenate(s1[0:4], 0), np.concatenate(s1[4:8], 0)]
    ins_np = {k: np.ascontiguousarray(np.asarray(v)) for k, v in inputs.items()
              if k.startswith("L0_attn_")}
    s2 = run_attn0(s1, x1b, ins_np)
    s3 = run_ffn0(s2, wg0, wu0, wd0)
    s4 = run_ffn1(s3, np.asarray(inputs["L1_ffn_w_gate"]),
                  np.asarray(inputs["L1_ffn_w_up"]), np.asarray(inputs["L1_ffn_w_down"]))
    s5 = run_attn1(s4, np.ascontiguousarray(np.asarray(inputs["L1_attn_Wq"]), dtype=np.float32),
                   np.ascontiguousarray(np.asarray(inputs["L1_attn_bq"]), dtype=np.float32),
                   np.ascontiguousarray(np.asarray(inputs["L1_attn_Wk"]), dtype=np.float32),
                   np.ascontiguousarray(np.asarray(inputs["L1_attn_bk"]), dtype=np.float32),
                   np.ascontiguousarray(np.asarray(inputs["L1_attn_Wv"]), dtype=np.float32),
                   np.ascontiguousarray(np.asarray(inputs["L1_attn_bv"]), dtype=np.float32))
    s6 = run_ffn1(s5, np.asarray(inputs["L1_ffn_w_gate"]),
                  np.asarray(inputs["L1_ffn_w_up"]), np.asarray(inputs["L1_ffn_w_down"]))
    out = np.stack([np.concatenate(s6[0:4], 0), np.concatenate(s6[4:8], 0)], 0)
    return out


def emit_load_sbuf(ctx, dst_sb, src_ap, tag):
    ch = ctx.ch(f"lds{tag}")
    b = ch.base()
    with ctx.nc.Block(no_gpsimd_drain=True) as block:
        @block.sync
        def _(sync):
            _inc(sync.dma_start(out=dst_sb[:], in_=src_ap), ch, 16)
            sync.wait_ge(ch.h, b + 16)


def build_fused():
    ctx = Ctx()
    nc = ctx.nc
    H0, H1 = 3072, 4096
    ein = lambda n, s, d: nc.dram_tensor(n, s, d, kind="ExternalInput")
    xin = ein("xin", [R, D], F32)
    xhalo = ein("xhalo", [HALO, D], F32)
    cosr = ein("cosr", [HALO, 512], F32)
    sinr = ein("sinr", [HALO, 512], F32)
    coskr = ein("coskr", [HALO, 128], F32)
    sinkr = ein("sinkr", [HALO, 128], F32)
    ident_h = ein("ident", [P, P], BF16)
    wg0 = ein("wg0", [H0, D], I32)
    wu0 = ein("wu0", [H0, D], I32)
    wd0 = ein("wd0", [D, H0], I32)
    Wq1 = ein("Wq1", [D, D], F32)
    Wq2 = ein("Wq2", [D, D], F32)
    Wk1 = ein("Wk1", [D, 128], F32)
    Wk2 = ein("Wk2", [D, 128], F32)
    Wv0 = ein("Wv0", [D, 128], F32)
    Wout = ein("Wout", [D, D], F32)
    bout = ein("bout", [D], F32)
    lq1 = ein("lq1", [64], F32)
    lk1 = ein("lk1", [64], F32)
    lq2 = ein("lq2", [64], F32)
    lk2 = ein("lk2", [64], F32)
    wg1 = ein("wg1", [H1, D], F32)
    wu1 = ein("wu1", [H1, D], F32)
    wd1 = ein("wd1", [D, H1], F32)
    Wq = ein("Wq", [D, D], F32)
    bq = ein("bq", [D], F32)
    Wk = ein("Wk", [D, D], F32)
    bk = ein("bk", [D], F32)
    Wv = ein("Wv", [D, D], F32)
    bv = ein("bv", [D], F32)
    xout = nc.dram_tensor("xout", [R, D], F32, kind="ExternalOutput")

    resid = nc.dram_tensor("resid", [R, D], F32)
    halo_scr = nc.dram_tensor("haloscr", [HALO, D], F32)
    hscr0 = nc.dram_tensor("hscr0", [R + HALO, H0], BF16)
    hscr1 = nc.dram_tensor("hscr1", [R, H1], BF16)
    wgT_s = nc.dram_tensor("wgTs", [P, DC, H0], BF16)
    wuT_s = nc.dram_tensor("wuTs", [P, DC, H0], BF16)
    wdT_s = nc.dram_tensor("wdTs", [P, H0 // P, D], BF16)
    tG_s = nc.dram_tensor("tGs", [P, DC, H1], BF16)
    tU_s = nc.dram_tensor("tUs", [P, DC, H1], BF16)
    tD_s = nc.dram_tensor("tDs", [P, H1 // P, D], BF16)

    consts = emit_consts(ctx)
    consts["ident"] = emit_ident(ctx, ident_h)
    qd = {
        "m_g": ctx.sb("m_g", [P, 1], F32),
        "m_u": ctx.sb("m_u", [P, 1], F32),
        "m_d": ctx.sb("m_d", [P, 1], F32),
        "inv_sh_all": ctx.sb("inv_sh_all", [P, NT], F32),
    }

    def own_tiles(t_ap):
        return [t_ap[i * P:(i + 1) * P, :] for i in range(NT)]

    def halo_tiles(t_ap):
        return [t_ap[i * P:(i + 1) * P, :] for i in range(NHT)]

    with contextlib.ExitStack() as es1:
        wgT = es1.enter_context(nc.sbuf_tensor("wgT1", [P, DC, H0], BF16))
        wuT = es1.enter_context(nc.sbuf_tensor("wuT1", [P, DC, H0], BF16))
        wdT = es1.enter_context(nc.sbuf_tensor("wdT1", [P, H0 // P, D], BF16))
        emit_conv_int_weight(ctx, wg0, H0, D, wgT, consts["ident"], "cv", stash_ap=wgT_s.ap())
        emit_conv_int_weight(ctx, wu0, H0, D, wuT, consts["ident"], "cv", stash_ap=wuT_s.ap())
        emit_conv_int_weight(ctx, wd0, D, H0, wdT, consts["ident"], "cv", stash_ap=wdT_s.ap())
        src1 = own_tiles(xin.ap()) + halo_tiles(xhalo.ap())
        emit_ffn_passA(ctx, src1, hscr0.ap(), wgT, wuT, H0, consts, False, None, "f")
        xd1 = ([(xin.ap()[i * P:(i + 1) * P, :], resid.ap()[i * P:(i + 1) * P, :])
                for i in range(NT)] +
               [(xhalo.ap()[i * P:(i + 1) * P, :], halo_scr.ap()[i * P:(i + 1) * P, :])
                for i in range(NHT)])
        emit_ffn_passB(ctx, xd1, hscr0.ap(), wdT, H0, consts, False, None, "f")

    emit_attn0(ctx, resid.ap(), resid.ap(), halo_scr.ap(), F32, cosr.ap(), sinr.ap(),
               coskr.ap(), sinkr.ap(),
               Wq1, Wq2, Wk1, Wk2, Wv0, Wout, bout, lq1, lk1, lq2, lk2, consts, tag="a0")

    with contextlib.ExitStack() as es3:
        wgT = es3.enter_context(nc.sbuf_tensor("wgT3", [P, DC, H0], BF16))
        wuT = es3.enter_context(nc.sbuf_tensor("wuT3", [P, DC, H0], BF16))
        wdT = es3.enter_context(nc.sbuf_tensor("wdT3", [P, H0 // P, D], BF16))
        emit_load_sbuf(ctx, wgT, wgT_s.ap(), "w")
        emit_load_sbuf(ctx, wuT, wuT_s.ap(), "w")
        emit_load_sbuf(ctx, wdT, wdT_s.ap(), "w")
        emit_ffn_passA(ctx, own_tiles(resid.ap()), hscr0.ap(), wgT, wuT, H0, consts,
                       False, None, "f")
        xd3 = [(resid.ap()[i * P:(i + 1) * P, :], resid.ap()[i * P:(i + 1) * P, :])
               for i in range(NT)]
        emit_ffn_passB(ctx, xd3, hscr0.ap(), wdT, H0, consts, False, None, "f")

    emit_quant_l1_weight(ctx, wd1, D, H1, None, consts["ident"], consts, qd["m_d"], "qw",
                         stash_ap=tD_s.ap())
    with contextlib.ExitStack() as es4:
        ternGT = es4.enter_context(nc.sbuf_tensor("tG4", [P, DC, H1], BF16))
        ternUT = es4.enter_context(nc.sbuf_tensor("tU4", [P, DC, H1], BF16))
        emit_quant_l1_weight(ctx, wg1, H1, D, ternGT, consts["ident"], consts, qd["m_g"],
                             "qw", stash_ap=tG_s.ap())
        emit_quant_l1_weight(ctx, wu1, H1, D, ternUT, consts["ident"], consts, qd["m_u"],
                             "qw", stash_ap=tU_s.ap())
        emit_ffn_passA(ctx, own_tiles(resid.ap()), hscr1.ap(), ternGT, ternUT, H1, consts,
                       True, qd, "f")
    with contextlib.ExitStack() as es4b:
        ternDT = es4b.enter_context(nc.sbuf_tensor("tD4", [P, H1 // P, D], BF16))
        emit_load_sbuf(ctx, ternDT, tD_s.ap(), "w")
        xd4 = [(resid.ap()[i * P:(i + 1) * P, :], resid.ap()[i * P:(i + 1) * P, :])
               for i in range(NT)]
        emit_ffn_passB(ctx, xd4, hscr1.ap(), ternDT, H1, consts, True, qd, "f")

    emit_attn1(ctx, resid.ap(), resid.ap(), Wq, bq, Wk, bk, Wv, bv, consts, tag="a0")

    with contextlib.ExitStack() as es6:
        ternGT = es6.enter_context(nc.sbuf_tensor("tG6", [P, DC, H1], BF16))
        ternUT = es6.enter_context(nc.sbuf_tensor("tU6", [P, DC, H1], BF16))
        emit_load_sbuf(ctx, ternGT, tG_s.ap(), "w")
        emit_load_sbuf(ctx, ternUT, tU_s.ap(), "w")
        emit_ffn_passA(ctx, own_tiles(resid.ap()), hscr1.ap(), ternGT, ternUT, H1, consts,
                       True, qd, "f")
    with contextlib.ExitStack() as es6b:
        ternDT = es6b.enter_context(nc.sbuf_tensor("tD6", [P, H1 // P, D], BF16))
        emit_load_sbuf(ctx, ternDT, tD_s.ap(), "w")
        xd6 = [(resid.ap()[i * P:(i + 1) * P, :], xout.ap()[i * P:(i + 1) * P, :])
               for i in range(NT)]
        emit_ffn_passB(ctx, xd6, hscr1.ap(), ternDT, H1, consts, True, qd, "f")
    ctx.close()
    return nc


def kernel_fused(inputs):
    x = np.ascontiguousarray(np.asarray(inputs["L0_attn_x"]), dtype=np.float32)
    shards = _x_shards(x)
    ident = _ident_np()
    f32 = lambda k: np.ascontiguousarray(np.asarray(inputs[k]), dtype=np.float32)
    i32 = lambda k: np.ascontiguousarray(np.asarray(inputs[k]), dtype=np.int32)
    common = {
        "ident": ident,
        "wg0": i32("L0_ffn_w_gate"), "wu0": i32("L0_ffn_w_up"), "wd0": i32("L0_ffn_w_down"),
        "Wq1": f32("L0_attn_Wq1"), "Wq2": f32("L0_attn_Wq2"), "Wk1": f32("L0_attn_Wk1"),
        "Wk2": f32("L0_attn_Wk2"), "Wv0": f32("L0_attn_Wv"), "Wout": f32("L0_attn_Wout"),
        "bout": f32("L0_attn_bout"), "lq1": f32("L0_attn_lq1"), "lk1": f32("L0_attn_lk1"),
        "lq2": f32("L0_attn_lq2"), "lk2": f32("L0_attn_lk2"),
        "wg1": f32("L1_ffn_w_gate"), "wu1": f32("L1_ffn_w_up"), "wd1": f32("L1_ffn_w_down"),
        "Wq": f32("L1_attn_Wq"), "bq": f32("L1_attn_bq"), "Wk": f32("L1_attn_Wk"),
        "bk": f32("L1_attn_bk"), "Wv": f32("L1_attn_Wv"), "bv": f32("L1_attn_bv"),
    }
    maps = []
    for c in range(N_CORES):
        b, cp = c // 4, c % 4
        halo = np.ascontiguousarray(x[b, 2048 * cp:2048 * cp + HALO, :], dtype=np.float32)
        cosr, sinr, coskr, sinkr = _rope_tables(cp)
        m = dict(common)
        m.update({"xin": shards[c], "xhalo": halo, "cosr": cosr, "sinr": sinr,
                  "coskr": coskr, "sinkr": sinkr})
        maps.append(m)
    res = _launch("fused", build_fused, maps)
    outs = [res.results[c]["xout"] for c in range(N_CORES)]
    return np.stack([np.concatenate(outs[0:4], 0), np.concatenate(outs[4:8], 0)], 0)


# version 25
# speedup vs baseline: 1.0139x; 1.0139x over previous
import contextlib

import numpy as np
import ml_dtypes

import concourse.bass as bass
import concourse.mybir as mybir
from concourse.bass_utils import run_bass_kernel_spmd

F32 = mybir.dt.float32
BF16 = mybir.dt.bfloat16
I32 = mybir.dt.int32
AF = mybir.ActivationFunctionType
ALU = mybir.AluOpType
AX = mybir.AxisListType

P = 128
R = 4096
NT = R // P
D = 1024
DC = D // P
HALO = 2176
NHT = HALO // P

N_CORES = 8


class Ch:

    def __init__(self, handle):
        self.h = handle
        self.tot = 0

    def base(self):
        return self.tot


def _inc(instr, ch, k=1):
    instr.then_inc(ch.h, k)
    ch.tot += k
    return instr


class Ctx:
    def __init__(self):
        import functools
        self.nc = bass.Bass("TRN2", target_bir_lowering=False, debug=False)
        self.nc.sbuf_tensor = functools.partial(self.nc.sbuf_tensor,
                                                allow_name_mangling=True)
        self.nc.psum_tensor = functools.partial(self.nc.psum_tensor,
                                                allow_name_mangling=True)
        self.es = contextlib.ExitStack()
        self._chs = {}

    def sb(self, name, shape, dt):
        return self.es.enter_context(self.nc.sbuf_tensor(name, shape, dt))

    def ch(self, name):
        if name not in self._chs:
            h = self.es.enter_context(self.nc.semaphore(name))
            self._chs[name] = Ch(h)
        return self._chs[name]

    def chn(self, name, n):
        return [self.ch(f"{name}_{j}") for j in range(n)]

    def close(self):
        self.es.close()


def emit_tern_chain(vector, pairs, bufs, scale_ap, ch, final_ch):
    W = bufs["W"]
    y, p, n = bufs["y"], bufs["c1"], bufs["c2"]
    hv = [slice(0, W), slice(W, 2 * W)]
    tok = {}
    n_ops = len(pairs) * 2 * 4
    cnt = [0]

    def op(emit, out_key, deps):
        v = max([tok[d] for d in deps if d in tok], default=0)
        if v:
            vector.wait_ge(ch.h, v)
        ins = emit()
        cnt[0] += 1
        if cnt[0] == n_ops:
            _inc(ins, final_ch, 1)
            tok[out_key] = 0
        else:
            _inc(ins, ch, 1)
            tok[out_key] = ch.tot
        return ins

    for (srcA, dstA), (srcB, dstB) in pairs:
        src_ = [srcA, srcB]
        dst = [dstA, dstB]
        for s in range(2):
            op(lambda s=s: vector.tensor_scalar(y[:, hv[s]], src_[s], scale_ap, None,
                                                op0=ALU.mult),
               ("y", s), [("o", 0), ("o", 1)])
        for s in range(2):
            op(lambda s=s: vector.tensor_scalar(p[:, hv[s]], y[:, hv[s]], 0.5, None,
                                                op0=ALU.is_gt),
               ("p", s), [("y", s)])
        for s in range(2):
            op(lambda s=s: vector.tensor_scalar(n[:, hv[s]], y[:, hv[s]], -0.5, None,
                                                op0=ALU.is_lt),
               ("n", s), [("y", s)])
        for s in range(2):
            op(lambda s=s: vector.tensor_tensor(dst[s], p[:, hv[s]], n[:, hv[s]],
                                                op=ALU.subtract),
               ("o", s), [("p", s), ("n", s), ("o", 1 - s)])
    return


def emit_consts(ctx):
    nc = ctx.nc
    c = {}
    c["eps"] = ctx.sb("c_eps", [P, 1], F32)
    c["ones_col"] = ctx.sb("c_ones_col", [P, 1], F32)
    c["ones_row"] = ctx.sb("c_ones_row", [1, P], F32)
    with nc.Block(no_gpsimd_drain=True) as block:
        @block.vector
        def _(vector):
            vector.memset(c["eps"][:], 1e-5)
            vector.memset(c["ones_col"][:], 1.0)
            vector.memset(c["ones_row"][:], 1.0)
    return c


def emit_conv_int_weight(ctx, w_h, O, I, wT, ident, tag, src_dt=I32, stash_ap=None):
    nc = ctx.nc
    OT = O // P
    KC = I // P
    lw = ctx.chn(f"lw{tag}", 2)
    cw, twr, wev = ctx.ch(f"cw{tag}"), ctx.ch(f"twr{tag}"), ctx.ch(f"wev{tag}")
    b_lw = [c.base() for c in lw]
    b_cw, b_twr, b_wev = cw.base(), twr.base(), wev.base()
    with contextlib.ExitStack() as es2:
        wint = [es2.enter_context(nc.sbuf_tensor(f"wint{tag}{j}", [P, I], src_dt)) for j in range(2)]
        wbf = [es2.enter_context(nc.sbuf_tensor(f"wbf{tag}{j}", [P, I], BF16)) for j in range(2)]
        trsl = [es2.enter_context(nc.psum_tensor(f"pstr{tag}{j}", [P, P], BF16)) for j in range(4)]
        with nc.Block(no_gpsimd_drain=True) as block:
            @block.sync
            def _(sync):
                for j in range(OT):
                    if j >= 2:
                        sync.wait_ge(cw.h, b_cw + j - 1)
                    _inc(sync.dma_start(out=wint[j % 2][:], in_=w_h[j * P:(j + 1) * P, :]),
                         lw[j % 2], 16)

            @block.scalar
            def _(scalar):
                for j in range(OT):
                    scalar.wait_ge(lw[j % 2].h, b_lw[j % 2] + 16 * (j // 2 + 1))
                    if j >= 2:
                        scalar.wait_ge(twr.h, b_twr + (j - 1) * KC)
                    _inc(scalar.activation(wbf[j % 2][:], wint[j % 2][:], AF.Copy), cw, 1)

            @block.tensor
            def _(tensor):
                for j in range(OT):
                    tensor.wait_ge(cw.h, b_cw + j + 1)
                    for k in range(KC):
                        g = j * KC + k
                        if g >= 4:
                            tensor.wait_ge(wev.h, b_wev + g - 3)
                        ins = tensor.transpose(trsl[g % 4][:],
                                               wbf[j % 2][:, k * P:(k + 1) * P], ident[:])
                        _inc(ins, twr, 1)

            @block.vector
            def _(vector):
                for j in range(OT):
                    for k in range(KC):
                        g = j * KC + k
                        vector.wait_ge(twr.h, b_twr + g + 1)
                        ins = vector.tensor_copy(wT[:, k, j * P:(j + 1) * P], trsl[g % 4][:])
                        _inc(ins, wev, 1)

            @block.gpsimd
            def _(gpsimd):
                if stash_ap is not None:
                    sst = ctx.chn("qssqw", 2)[0]
                    gpsimd.wait_ge(wev.h, b_wev + OT * KC)
                    _inc(gpsimd.dma_start(out=stash_ap, in_=wT[:]), sst, 16)
                    gpsimd.wait_ge(sst.h, sst.tot)


def emit_quant_l1_weight(ctx, w_h, O, I, wT, ident, consts, m_b, tag, stash_ap=None):
    nc = ctx.nc
    OT = O // P
    KC = I // P
    QC = I // D
    lw = ctx.chn(f"qlw{tag}", 2)
    ar, cw, twr, wev = (ctx.ch(f"qar{tag}"), ctx.ch(f"qcw{tag}"),
                        ctx.ch(f"qtw{tag}"), ctx.ch(f"qwe{tag}"))
    sst2 = ctx.chn(f"qss{tag}", 2)
    b_lw = [c.base() for c in lw]
    b_ar, b_cw = ar.base(), cw.base()
    b_twr, b_wev = twr.base(), wev.base()
    b_sst2 = [c.base() for c in sst2]
    to_stash = wT is None
    with contextlib.ExitStack() as es2:
        sb = lambda n, s, d: es2.enter_context(nc.sbuf_tensor(n, s, d))
        wfl = [sb(f"qwf{tag}{j}", [P, I], F32) for j in range(2)]
        absr = [sb(f"qab{tag}{j}", [P, 1], F32) for j in range(2)]
        acc = sb(f"qac{tag}", [P, 1], F32)
        tot = sb(f"qto{tag}", [P, 1], F32)
        sw_b = sb(f"qsw{tag}", [P, 1], F32)
        qy = sb(f"qqy{tag}", [P, D], F32)
        qc1 = sb(f"qq1{tag}", [P, D], F32)
        qc2 = sb(f"qq2{tag}", [P, D], F32)
        wbf = [sb(f"qwb{tag}{j}", [P, I], BF16) for j in range(2)]
        if to_stash:
            stg = [sb(f"qsg{tag}{j}", [P, KC, P], BF16) for j in range(2)]
        trsl = [es2.enter_context(nc.psum_tensor(f"qpt{tag}{j}", [P, P], BF16)) for j in range(4)]
        psum1 = es2.enter_context(nc.psum_tensor(f"qp1{tag}", [P, 1], F32))
        psumb = es2.enter_context(nc.psum_tensor(f"qpb{tag}", [P, 1], F32))

        with nc.Block(no_gpsimd_drain=True) as block:
            @block.sync
            def _(sync):
                for j in range(OT):
                    if j >= 2:
                        sync.wait_ge(ar.h, b_ar + j - 1)
                    _inc(sync.dma_start(out=wfl[j % 2][:], in_=w_h[j * P:(j + 1) * P, :]),
                         lw[j % 2], 16)

            @block.vector
            def _(vector):
                vector.memset(acc[:], 0.0)
                for j in range(OT):
                    vector.wait_ge(lw[j % 2].h, b_lw[j % 2] + 16 * (j // 2 + 1))
                    ins = vector.tensor_reduce(absr[j % 2][:], wfl[j % 2][:], axis=AX.X,
                                               op=ALU.add, apply_absolute_value=True)
                    _inc(ins, ar, 1)
                    vector.wait_ge(ar.h, ar.tot)
                    vector.tensor_tensor(acc[:], acc[:], absr[j % 2][:], op=ALU.add)

        with nc.Block(no_gpsimd_drain=True) as block:
            @block.tensor
            def _(tensor):
                _inc(tensor.matmul(psum1[:1, :], acc[:], consts["ones_col"][:],
                                   start=True, stop=True), twr, 1)

            @block.vector
            def _(vector):
                vector.wait_ge(twr.h, b_twr + 1)
                _inc(vector.tensor_scalar(tot[:1, :], psum1[:1, :], 1.0 / float(O * I), None,
                                          op0=ALU.mult), ar, 1)
                vector.wait_ge(ar.h, ar.tot)
                _inc(vector.tensor_scalar(tot[:1, :], tot[:1, :], 1e-5, None, op0=ALU.max),
                     ar, 1)
                vector.wait_ge(ar.h, ar.tot)
                _inc(vector.reciprocal(sw_b[:1, :], tot[:1, :]), ar, 1)

        with nc.Block(no_gpsimd_drain=True) as block:
            @block.tensor
            def _(tensor):
                tensor.matmul(psum1[:], consts["ones_row"][:], tot[:1, :], start=True, stop=True)
                _inc(tensor.matmul(psumb[:], consts["ones_row"][:], sw_b[:1, :],
                                   start=True, stop=True), twr, 1)

            @block.vector
            def _(vector):
                vector.wait_ge(twr.h, b_twr + 2)
                vector.tensor_copy(m_b[:], psum1[:])
                vector.tensor_copy(sw_b[:], psumb[:])

        b_lw2 = [c.base() for c in lw]
        with nc.Block() as block:
            @block.sync
            def _(sync):
                for j in range(OT):
                    if j >= 2:
                        sync.wait_ge(cw.h, b_cw + j - 1)
                    _inc(sync.dma_start(out=wfl[j % 2][:], in_=w_h[j * P:(j + 1) * P, :]),
                         lw[j % 2], 16)

            @block.vector
            def _(vector):
                qbufs = {"W": 512, "y": qy, "c1": qc1, "c2": qc2}
                for j in range(OT):
                    vector.wait_ge(lw[j % 2].h, b_lw2[j % 2] + 16 * (j // 2 + 1))
                    if j >= 2:
                        vector.wait_ge(twr.h, b_twr + 2 + (j - 1) * KC)
                    if j >= 1:
                        vector.wait_ge(cw.h, b_cw + j)
                    pairs = []
                    for cc in range(QC):
                        s0 = slice(cc * D, cc * D + 512)
                        s1 = slice(cc * D + 512, (cc + 1) * D)
                        pairs.append(((wfl[j % 2][:, s0], wbf[j % 2][:, s0]),
                                      (wfl[j % 2][:, s1], wbf[j % 2][:, s1])))
                    emit_tern_chain(vector, pairs, qbufs, sw_b[:], ar, cw)

            @block.tensor
            def _(tensor):
                for j in range(OT):
                    tensor.wait_ge(cw.h, b_cw + j + 1)
                    for k in range(KC):
                        g = j * KC + k
                        if g >= 4:
                            tensor.wait_ge(wev.h, b_wev + g - 3)
                        ins = tensor.transpose(trsl[g % 4][:],
                                               wbf[j % 2][:, k * P:(k + 1) * P], ident[:])
                        _inc(ins, twr, 1)

            @block.scalar
            def _(scalar):
                for j in range(OT):
                    if to_stash and j >= 2:
                        scalar.wait_ge(sst2[j % 2].h, b_sst2[j % 2] + 16 * (j // 2))
                    for k in range(KC):
                        g = j * KC + k
                        scalar.wait_ge(twr.h, b_twr + 2 + g + 1)
                        dst = stg[j % 2][:, k, :] if to_stash else wT[:, k, j * P:(j + 1) * P]
                        ins = scalar.activation(dst, trsl[g % 4][:], AF.Copy)
                        _inc(ins, wev, 1)

            @block.gpsimd
            def _(gpsimd):
                if to_stash:
                    for j in range(OT):
                        gpsimd.wait_ge(wev.h, b_wev + (j + 1) * KC)
                        _inc(gpsimd.dma_start(out=stash_ap[:, :, j * P:(j + 1) * P],
                                              in_=stg[j % 2][:]), sst2[j % 2], 16)
                    for q in range(2):
                        gpsimd.wait_ge(sst2[q].h, sst2[q].tot)
                elif stash_ap is not None:
                    gpsimd.wait_ge(wev.h, b_wev + OT * KC)
                    _inc(gpsimd.dma_start(out=stash_ap, in_=wT[:]), sst2[0], 16)
                    gpsimd.wait_ge(sst2[0].h, sst2[0].tot)


def emit_ffn_passA(ctx, src_tiles, h_scr, wgT, wuT, H, consts, quant, qd, tag):
    ntot = len(src_tiles)
    nc = ctx.nc
    NB = H // 512
    NHC = H // 512
    NCC = 2 + NHC
    ident = consts["ident"]
    ld3 = ctx.chn(f"ldA{tag}", 3)
    xq = ctx.ch(f"xqA{tag}")
    sq = ctx.ch(f"sqA{tag}")
    agg = ctx.ch(f"aggA{tag}")
    trr = ctx.ch(f"trrA{tag}")
    tev = ctx.ch(f"tevA{tag}")
    mg = ctx.ch(f"mgA{tag}")
    mu = ctx.ch(f"muA{tag}")
    sil = ctx.ch(f"silA{tag}")
    hh = ctx.ch(f"hhA{tag}")
    qv = ctx.ch(f"hqA{tag}")
    qa = ctx.ch(f"qaA{tag}")
    qp = ctx.ch(f"qpA{tag}")
    sth2 = ctx.chn(f"sthA{tag}", 2)
    vr = ctx.ch(f"vrA{tag}")
    cs = ctx.ch(f"csA{tag}")
    b_ld3 = [c.base() for c in ld3]
    b_xq, b_sq, b_agg = xq.base(), sq.base(), agg.base()
    b_trr, b_tev, b_mg, b_mu = trr.base(), tev.base(), mg.base(), mu.base()
    b_sil, b_hh = sil.base(), hh.base()
    b_qv, b_qa, b_qp = qv.base(), qa.base(), qp.base()
    b_sth2 = [c.base() for c in sth2]
    b_cs = cs.base()
    HV = [slice(0, 512), slice(512, 1024), slice(1024, 1536)]
    NSL = 3
    qidx = {}
    marks = {}
    cctr = [0]

    with contextlib.ExitStack() as es2:
        sb = lambda n, s, d: es2.enter_context(nc.sbuf_tensor(n, s, d))
        ps = lambda n, s, d: es2.enter_context(nc.psum_tensor(n, s, d))
        xbuf = [sb(f"xA{tag}{j}", [P, D], F32) for j in range(3)]
        stats = [sb(f"stA{tag}{j}", [P, 12], F32) for j in range(2)]
        mv = [sb(f"mvA{tag}{j}", [P, 2], F32) for j in range(2)]
        sqv = [sb(f"sqvA{tag}{j}", [P, 1], F32) for j in range(2)]
        rstd = [sb(f"rsA{tag}{j}", [P, 1], F32) for j in range(2)]
        xop = [sb(f"xoA{tag}{j}", [P, D], BF16) for j in range(2)]
        xT = [sb(f"xTA{tag}{j}", [P, DC, P], BF16) for j in range(2)]
        sg = [sb(f"sgA{tag}{j}", [P, 512], F32) for j in range(2)]
        hsb = [sb(f"hsA{tag}{j}", [P, H], BF16) for j in range(2)]
        if quant:
            xn = [sb(f"xnA{tag}{j}", [P, D], F32) for j in range(2)]
            am = [sb(f"amA{tag}{j}", [P, 1], F32) for j in range(2)]
            sa = [sb(f"saA{tag}{j}", [P, 1], F32) for j in range(2)]
            csg = [sb(f"csgA{tag}{j}", [P, 1], F32) for j in range(2)]
            csu_all = sb(f"csuA{tag}", [P, ntot], F32)
            ham = [sb(f"hamA{tag}{j}", [P, 1], F32) for j in range(2)]
            hsa = [sb(f"hsaA{tag}{j}", [P, 1], F32) for j in range(2)]
            qy = sb(f"qyA{tag}", [P, 3 * 512], F32)
            qyi = sb(f"qyiA{tag}", [P, 3 * 512], I32)
            qc1 = sb(f"qc1A{tag}", [P, 3 * 512], F32)
            qc2 = sb(f"qc2A{tag}", [P, 3 * 512], F32)
        trsl = [ps(f"ptA{tag}{j}", [P, P], BF16) for j in range(4)]
        psg = [ps(f"pgA{tag}{j}", [P, 512], F32) for j in range(2)]
        psu = [ps(f"puA{tag}{j}", [P, 512], F32) for j in range(2)]

        def chain_head(vector, src_half, scale_ap, vrk):
            c = cctr[0]
            cctr[0] += 1
            qidx[(c, 'src')] = src_half
            qidx[(c, 'scl')] = scale_ap
            qidx[(c, 'vrk')] = vrk
            return c

        def chain_r(vector, c):
            h = HV[c % NSL]
            vector.wait_ge(qa.h, b_qa + c + 1)
            _inc(vector.scalar_tensor_tensor(qy[:, h], qidx[(c, 'src')],
                                             qidx[(c, 'scl')], qyi[:, h],
                                             op0=ALU.mult, op1=ALU.subtract), qv, 1)
            qidx[(c, 'r')] = qv.tot

        def chain_ccc(vector, c, dst_half):
            h = HV[c % NSL]
            if c >= NSL:
                vector.wait_ge(qp.h, b_qp + (c - NSL + 1))
            vector.wait_ge(qv.h, qidx[(c, 'r')])
            _inc(vector.tensor_scalar(qc1[:, h], qy[:, h], 0.5, None, op0=ALU.is_gt),
                 qv, 1)
            _inc(vector.tensor_scalar(qc2[:, h], qy[:, h], -0.5, None, op0=ALU.is_lt),
                 qv, 1)
            qidx[(c, 'cc')] = qv.tot
            vector.wait_ge(qv.h, qidx[(c, 'cc')])
            _inc(vector.tensor_tensor(qc1[:, h], qyi[:, h], qc1[:, h], op=ALU.add), qv, 1)
            qidx[(c, 't2a')] = qv.tot
            qidx[(c, 'dst')] = dst_half

        def emit_ln(vector, i):
            pr = i % 2
            vector.wait_ge(ld3[i % 3].h, b_ld3[i % 3] + 16 * (i // 3 + 1))
            if i >= 2:
                vector.wait_ge(sq.h, b_sq + i - 1)
                vector.wait_ge(trr.h, b_trr + DC * (i - 1))
                if quant:
                    vector.wait_ge(sil.h, b_sil + (i - 1) * NB)
            vector.bn_stats(stats[pr][:, 0:6], xbuf[i % 3][:, 0:512])
            _inc(vector.bn_stats(stats[pr][:, 6:12], xbuf[i % 3][:, 512:1024]), vr, 1)
            vector.wait_ge(vr.h, vr.tot)
            _inc(vector.bn_aggr(mv[pr][:], stats[pr][:]), agg, 1)
            vector.wait_ge(agg.h, agg.tot)
            vector.wait_ge(sq.h, b_sq + i + 1)
            _inc(vector.reciprocal(rstd[pr][:], sqv[pr][:]), vr, 1)
            vector.wait_ge(vr.h, vr.tot)
            if not quant:
                ins = vector.tensor_scalar(xop[pr][:], xbuf[i % 3][:], mv[pr][:, 0:1],
                                           rstd[pr][:], op0=ALU.subtract, op1=ALU.mult)
                _inc(ins, xq, 1)
            else:
                if i >= 2:
                    vector.wait_ge(qa.h, b_qa + marks[i - 2]['xc'][1] + 1)
                _inc(vector.tensor_scalar(xn[pr][:], xbuf[i % 3][:], mv[pr][:, 0:1],
                                          rstd[pr][:], op0=ALU.subtract, op1=ALU.mult), vr, 1)
                vector.wait_ge(vr.h, vr.tot)
                _inc(vector.tensor_reduce(am[pr][:], xn[pr][:], axis=AX.X, op=ALU.max,
                                          apply_absolute_value=True), vr, 1)
                vector.wait_ge(vr.h, vr.tot)
                _inc(vector.tensor_scalar(am[pr][:], am[pr][:], 1e-5, None, op0=ALU.max),
                     vr, 1)
                vector.wait_ge(vr.h, vr.tot)
                _inc(vector.reciprocal(sa[pr][:], am[pr][:]), vr, 1)
                vector.wait_ge(vr.h, vr.tot)
                _inc(vector.tensor_scalar(sa[pr][:], sa[pr][:], 127.0, None, op0=ALU.mult),
                     vr, 1)
                _inc(vector.tensor_scalar(csg[pr][:], am[pr][:], 1.0 / 127.0, qd["m_g"][:],
                                          op0=ALU.mult, op1=ALU.mult), cs, 1)
                _inc(vector.tensor_scalar(csu_all[:, i:i + 1], am[pr][:], 1.0 / 127.0,
                                          qd["m_u"][:], op0=ALU.mult, op1=ALU.mult), cs, 1)
                vector.wait_ge(vr.h, vr.tot)
                vrk = vr.tot
                ca = chain_head(vector, xn[pr][:, HV[0]], sa[pr][:], vrk)
                cb = chain_head(vector, xn[pr][:, HV[1]], sa[pr][:], vrk)
                chain_r(vector, ca)
                chain_r(vector, cb)
                chain_ccc(vector, ca, xop[pr][:, HV[0]])
                chain_ccc(vector, cb, xop[pr][:, HV[1]])
                marks[i] = {'xc': [ca, cb], 'hc': []}

        def emit_hquant_prologue(vector, i):
            pr = i % 2
            vector.wait_ge(hh.h, b_hh + (i + 1) * NB)
            if i >= 2:
                vector.wait_ge(qa.h, b_qa + marks[i - 2]['hc'][-1] + 1)
            _inc(vector.tensor_reduce(ham[pr][:], hsb[pr][:], axis=AX.X, op=ALU.max,
                                      apply_absolute_value=True), vr, 1)
            vector.wait_ge(vr.h, vr.tot)
            _inc(vector.tensor_scalar(ham[pr][:], ham[pr][:], 1e-5, None, op0=ALU.max), vr, 1)
            vector.wait_ge(vr.h, vr.tot)
            _inc(vector.reciprocal(hsa[pr][:], ham[pr][:]), vr, 1)
            vector.wait_ge(vr.h, vr.tot)
            _inc(vector.tensor_scalar(hsa[pr][:], hsa[pr][:], 127.0, None, op0=ALU.mult), vr, 1)
            _inc(vector.tensor_scalar(qd["inv_sh_all"][:, i:i + 1], ham[pr][:], 1.0 / 127.0,
                                      csu_all[:, i:i + 1], op0=ALU.mult, op1=ALU.mult), vr, 1)
            vector.wait_ge(vr.h, vr.tot)
            marks[i]['hvrk'] = vr.tot
            marks[i]['hc'] = []

        def emit_hquant_step(vector, i, hc):
            pr = i % 2
            cs_list = marks[i]['hc']
            half = hsb[pr][:, hc * 512:(hc + 1) * 512]
            cs_list.append(chain_head(vector, half, hsa[pr][:], marks[i]['hvrk']))
            if hc >= 1:
                chain_r(vector, cs_list[hc - 1])
            if hc >= 2:
                j = hc - 2
                chain_ccc(vector, cs_list[j], hsb[pr][:, j * 512:(j + 1) * 512])
            if hc == NHC - 1:
                chain_r(vector, cs_list[NHC - 1])
                for j in (NHC - 2, NHC - 1):
                    chain_ccc(vector, cs_list[j], hsb[pr][:, j * 512:(j + 1) * 512])

        with nc.Block() as block:
            @block.vector
            def _(vector):
                emit_ln(vector, 0)
                for i in range(ntot):
                    pr = i % 2
                    if i >= 2:
                        vector.wait_ge(mg.h, b_mg + (i - 1) * NB)
                        vector.wait_ge(mu.h, b_mu + (i - 1) * NB)
                    for k in range(DC):
                        g = i * DC + k
                        vector.wait_ge(trr.h, b_trr + g + 1)
                        ins = vector.tensor_copy(xT[pr][:, k, :], trsl[g % 4][:])
                        _inc(ins, tev, 1)
                    if quant and i >= 1:
                        emit_hquant_prologue(vector, i - 1)
                    if i + 1 < ntot:
                        emit_ln(vector, i + 1)
                    if i >= 2:
                        vector.wait_ge(sth2[i % 2].h, b_sth2[i % 2] + 16 * (i // 2))
                    for nb in range(NB):
                        G = i * NB + nb
                        vector.wait_ge(sil.h, b_sil + G + 1)
                        vector.wait_ge(mu.h, b_mu + G + 1)
                        sl = slice(nb * 512, (nb + 1) * 512)
                        ins = vector.tensor_tensor(hsb[pr][:, sl], sg[nb % 2][:],
                                                   psu[nb % 2][:], op=ALU.mult)
                        _inc(ins, hh, 1)
                        if quant and i >= 1 and nb < NHC:
                            emit_hquant_step(vector, i - 1, nb)
                if quant:
                    emit_hquant_prologue(vector, ntot - 1)
                    for hc in range(NHC):
                        emit_hquant_step(vector, ntot - 1, hc)

            @block.sync
            def _(sync):
                for i in range(ntot):
                    if i >= 3:
                        if quant:
                            sync.wait_ge(qp.h, b_qp + marks[i - 3]['xc'][1] + 1)
                        else:
                            sync.wait_ge(xq.h, b_xq + i - 2)
                    _inc(sync.dma_start(out=xbuf[i % 3][:], in_=src_tiles[i]), ld3[i % 3], 16)

            @block.scalar
            def _(scalar):
                def emit_sqrt(i):
                    pr = i % 2
                    scalar.wait_ge(agg.h, b_agg + i + 1)
                    if i >= 2:
                        if quant:
                            scalar.wait_ge(qp.h, b_qp + marks[i - 2]['xc'][1] + 1)
                        else:
                            scalar.wait_ge(xq.h, b_xq + i - 1)
                    _inc(scalar.activation(sqv[pr][:], mv[pr][:, 1:2], AF.Sqrt,
                                           bias=consts["eps"][:]), sq, 1)

                def act_chunk(c):
                    h = HV[c % NSL]
                    scalar.wait_ge(vr.h, qidx[(c, 'vrk')])
                    if c >= NSL:
                        scalar.wait_ge(qv.h, qidx[(c - NSL, 't2a')])
                    _inc(scalar.activation(qyi[:, h], qidx[(c, 'src')], AF.Copy,
                                           scale=qidx[(c, 'scl')]), qa, 1)

                emit_sqrt(0)
                if quant:
                    for c in marks[0]['xc']:
                        act_chunk(c)
                for i in range(ntot):
                    pr = i % 2
                    if i + 1 < ntot:
                        emit_sqrt(i + 1)
                        if quant:
                            for c in marks[i + 1]['xc']:
                                act_chunk(c)
                    if quant:
                        scalar.wait_ge(cs.h, b_cs + 2 * i + 1)
                    for nb in range(NB):
                        G = i * NB + nb
                        scalar.wait_ge(mg.h, b_mg + G + 1)
                        if quant:
                            ins = scalar.activation(sg[nb % 2][:], psg[nb % 2][:], AF.Silu,
                                                    scale=csg[pr][:])
                        else:
                            ins = scalar.activation(sg[nb % 2][:], psg[nb % 2][:], AF.Silu,
                                                    scale=1.0 / 127.0)
                        _inc(ins, sil, 1)
                        if quant and i >= 1 and nb < NHC:
                            act_chunk(marks[i - 1]['hc'][nb])
                if quant:
                    for c in marks[ntot - 1]['hc']:
                        act_chunk(c)

            @block.tensor
            def _(tensor):
                for i in range(ntot):
                    pr = i % 2
                    if quant:
                        tensor.wait_ge(qp.h, b_qp + marks[i]['xc'][1] + 1)
                    else:
                        tensor.wait_ge(xq.h, b_xq + i + 1)
                    for k in range(DC):
                        g = i * DC + k
                        if g >= 4:
                            tensor.wait_ge(tev.h, b_tev + g - 3)
                        ins = tensor.transpose(trsl[g % 4][:],
                                               xop[pr][:, k * P:(k + 1) * P], ident[:])
                        _inc(ins, trr, 1)
                    tensor.wait_ge(tev.h, b_tev + DC * (i + 1))
                    for nb in range(NB):
                        G = i * NB + nb
                        if G >= 2:
                            tensor.wait_ge(hh.h, b_hh + G - 1)
                        for k in range(DC):
                            ins = tensor.matmul(psg[nb % 2][:], xT[pr][:, k, :],
                                                wgT[:, k, nb * 512:(nb + 1) * 512],
                                                start=(k == 0), stop=(k == DC - 1))
                            if k == DC - 1:
                                _inc(ins, mg, 1)
                        for k in range(DC):
                            ins = tensor.matmul(psu[nb % 2][:], xT[pr][:, k, :],
                                                wuT[:, k, nb * 512:(nb + 1) * 512],
                                                start=(k == 0), stop=(k == DC - 1))
                            if k == DC - 1:
                                _inc(ins, mu, 1)

            @block.gpsimd
            def _(gpsimd):
                def pool_chunk(c):
                    h = HV[c % NSL]
                    gpsimd.wait_ge(qv.h, qidx[(c, 't2a')])
                    _inc(gpsimd.tensor_tensor(qidx[(c, 'dst')], qc1[:, h], qc2[:, h],
                                              op=ALU.subtract), qp, 1)

                if quant:
                    for c in marks[0]['xc']:
                        pool_chunk(c)

                def store_tile(i):
                    pr = i % 2
                    if quant:
                        gpsimd.wait_ge(qp.h, b_qp + marks[i]['hc'][-1] + 1)
                    gpsimd.wait_ge(hh.h, b_hh + (i + 1) * NB)
                    _inc(gpsimd.dma_start(out=h_scr[i * P:(i + 1) * P, :], in_=hsb[pr][:]),
                         sth2[i % 2], 16)

                for i in range(ntot):
                    if quant and i + 1 < ntot:
                        for c in marks[i + 1]['xc']:
                            pool_chunk(c)
                    if quant and i >= 1:
                        for c in marks[i - 1]['hc']:
                            pool_chunk(c)
                        store_tile(i - 1)
                    if not quant:
                        store_tile(i)
                if quant:
                    for c in marks[ntot - 1]['hc']:
                        pool_chunk(c)
                    store_tile(ntot - 1)
                for q in range(2):
                    gpsimd.wait_ge(sth2[q].h, sth2[q].tot)


def emit_ffn_passB(ctx, xd_tiles, h_scr, wdT, H, consts, quant, qd, tag,
                  bf16_copy_ap=None):
    ntot = len(xd_tiles)
    nc = ctx.nc
    HC = H // P
    NBAT = (HC + 7) // 8
    ident = consts["ident"]
    ldh2 = ctx.chn(f"ldhB{tag}", 2)
    ldx2 = ctx.chn(f"ldxB{tag}", 2)
    trr = ctx.ch(f"trrB{tag}")
    tev = ctx.ch(f"tevB{tag}")
    md = ctx.ch(f"mdB{tag}")
    epi = ctx.ch(f"epiB{tag}")
    cbf = ctx.ch(f"cbfB{tag}")
    st2 = ctx.chn(f"stB{tag}", 2)
    stb2 = ctx.chn(f"stbB{tag}", 2)
    vrB = ctx.ch(f"vrB{tag}")
    b_vrB = vrB.base()
    b_ldh2 = [c.base() for c in ldh2]
    b_ldx2 = [c.base() for c in ldx2]
    b_trr, b_tev, b_md = trr.base(), tev.base(), md.base()
    b_epi, b_cbf = epi.base(), cbf.base()
    b_st2 = [c.base() for c in st2]
    b_stb2 = [c.base() for c in stb2]

    with contextlib.ExitStack() as es2:
        sb = lambda n, s, d: es2.enter_context(nc.sbuf_tensor(n, s, d))
        ps = lambda n, s, d: es2.enter_context(nc.psum_tensor(n, s, d))
        hbuf = [sb(f"hB{tag}{j}", [P, H], BF16) for j in range(2)]
        xbuf = [sb(f"xB{tag}{j}", [P, D], F32) for j in range(2)]
        hT = [sb(f"hTB{tag}{j}", [P, HC, P], BF16) for j in range(2)]
        obuf = [sb(f"oB{tag}{j}", [P, D], F32) for j in range(2)]
        obf = [sb(f"obB{tag}{j}", [P, D], BF16) for j in range(2)] if bf16_copy_ap is not None else None
        csd = [sb(f"csdB{tag}{j}", [P, 1], F32) for j in range(2)] if quant else None
        trsl = [ps(f"ptB{tag}{j}", [P, P], BF16) for j in range(4)]
        psd = [ps(f"pdB{tag}{j}", [P, 512], F32) for j in range(4)]

        with nc.Block() as block:
            @block.sync
            def _(sync):
                for i in range(ntot):
                    if i >= 2:
                        sync.wait_ge(trr.h, b_trr + (i - 1) * HC)
                        sync.wait_ge(epi.h, b_epi + 2 * (i - 1))
                    _inc(sync.dma_start(out=hbuf[i % 2][:], in_=h_scr[i * P:(i + 1) * P, :]),
                         ldh2[i % 2], 16)
                    _inc(sync.dma_start(out=xbuf[i % 2][:], in_=xd_tiles[i][0]),
                         ldx2[i % 2], 16)

            @block.tensor
            def _(tensor):
                for i in range(ntot):
                    pr = i % 2
                    tensor.wait_ge(ldh2[pr].h, b_ldh2[pr] + 16 * (i // 2 + 1))
                    for k in range(HC):
                        g = i * HC + k
                        if g >= 4:
                            tensor.wait_ge(tev.h, b_tev + g - 3)
                        ins = tensor.transpose(trsl[g % 4][:],
                                               hbuf[pr][:, k * P:(k + 1) * P], ident[:])
                        _inc(ins, trr, 1)
                    tensor.wait_ge(tev.h, b_tev + (i + 1) * HC)
                    for nb in range(2):
                        if i >= 2:
                            tensor.wait_ge(epi.h, b_epi + 2 * (i - 1))
                        for k in range(HC):
                            ins = tensor.matmul(psd[2 * pr + nb][:], hT[pr][:, k, :],
                                                wdT[:, k, nb * 512:(nb + 1) * 512],
                                                start=(k == 0), stop=(k == HC - 1))
                            if k == HC - 1:
                                _inc(ins, md, 1)

            @block.vector
            def _(vector):
                for i in range(ntot):
                    pr = i % 2
                    if quant:
                        if i >= 2:
                            vector.wait_ge(epi.h, b_epi + 2 * (i - 1))
                        _inc(vector.tensor_scalar(csd[pr][:], qd["inv_sh_all"][:, i:i + 1], 0.5,
                                                  qd["m_d"][:], op0=ALU.mult, op1=ALU.mult),
                             vrB, 1)
                    if i >= 2:
                        vector.wait_ge(md.h, b_md + 2 * (i - 1))
                    for k in range(HC):
                        g = i * HC + k
                        vector.wait_ge(trr.h, b_trr + g + 1)
                        ins = vector.tensor_copy(hT[pr][:, k, :], trsl[g % 4][:])
                        _inc(ins, tev, 1)
                    if i >= 2:
                        vector.wait_ge(st2[pr].h, b_st2[pr] + 16 * (i // 2))
                    vector.wait_ge(ldx2[pr].h, b_ldx2[pr] + 16 * (i // 2 + 1))
                    if quant:
                        vector.wait_ge(vrB.h, b_vrB + i + 1)
                    for nb in range(2):
                        vector.wait_ge(md.h, b_md + 2 * i + nb + 1)
                        sl = slice(nb * 512, (nb + 1) * 512)
                        if quant:
                            ins = vector.scalar_tensor_tensor(obuf[pr][:, sl],
                                                              psd[2 * pr + nb][:], csd[pr][:],
                                                              xbuf[pr][:, sl],
                                                              op0=ALU.mult, op1=ALU.add)
                        else:
                            ins = vector.scalar_tensor_tensor(obuf[pr][:, sl],
                                                              psd[2 * pr + nb][:],
                                                              0.5 / (127.0 * 127.0),
                                                              xbuf[pr][:, sl],
                                                              op0=ALU.mult, op1=ALU.add)
                        _inc(ins, epi, 1)

            @block.scalar
            def _(scalar):
                if bf16_copy_ap is not None:
                    for i in range(ntot):
                        pr = i % 2
                        scalar.wait_ge(epi.h, b_epi + 2 * (i + 1))
                        if i >= 2:
                            scalar.wait_ge(stb2[pr].h, b_stb2[pr] + 16 * (i // 2))
                        _inc(scalar.activation(obf[pr][:], obuf[pr][:], AF.Copy), cbf, 1)

            @block.gpsimd
            def _(gpsimd):
                for i in range(ntot):
                    pr = i % 2
                    gpsimd.wait_ge(epi.h, b_epi + 2 * (i + 1))
                    _inc(gpsimd.dma_start(out=xd_tiles[i][1], in_=obuf[pr][:]),
                         st2[pr], 16)
                    if bf16_copy_ap is not None:
                        gpsimd.wait_ge(cbf.h, b_cbf + i + 1)
                        _inc(gpsimd.dma_start(out=bf16_copy_ap[i * P:(i + 1) * P, :],
                                              in_=obf[pr][:]), stb2[pr], 16)
                for q in range(2):
                    gpsimd.wait_ge(st2[q].h, st2[q].tot)
                    if bf16_copy_ap is not None:
                        gpsimd.wait_ge(stb2[q].h, stb2[q].tot)




def emit_attn0(ctx, src_ap, dst_ap, halo_ap, halo_dt, cos_h, sin_h, cosk_h, sink_h,
               Wq1_h, Wq2_h, Wk1_h, Wk2_h, Wv_h, Wout_h, bout_h,
               lq1_h, lk1_h, lq2_h, lk2_h, consts, tag="a0"):
    nc = ctx.nc
    ident = consts["ident"]
    NW = 16
    U = NW * 2 * 16
    es0 = contextlib.ExitStack()
    sb = lambda n, s, d: es0.enter_context(nc.sbuf_tensor(n, s, d))

    q1T = sb(f"q1T{tag}", [P, DC, HALO], BF16)
    q2T = sb(f"q2T{tag}", [P, DC, HALO], BF16)
    k1T = sb(f"k1T{tag}", [P, HALO], BF16)
    k2T = sb(f"k2T{tag}", [P, HALO], BF16)
    v_sb = sb(f"v{tag}", [P, NHT, P], BF16)
    neglam = sb(f"nlam{tag}", [P, 1], F32)

    with contextlib.ExitStack() as esw:
        wsb = lambda n, s, d: esw.enter_context(nc.sbuf_tensor(n, s, d))
        wps = lambda n, s, d: esw.enter_context(nc.psum_tensor(n, s, d))
        Wq1 = wsb(f"Wq1{tag}", [P, DC, D], BF16)
        Wq2 = wsb(f"Wq2{tag}", [P, DC, D], BF16)
        Wkv = wsb(f"Wkv{tag}", [P, DC, 384], BF16)
        wf = [wsb(f"wf{tag}{j}", [P, D], F32) for j in range(2)]
        lvec = wsb(f"lv{tag}", [1, 256], F32)
        lsc = wsb(f"lsc{tag}", [1, 8], F32)
        lw2 = ctx.chn(f"lw{tag}", 2)
        cwc = ctx.ch(f"cw{tag}")
        lml = ctx.ch(f"lml{tag}")
        lmv = ctx.ch(f"lmv{tag}")
        lme = ctx.ch(f"lme{tag}")
        b_lw2 = [c.base() for c in lw2]
        b_cw, b_lml = cwc.base(), lml.base()
        b_lmv, b_lme = lmv.base(), lme.base()
        with nc.Block(no_gpsimd_drain=True) as block:
            @block.sync
            def _(sync):
                j = 0
                for src_h in (Wq1_h, Wq2_h):
                    for k in range(DC):
                        if j >= 2:
                            sync.wait_ge(cwc.h, b_cw + j - 1)
                        _inc(sync.dma_start(out=wf[j % 2][:],
                                            in_=src_h[k * P:(k + 1) * P, :]), lw2[j % 2], 16)
                        j += 1
                for k in range(DC):
                    if j >= 2:
                        sync.wait_ge(cwc.h, b_cw + j - 1)
                    _inc(sync.dma_start(out=wf[j % 2][:, 0:128],
                                        in_=Wk1_h[k * P:(k + 1) * P, :]), lw2[j % 2], 16)
                    _inc(sync.dma_start(out=wf[j % 2][:, 128:256],
                                        in_=Wk2_h[k * P:(k + 1) * P, :]), lw2[j % 2], 16)
                    _inc(sync.dma_start(out=wf[j % 2][:, 256:384],
                                        in_=Wv_h[k * P:(k + 1) * P, :]), lw2[j % 2], 16)
                    j += 1
                for li, lh in enumerate((lq1_h, lk1_h, lq2_h, lk2_h)):
                    _inc(sync.dma_start(out=lvec[:, li * 64:(li + 1) * 64],
                                        in_=lh.ap().rearrange("(one d) -> one d", one=1)),
                         lml, 16)

            @block.scalar
            def _(scalar):
                j = 0
                nld = [0, 0]
                for dst_T in (Wq1, Wq2):
                    for k in range(DC):
                        nld[j % 2] += 16
                        scalar.wait_ge(lw2[j % 2].h, b_lw2[j % 2] + nld[j % 2])
                        if j >= 4:
                            pass
                        _inc(scalar.activation(dst_T[:, k, :], wf[j % 2][:], AF.Copy), cwc, 1)
                        j += 1
                for k in range(DC):
                    nld[j % 2] += 48
                    scalar.wait_ge(lw2[j % 2].h, b_lw2[j % 2] + nld[j % 2])
                    _inc(scalar.activation(Wkv[:, k, :], wf[j % 2][:, 0:384], AF.Copy), cwc, 1)
                    j += 1
                scalar.wait_ge(lmv.h, b_lmv + 4)
                _inc(scalar.activation(lsc[:, 2:3], lsc[:, 0:1], AF.Exp), lme, 1)
                _inc(scalar.activation(lsc[:, 3:4], lsc[:, 1:2], AF.Exp), lme, 1)

            @block.vector
            def _(vector):
                vector.wait_ge(lml.h, b_lml + 64)
                _inc(vector.tensor_tensor(lvec[:, 0:64], lvec[:, 0:64], lvec[:, 64:128],
                                          op=ALU.mult), lmv, 1)
                vector.wait_ge(lmv.h, lmv.tot)
                _inc(vector.tensor_reduce(lsc[:, 0:1], lvec[:, 0:64], axis=AX.X, op=ALU.add),
                     lmv, 1)
                vector.wait_ge(lmv.h, lmv.tot)
                _inc(vector.tensor_tensor(lvec[:, 128:192], lvec[:, 128:192],
                                          lvec[:, 192:256], op=ALU.mult), lmv, 1)
                vector.wait_ge(lmv.h, lmv.tot)
                _inc(vector.tensor_reduce(lsc[:, 1:2], lvec[:, 128:192], axis=AX.X,
                                          op=ALU.add), lmv, 1)
                vector.wait_ge(lme.h, b_lme + 2)
                _inc(vector.tensor_tensor(lsc[:, 4:5], lsc[:, 2:3], lsc[:, 3:4],
                                          op=ALU.subtract), lmv, 1)
                vector.wait_ge(lmv.h, lmv.tot)
                _inc(vector.tensor_scalar(lsc[:, 4:5], lsc[:, 4:5], 0.8, None, op0=ALU.add),
                     lmv, 1)
                vector.wait_ge(lmv.h, lmv.tot)
                _inc(vector.tensor_scalar(lsc[:, 4:5], lsc[:, 4:5], 0.9, None, op0=ALU.min),
                     lmv, 1)
                vector.wait_ge(lmv.h, lmv.tot)
                _inc(vector.tensor_scalar(lsc[:, 4:5], lsc[:, 4:5], 0.1, None, op0=ALU.max),
                     lmv, 1)
                vector.wait_ge(lmv.h, lmv.tot)
                _inc(vector.tensor_scalar(lsc[:, 5:6], lsc[:, 4:5], -1.0, None,
                                          op0=ALU.mult), lmv, 1)

        with contextlib.ExitStack() as esl:
            psl = esl.enter_context(nc.psum_tensor(f"psl{tag}", [P, 1], F32))
            with nc.Block(no_gpsimd_drain=True) as block:
                @block.tensor
                def _(tensor):
                    _inc(tensor.matmul(psl[:], consts["ones_row"][:], lsc[:, 5:6],
                                       start=True, stop=True), lme, 1)

                @block.vector
                def _(vector):
                    vector.wait_ge(lme.h, b_lme + 3)
                    vector.tensor_copy(neglam[:], psl[:])

        xhb = [wsb(f"xh{tag}{j}", [P, D], halo_dt) for j in range(3)]
        cosb = [wsb(f"cos{tag}{j}", [P, 512], F32) for j in range(2)]
        sinb = [wsb(f"sin{tag}{j}", [P, 512], F32) for j in range(2)]
        coskb = [wsb(f"cosk{tag}{j}", [P, 128], F32) for j in range(2)]
        sinkb = [wsb(f"sink{tag}{j}", [P, 128], F32) for j in range(2)]
        stats = [wsb(f"st{tag}{j}", [P, 12], F32) for j in range(2)]
        mvt = [wsb(f"mv{tag}{j}", [P, 2], F32) for j in range(2)]
        sqvt = [wsb(f"sqv{tag}{j}", [P, 1], F32) for j in range(2)]
        rstd = [wsb(f"rstd{tag}{j}", [P, 1], F32) for j in range(2)]
        xnb = [wsb(f"xn{tag}{j}", [P, D], BF16) for j in range(2)]
        lnT = [wsb(f"lnT{tag}{j}", [P, DC, P], BF16) for j in range(2)]
        q1r = [wsb(f"q1r{tag}{j}", [P, D], BF16) for j in range(2)]
        q2r = [wsb(f"q2r{tag}{j}", [P, D], BF16) for j in range(2)]
        krb = [wsb(f"kr{tag}{j}", [P, 256], BF16) for j in range(2)]
        rt1 = wsb(f"rt1{tag}", [P, 512], F32)
        rt2 = wsb(f"rt2{tag}", [P, 512], F32)
        trsl = [wps(f"tr{tag}{j}", [P, P], BF16) for j in range(3)]
        psq1 = [wps(f"pq1{tag}{j}", [P, 512], F32) for j in range(2)]
        psq2 = [wps(f"pq2{tag}{j}", [P, 512], F32) for j in range(2)]
        pskv = wps(f"pkv{tag}", [P, 384], F32)

        ldx3 = ctx.chn(f"ldx{tag}", 3)
        ldc2 = ctx.chn(f"ldc{tag}", 2)
        sq = ctx.ch(f"sq{tag}")
        agg = ctx.ch(f"agg{tag}")
        vr = ctx.ch(f"vr{tag}")
        xq = ctx.ch(f"xq{tag}")
        trr = ctx.ch(f"trr{tag}")
        tev = ctx.ch(f"tev{tag}")
        rte = ctx.ch(f"rte{tag}")
        mq1 = ctx.ch(f"mq1{tag}")
        mq2 = ctx.ch(f"mq2{tag}")
        mkv = ctx.ch(f"mkv{tag}")
        ropd = ctx.ch(f"ropd{tag}")
        vev = ctx.ch(f"vev{tag}")
        b_ldx3 = [c.base() for c in ldx3]
        b_ldc2 = [c.base() for c in ldc2]
        b_sq, b_agg, b_xq = sq.base(), agg.base(), xq.base()
        b_trr, b_tev, b_rte = trr.base(), tev.base(), rte.base()
        b_mq1, b_mq2, b_mkv = mq1.base(), mq2.base(), mkv.base()
        b_ropd, b_vev = ropd.base(), vev.base()

        def slot_guard(tensor_eng, t, j):
            gt = t * 26 + j
            if gt < 3:
                return
            pt, pj = (gt - 3) // 26, (gt - 3) % 26
            if pj < 8:
                tensor_eng.wait_ge(tev.h, b_tev + pt * 8 + pj + 1)
            else:
                tensor_eng.wait_ge(rte.h, b_rte + pt * 18 + (pj - 8) + 1)

        with nc.Block(no_gpsimd_drain=True) as block:
            @block.sync
            def _(sync):
                for t in range(NHT):
                    if t >= 3:
                        sync.wait_ge(xq.h, b_xq + t - 2)
                    _inc(sync.dma_start(out=xhb[t % 3][:],
                                        in_=halo_ap[t * P:(t + 1) * P, :]), ldx3[t % 3], 16)
                    if t >= 2:
                        sync.wait_ge(ropd.h, b_ropd + 5 * (t - 1))
                    _inc(sync.dma_start(out=cosb[t % 2][:],
                                        in_=cos_h[t * P:(t + 1) * P, :]), ldc2[t % 2], 16)
                    _inc(sync.dma_start(out=sinb[t % 2][:],
                                        in_=sin_h[t * P:(t + 1) * P, :]), ldc2[t % 2], 16)
                    _inc(sync.dma_start(out=coskb[t % 2][:],
                                        in_=cosk_h[t * P:(t + 1) * P, :]), ldc2[t % 2], 16)
                    _inc(sync.dma_start(out=sinkb[t % 2][:],
                                        in_=sink_h[t * P:(t + 1) * P, :]), ldc2[t % 2], 16)

            @block.vector
            def _(vector):
                def ln(t):
                    pr = t % 2
                    vector.wait_ge(ldx3[t % 3].h, b_ldx3[t % 3] + 16 * (t // 3 + 1))
                    if t >= 2:
                        vector.wait_ge(sq.h, b_sq + t - 1)
                        vector.wait_ge(trr.h, b_trr + 26 * (t - 2) + 8)
                    vector.bn_stats(stats[pr][:, 0:6], xhb[t % 3][:, 0:512])
                    _inc(vector.bn_stats(stats[pr][:, 6:12], xhb[t % 3][:, 512:1024]), vr, 1)
                    vector.wait_ge(vr.h, vr.tot)
                    _inc(vector.bn_aggr(mvt[pr][:], stats[pr][:]), agg, 1)
                    vector.wait_ge(agg.h, agg.tot)
                    vector.wait_ge(sq.h, b_sq + t + 1)
                    _inc(vector.reciprocal(rstd[pr][:], sqvt[pr][:]), vr, 1)
                    vector.wait_ge(vr.h, vr.tot)
                    ins = vector.tensor_scalar(xnb[pr][:], xhb[t % 3][:], mvt[pr][:, 0:1],
                                               rstd[pr][:], op0=ALU.subtract, op1=ALU.mult)
                    _inc(ins, xq, 1)

                def rope(t):
                    pr = t % 2
                    vector.wait_ge(ldc2[t % 2].h, b_ldc2[t % 2] + 64 * (t // 2 + 1))
                    if t >= 2:
                        vector.wait_ge(trr.h, b_trr + 26 * (t - 1))
                    jobs = [(psq1[0], q1r[pr][:, 0:512], 8, mq1, b_mq1, 2, 1),
                            (psq1[1], q1r[pr][:, 512:1024], 8, mq1, b_mq1, 2, 2),
                            (psq2[0], q2r[pr][:, 0:512], 8, mq2, b_mq2, 2, 1),
                            (psq2[1], q2r[pr][:, 512:1024], 8, mq2, b_mq2, 2, 2),
                            (pskv[:, 0:256], krb[pr][:, 0:256], 4, mkv, b_mkv, 1, 1)]
                    for ji, (bnk, dstslice, nh, mch, mbase, per, mcnt) in enumerate(jobs):
                        vector.wait_ge(mch.h, mbase + per * t + mcnt)
                        vector.wait_ge(ropd.h, b_ropd + 5 * t + ji)
                        W2 = nh * 32
                        bv = bnk if ji == 4 else bnk[:]
                        src_e = bv.rearrange("p (h d two) -> p h d two", h=nh, d=32,
                                             two=2)[:, :, :, 0:1]
                        src_o = bv.rearrange("p (h d two) -> p h d two", h=nh, d=32,
                                             two=2)[:, :, :, 1:2]
                        dst_h0 = dstslice.rearrange("p (h two d) -> p h two d", h=nh, two=2,
                                                    d=32)[:, :, 0:1, :]
                        dst_h1 = dstslice.rearrange("p (h two d) -> p h two d", h=nh, two=2,
                                                    d=32)[:, :, 1:2, :]
                        if mch is mkv:
                            cosv = coskb[pr][:, 0:W2]
                            sinv = sinkb[pr][:, 0:W2]
                        else:
                            cosv = cosb[pr][:, 0:W2]
                            sinv = sinb[pr][:, 0:W2]
                        _inc(vector.tensor_tensor(rt1[:, 0:W2], src_e, cosv, op=ALU.mult),
                             vr, 1)
                        _inc(vector.tensor_tensor(rt2[:, 0:W2], src_o, sinv, op=ALU.mult),
                             vr, 1)
                        vector.wait_ge(vr.h, vr.tot)
                        _inc(vector.tensor_tensor(dst_h0, rt1[:, 0:W2], rt2[:, 0:W2],
                                                  op=ALU.subtract), vr, 1)
                        vector.wait_ge(vr.h, vr.tot)
                        _inc(vector.tensor_tensor(rt1[:, 0:W2], src_e, sinv, op=ALU.mult),
                             vr, 1)
                        _inc(vector.tensor_tensor(rt2[:, 0:W2], src_o, cosv, op=ALU.mult),
                             vr, 1)
                        vector.wait_ge(vr.h, vr.tot)
                        _inc(vector.tensor_tensor(dst_h1, rt1[:, 0:W2], rt2[:, 0:W2],
                                                  op=ALU.add), ropd, 1)

                ln(0)
                for t in range(NHT):
                    if t + 1 < NHT:
                        ln(t + 1)
                    pr = t % 2
                    if t >= 2:
                        vector.wait_ge(mkv.h, b_mkv + t - 1)
                    for k in range(DC):
                        vector.wait_ge(trr.h, b_trr + t * 26 + k + 1)
                        ins = vector.tensor_copy(lnT[pr][:, k, :], trsl[(t * 26 + k) % 3][:])
                        _inc(ins, tev, 1)
                    rope(t)

            @block.scalar
            def _(scalar):
                def sqrt_t(t):
                    pr = t % 2
                    scalar.wait_ge(agg.h, b_agg + t + 1)
                    if t >= 2:
                        scalar.wait_ge(xq.h, b_xq + t - 1)
                    _inc(scalar.activation(sqvt[pr][:], mvt[pr][:, 1:2], AF.Sqrt,
                                           bias=consts["eps"][:]), sq, 1)
                sqrt_t(0)
                for t in range(NHT):
                    if t + 1 < NHT:
                        sqrt_t(t + 1)
                    scalar.wait_ge(mkv.h, b_mkv + t + 1)
                    if t >= 1:
                        scalar.wait_ge(ropd.h, b_ropd + 5 * t)
                    _inc(scalar.activation(v_sb[:, t, :], pskv[:, 256:384], AF.Copy), vev, 1)
                    for j in range(18):
                        gt = t * 26 + 8 + j
                        scalar.wait_ge(trr.h, b_trr + gt + 1)
                        if j < 8:
                            dst = q1T[:, j, t * P:(t + 1) * P]
                        elif j < 16:
                            dst = q2T[:, j - 8, t * P:(t + 1) * P]
                        elif j == 16:
                            dst = k1T[:, t * P:(t + 1) * P]
                        else:
                            dst = k2T[:, t * P:(t + 1) * P]
                        _inc(scalar.activation(dst, trsl[gt % 3][:], AF.Copy), rte, 1)

            @block.tensor
            def _(tensor):
                for t in range(NHT):
                    pr = t % 2
                    tensor.wait_ge(xq.h, b_xq + t + 1)
                    for k in range(DC):
                        slot_guard(tensor, t, k)
                        ins = tensor.transpose(trsl[(t * 26 + k) % 3][:],
                                               xnb[pr][:, k * P:(k + 1) * P], ident[:])
                        _inc(ins, trr, 1)
                    tensor.wait_ge(tev.h, b_tev + 8 * (t + 1))
                    if t >= 1:
                        tensor.wait_ge(ropd.h, b_ropd + 5 * t)
                        tensor.wait_ge(vev.h, b_vev + t)
                    for bq in range(2):
                        for k in range(DC):
                            ins = tensor.matmul(psq1[bq][:], lnT[pr][:, k, :],
                                                Wq1[:, k, bq * 512:(bq + 1) * 512],
                                                start=(k == 0), stop=(k == DC - 1))
                        _inc(ins, mq1, 1)
                    for bq in range(2):
                        for k in range(DC):
                            ins = tensor.matmul(psq2[bq][:], lnT[pr][:, k, :],
                                                Wq2[:, k, bq * 512:(bq + 1) * 512],
                                                start=(k == 0), stop=(k == DC - 1))
                        _inc(ins, mq2, 1)
                    for k in range(DC):
                        ins = tensor.matmul(pskv[:], lnT[pr][:, k, :], Wkv[:, k, :],
                                            start=(k == 0), stop=(k == DC - 1))
                    _inc(ins, mkv, 1)
                    tensor.wait_ge(ropd.h, b_ropd + 5 * (t + 1))
                    for j in range(18):
                        slot_guard(tensor, t, 8 + j)
                        if j < 8:
                            srcv = q1r[pr][:, j * P:(j + 1) * P]
                        elif j < 16:
                            srcv = q2r[pr][:, (j - 8) * P:(j - 7) * P]
                        elif j == 16:
                            srcv = krb[pr][:, 0:128]
                        else:
                            srcv = krb[pr][:, 128:256]
                        ins = tensor.transpose(trsl[(t * 26 + 8 + j) % 3][:], srcv, ident[:])
                        _inc(ins, trr, 1)

    A_sb = sb(f"A{tag}", [P, 32, D], BF16)
    with contextlib.ExitStack() as esb:
        bsb = lambda n, s, d: esb.enter_context(nc.sbuf_tensor(n, s, d))
        bps = lambda n, s, d: esb.enter_context(nc.psum_tensor(n, s, d))
        p_sb2 = [bsb(f"p2{tag}{j}", [P, 2, 256], BF16) for j in range(2)]
        pt_sb = bsb(f"pt{tag}", [P, 4, P], BF16)
        nm1 = [bsb(f"nm1{tag}{j}", [P, 1], F32) for j in range(2)]
        nm2 = [bsb(f"nm2{tag}{j}", [P, 1], F32) for j in range(2)]
        bb1 = [bsb(f"bb1{tag}{j}", [P, 1], F32) for j in range(2)]
        bb2 = [bsb(f"bb2{tag}{j}", [P, 1], F32) for j in range(2)]
        ss1 = [bsb(f"ss1{tag}{j}", [P, 1], F32) for j in range(2)]
        ss2 = [bsb(f"ss2{tag}{j}", [P, 1], F32) for j in range(2)]
        rr1 = [bsb(f"rr1{tag}{j}", [P, 1], F32) for j in range(2)]
        rr2 = [bsb(f"rr2{tag}{j}", [P, 1], F32) for j in range(2)]
        tmpa = bsb(f"tmpa{tag}", [P, 64], F32)
        tmpb = bsb(f"tmpb{tag}", [P, 64], F32)
        pss1 = [bps(f"ps1{tag}{j}", [P, 256], F32) for j in range(2)]
        pss2 = [bps(f"ps2{tag}{j}", [P, 256], F32) for j in range(2)]
        pts = [bps(f"pts{tag}{j}", [P, P], BF16) for j in range(2)]
        psa1 = bps(f"pa1{tag}", [P, 64], F32)
        psa2 = bps(f"pa2{tag}", [P, 64], F32)

        mqk = ctx.ch(f"mqk{tag}")
        nmb = ctx.ch(f"nmb{tag}")
        expd = ctx.ch(f"expd{tag}")
        rch = ctx.ch(f"rch{tag}")
        ptrc = ctx.ch(f"ptr{tag}")
        ptev = ctx.ch(f"ptev{tag}")
        mpv = ctx.ch(f"mpv{tag}")
        cmb = ctx.ch(f"cmb{tag}")
        b_mqk, b_nmb, b_expd, b_rch = mqk.base(), nmb.base(), expd.base(), rch.base()
        b_ptr, b_ptev, b_mpv, b_cmb = ptrc.base(), ptev.base(), mpv.base(), cmb.base()

        def unit(u):
            a, h = u // 16, u % 16
            w, qh = a // 2, a % 2
            return w, qh, h, a

        with nc.Block(no_gpsimd_drain=True) as block:
            @block.tensor
            def _(tensor):
                for s in range(U + 2):
                    if s < U:
                        u = s
                        w, qh, h, a = unit(u)
                        g = h % 2
                        qcol = 128 * (w + qh)
                        kcol = 128 * w
                        if u >= 2:
                            tensor.wait_ge(expd.h, b_expd + 2 * (u - 1))
                        ins = tensor.matmul(pss1[u % 2][:],
                                            q1T[64 * g:64 * g + 64, h // 2, qcol:qcol + 128],
                                            k1T[64 * g:64 * g + 64, kcol:kcol + 256],
                                            start=True, stop=True)
                        _inc(ins, mqk, 1)
                        ins = tensor.matmul(pss2[u % 2][:],
                                            q2T[64 * g:64 * g + 64, h // 2, qcol:qcol + 128],
                                            k2T[64 * g:64 * g + 64, kcol:kcol + 256],
                                            start=True, stop=True)
                        _inc(ins, mqk, 1)
                    if 2 <= s:
                        u = s - 2
                        w, qh, h, a = unit(u)
                        g = h % 2
                        tensor.wait_ge(ptev.h, b_ptev + 4 * (u + 1))
                        if u >= 1:
                            tensor.wait_ge(cmb.h, b_cmb + 3 * u)
                        for br, dstp in ((0, psa1), (1, psa2)):
                            for j in range(2):
                                rv = v_sb[:, w + j, 64 * g:64 * g + 64]
                                ins = tensor.matmul(dstp[:], pt_sb[:, 2 * br + j, :], rv,
                                                    start=(j == 0), stop=(j == 1))
                            _inc(ins, mpv, 1)
                    if 1 <= s <= U:
                        u = s - 1
                        tensor.wait_ge(expd.h, b_expd + 2 * (u + 1))
                        for j in range(4):
                            gj = 4 * u + j
                            if gj >= 2:
                                tensor.wait_ge(ptev.h, b_ptev + gj - 1)
                            br, half = j // 2, j % 2
                            srcv = p_sb2[u % 2][:, br, half * 128:(half + 1) * 128]
                            ins = tensor.transpose(pts[gj % 2][:], srcv, ident[:])
                            _inc(ins, ptrc, 1)

            @block.vector
            def _(vector):
                for s in range(U + 2):
                    if s < U:
                        u = s
                        pr = u % 2
                        if u >= 2:
                            vector.wait_ge(expd.h, b_expd + 2 * (u - 1))
                        vector.wait_ge(mqk.h, b_mqk + 2 * u + 1)
                        _inc(vector.tensor_reduce(nm1[pr][:], pss1[pr][:], axis=AX.X,
                                                  op=ALU.max, negate=True), nmb, 1)
                        vector.wait_ge(mqk.h, b_mqk + 2 * u + 2)
                        _inc(vector.tensor_reduce(nm2[pr][:], pss2[pr][:], axis=AX.X,
                                                  op=ALU.max, negate=True), nmb, 1)
                    if 1 <= s <= U:
                        u = s - 1
                        pr = u % 2
                        vector.wait_ge(expd.h, b_expd + 2 * (u + 1))
                        if u >= 2:
                            vector.wait_ge(cmb.h, b_cmb + 3 * (u - 1))
                        _inc(vector.reciprocal(rr1[pr][:], ss1[pr][:]), rch, 1)
                        _inc(vector.reciprocal(rr2[pr][:], ss2[pr][:]), rch, 1)
                    if 2 <= s:
                        u = s - 2
                        pr = u % 2
                        w, qh, h, a = unit(u)
                        vector.wait_ge(mpv.h, b_mpv + 2 * (u + 1))
                        vector.wait_ge(rch.h, b_rch + 2 * (u + 1))
                        _inc(vector.tensor_scalar(tmpa[:], psa1[:], rr1[pr][:], None,
                                                  op0=ALU.mult), cmb, 1)
                        _inc(vector.tensor_scalar(tmpb[:], psa2[:], rr2[pr][:], neglam[:],
                                                  op0=ALU.mult, op1=ALU.mult), cmb, 1)
                        vector.wait_ge(cmb.h, cmb.tot)
                        ins = vector.tensor_tensor(A_sb[:, a, 64 * h:64 * h + 64],
                                                   tmpa[:], tmpb[:], op=ALU.add)
                        _inc(ins, cmb, 1)

            @block.scalar
            def _(scalar):
                for u in range(U):
                    pr = u % 2
                    scalar.wait_ge(nmb.h, b_nmb + 2 * u + 1)
                    if u >= 2:
                        scalar.wait_ge(ptrc.h, b_ptr + 4 * (u - 1))
                        scalar.wait_ge(rch.h, b_rch + 2 * (u - 1))
                    _inc(scalar.activation(p_sb2[pr][:, 0, :], pss1[pr][:], AF.Exp,
                                           bias=nm1[pr][:],
                                           accum_out=ss1[pr][:]), expd, 1)
                    scalar.wait_ge(nmb.h, b_nmb + 2 * u + 2)
                    _inc(scalar.activation(p_sb2[pr][:, 1, :], pss2[pr][:], AF.Exp,
                                           bias=nm2[pr][:],
                                           accum_out=ss2[pr][:]), expd, 1)
                    if u >= 1:
                        v = u - 1
                        scalar.wait_ge(mpv.h, b_mpv + 2 * v)
                        for j in range(4):
                            gj = 4 * v + j
                            scalar.wait_ge(ptrc.h, b_ptr + gj + 1)
                            _inc(scalar.activation(pt_sb[:, j, :], pts[gj % 2][:], AF.Copy),
                                 ptev, 1)
                for v in (U - 1,):
                    scalar.wait_ge(mpv.h, b_mpv + 2 * v)
                    for j in range(4):
                        gj = 4 * v + j
                        scalar.wait_ge(ptrc.h, b_ptr + gj + 1)
                        _inc(scalar.activation(pt_sb[:, j, :], pts[gj % 2][:], AF.Copy),
                             ptev, 1)

    with contextlib.ExitStack() as esc:
        csb = lambda n, s, d: esc.enter_context(nc.sbuf_tensor(n, s, d))
        cps = lambda n, s, d: esc.enter_context(nc.psum_tensor(n, s, d))
        Wout = csb(f"Wout{tag}", [P, DC, D], BF16)
        bout_row = csb(f"bor{tag}", [1, D], F32)
        bout_b = csb(f"bob{tag}", [P, D], F32)
        wf2 = [csb(f"wf2{tag}{j}", [P, D], F32) for j in range(2)]
        lwo2 = ctx.chn(f"lw{tag}", 2)
        cwo = ctx.ch(f"cw{tag}")
        mbo = ctx.ch(f"lme{tag}")
        ldbo = ctx.chn(f"ldc{tag}", 2)[0]
        b_lwo2 = [c.base() for c in lwo2]
        b_cwo, b_mbo = cwo.base(), mbo.base()
        b_ldbo = ldbo.base()
        poC = [cps(f"po{tag}{j}", [P, 512], F32) for j in range(2)]
        with nc.Block(no_gpsimd_drain=True) as block:
            @block.sync
            def _(sync):
                for k in range(DC):
                    if k >= 2:
                        sync.wait_ge(cwo.h, b_cwo + k - 1)
                    _inc(sync.dma_start(out=wf2[k % 2][:],
                                        in_=Wout_h[k * P:(k + 1) * P, :]), lwo2[k % 2], 16)
                _inc(sync.dma_start(out=bout_row[:],
                                    in_=bout_h.ap().rearrange("(one d) -> one d", one=1)),
                     ldbo, 16)

            @block.scalar
            def _(scalar):
                for k in range(DC):
                    scalar.wait_ge(lwo2[k % 2].h, b_lwo2[k % 2] + 16 * (k // 2 + 1))
                    _inc(scalar.activation(Wout[:, k, :], wf2[k % 2][:], AF.Copy), cwo, 1)

            @block.tensor
            def _(tensor):
                tensor.wait_ge(ldbo.h, b_ldbo + 16)
                _inc(tensor.matmul(poC[0][:], consts["ones_row"][:], bout_row[:, 0:512],
                                   start=True, stop=True), mbo, 1)
                _inc(tensor.matmul(poC[1][:], consts["ones_row"][:], bout_row[:, 512:1024],
                                   start=True, stop=True), mbo, 1)

            @block.vector
            def _(vector):
                vector.wait_ge(mbo.h, b_mbo + 2)
                vector.tensor_copy(bout_b[:, 0:512], poC[0][:])
                vector.tensor_copy(bout_b[:, 512:1024], poC[1][:])

        xbC = [csb(f"xbC{tag}{j}", [P, D], F32) for j in range(2)]
        AT = [csb(f"AT{tag}{j}", [P, DC, P], BF16) for j in range(2)]
        obuf = [csb(f"obC{tag}{j}", [P, D], F32) for j in range(2)]
        trslC = [cps(f"trC{tag}{j}", [P, P], BF16) for j in range(4)]
        ldxC = ctx.chn(f"ldx{tag}", 2)
        trrC = ctx.ch(f"trr{tag}")
        tevC = ctx.ch(f"tev{tag}")
        moC = ctx.ch(f"mq1{tag}")
        vrC = ctx.ch(f"vr{tag}")
        epoC = ctx.ch(f"xq{tag}")
        stoC = ctx.chn("qssqw", 2)
        b_ldxC = [c.base() for c in ldxC]
        b_trrC, b_tevC, b_moC = trrC.base(), tevC.base(), moC.base()
        b_vrC, b_epoC = vrC.base(), epoC.base()
        b_stoC = [c.base() for c in stoC]
        with nc.Block() as block:
            @block.sync
            def _(sync):
                for a in range(32):
                    if a >= 2:
                        sync.wait_ge(epoC.h, b_epoC + 2 * (a - 1))
                    _inc(sync.dma_start(out=xbC[a % 2][:],
                                        in_=src_ap[a * P:(a + 1) * P, :]), ldxC[a % 2], 16)

            @block.tensor
            def _(tensor):
                for a in range(32):
                    pr = a % 2
                    for k in range(DC):
                        g = a * DC + k
                        if g >= 4:
                            tensor.wait_ge(tevC.h, b_tevC + g - 3)
                        ins = tensor.transpose(trslC[g % 4][:], A_sb[:, a, k * P:(k + 1) * P],
                                               ident[:])
                        _inc(ins, trrC, 1)
                    tensor.wait_ge(tevC.h, b_tevC + DC * (a + 1))
                    if a >= 1:
                        tensor.wait_ge(epoC.h, b_epoC + 2 * a)
                    for nb in range(2):
                        for k in range(DC):
                            ins = tensor.matmul(poC[nb][:], AT[pr][:, k, :],
                                                Wout[:, k, nb * 512:(nb + 1) * 512],
                                                start=(k == 0), stop=(k == DC - 1))
                        _inc(ins, moC, 1)

            @block.vector
            def _(vector):
                for a in range(32):
                    pr = a % 2
                    if a >= 2:
                        vector.wait_ge(moC.h, b_moC + 2 * (a - 1))
                    for k in range(DC):
                        g = a * DC + k
                        vector.wait_ge(trrC.h, b_trrC + g + 1)
                        ins = vector.tensor_copy(AT[pr][:, k, :], trslC[g % 4][:])
                        _inc(ins, tevC, 1)
                    vector.wait_ge(ldxC[pr].h, b_ldxC[pr] + 16 * (a // 2 + 1))
                    if a >= 2:
                        vector.wait_ge(stoC[pr].h, b_stoC[pr] + 16 * (a // 2))
                    for nb in range(2):
                        vector.wait_ge(moC.h, b_moC + 2 * a + nb + 1)
                        sl = slice(nb * 512, (nb + 1) * 512)
                        _inc(vector.tensor_tensor(obuf[pr][:, sl], poC[nb][:],
                                                  xbC[pr][:, sl], op=ALU.add), vrC, 1)
                        vector.wait_ge(vrC.h, vrC.tot)
                        _inc(vector.tensor_tensor(obuf[pr][:, sl], obuf[pr][:, sl],
                                                  bout_b[:, sl], op=ALU.add), epoC, 1)

            @block.gpsimd
            def _(gpsimd):
                for a in range(32):
                    pr = a % 2
                    gpsimd.wait_ge(epoC.h, b_epoC + 2 * (a + 1))
                    _inc(gpsimd.dma_start(out=dst_ap[a * P:(a + 1) * P, :], in_=obuf[pr][:]),
                         stoC[pr], 16)
                for q in range(2):
                    gpsimd.wait_ge(stoC[q].h, stoC[q].tot)
    es0.close()


def emit_attn1(ctx, src_ap, dst_ap, Wq_h, bq_h, Wk_h, bk_h, Wv_h, bv_h, consts,
               tag="a1"):
    nc = ctx.nc
    ident = consts["ident"]
    with contextlib.ExitStack() as es2:
        sb = lambda n, s, d: es2.enter_context(nc.sbuf_tensor(n, s, d))
        ps = lambda n, s, d: es2.enter_context(nc.psum_tensor(n, s, d))
        WqT = sb(f"WqT{tag}", [P, DC, D], BF16)
        WkT = sb(f"WkT{tag}", [P, DC, D], BF16)
        WvT = sb(f"WvT{tag}", [P, DC, D], BF16)
        emit_conv_int_weight(ctx, Wq_h, D, D, WqT, ident, "cv", src_dt=F32)
        emit_conv_int_weight(ctx, Wk_h, D, D, WkT, ident, "cv", src_dt=F32)
        emit_conv_int_weight(ctx, Wv_h, D, D, WvT, ident, "cv", src_dt=F32)
        bq_sb = sb(f"bq{tag}", [P, DC], F32)
        bk_sb = sb(f"bk{tag}", [P, DC], F32)
        bv_row = sb(f"bvr{tag}", [1, D], F32)
        bv_b = sb(f"bvb{tag}", [P, D], F32)
        es_b = contextlib.ExitStack()
        psb = [es_b.enter_context(nc.psum_tensor(f"psb{tag}{j}", [P, 512], F32))
               for j in range(2)]
        ldb = ctx.chn(f"ldx{tag}", 3)[0]
        mmb = ctx.ch(f"lme{tag}")
        b_ldb, b_mmb = ldb.base(), mmb.base()
        with nc.Block(no_gpsimd_drain=True) as block:
            @block.sync
            def _(sync):
                with nc.allow_non_contiguous_dma(reason="tiny bias rearrange"):
                    _inc(sync.dma_start(out=bq_sb[:], in_=bq_h.ap().rearrange("(k p) -> p k", p=P),
                                        allow_slow_non_contiguous=True), ldb, 16)
                    _inc(sync.dma_start(out=bk_sb[:], in_=bk_h.ap().rearrange("(k p) -> p k", p=P),
                                        allow_slow_non_contiguous=True), ldb, 16)
                _inc(sync.dma_start(out=bv_row[:], in_=bv_h.ap().rearrange("(one d) -> one d", one=1)),
                     ldb, 16)
                sync.wait_ge(ldb.h, b_ldb + 48)

            @block.tensor
            def _(tensor):
                tensor.wait_ge(ldb.h, b_ldb + 48)
                _inc(tensor.matmul(psb[0][:], consts["ones_row"][:], bv_row[:, 0:512],
                                   start=True, stop=True), mmb, 1)
                _inc(tensor.matmul(psb[1][:], consts["ones_row"][:], bv_row[:, 512:1024],
                                   start=True, stop=True), mmb, 1)

            @block.vector
            def _(vector):
                vector.wait_ge(mmb.h, b_mmb + 2)
                vector.tensor_copy(bv_b[:, 0:512], psb[0][:])
                vector.tensor_copy(bv_b[:, 512:1024], psb[1][:])
        es_b.close()

        lnT = sb(f"lnT{tag}", [P, DC, 1024], BF16)
        qT = sb(f"qT{tag}", [P, DC, 1024], BF16)
        kT = sb(f"kT{tag}", [P, DC, 1024], BF16)
        v_sb = sb(f"v{tag}", [P, DC, 1024], BF16)
        p_sb = [sb(f"p{tag}{j}", [P, 1024], BF16) for j in range(2)]
        pt_sb = sb(f"pt{tag}", [P, DC, P], BF16)
        xbuf = [sb(f"xa{tag}{j}", [P, D], F32) for j in range(3)]
        xn = [sb(f"xn{tag}{j}", [P, D], BF16) for j in range(2)]
        stats = [sb(f"st{tag}{j}", [P, 12], F32) for j in range(2)]
        mv = [sb(f"mv{tag}{j}", [P, 2], F32) for j in range(2)]
        sqv = [sb(f"sq{tag}{j}", [P, 1], F32) for j in range(2)]
        rstd = [sb(f"rs{tag}{j}", [P, 1], F32) for j in range(2)]
        nmt = [sb(f"nm{tag}{j}", [P, 1], F32) for j in range(3)]
        ssm = [sb(f"ss{tag}{j}", [P, 1], F32) for j in range(3)]
        rr = sb(f"rr{tag}", [P, 1], F32)
        xb2 = [sb(f"xb2{tag}{j}", [P, D], F32) for j in range(2)]
        obuf = [sb(f"ob{tag}{j}", [P, D], F32) for j in range(2)]
        trsl = [ps(f"tr{tag}{j}", [P, P], BF16) for j in range(4)]
        psqk = [ps(f"pqk{tag}{j}", [P, 512], F32) for j in range(2)]
        psv = psqk
        pss = [ps(f"pss{tag}{j}", [P, 512], F32) for j in range(2)]
        pso = psqk

        for chn in range(4):
            base = chn * 1024
            ld3 = ctx.chn(f"ldx{tag}", 3)
            sq = ctx.ch(f"sq{tag}")
            agg = ctx.ch(f"agg{tag}")
            vr = ctx.ch(f"vr{tag}")
            xq = ctx.ch(f"xq{tag}")
            trr = ctx.ch(f"trr{tag}")
            tev = ctx.ch(f"tev{tag}")
            b_ld3 = [c.base() for c in ld3]
            b_sq, b_agg, b_xq = sq.base(), agg.base(), xq.base()
            b_trr, b_tev = trr.base(), tev.base()
            with nc.Block(no_gpsimd_drain=True) as block:
                @block.sync
                def _(sync):
                    for t in range(8):
                        if t >= 3:
                            sync.wait_ge(xq.h, b_xq + t - 2)
                        _inc(sync.dma_start(out=xbuf[t % 3][:],
                                            in_=src_ap[base + t * P:base + (t + 1) * P, :]),
                             ld3[t % 3], 16)

                @block.vector
                def _(vector):
                    def ln(t):
                        pr = t % 2
                        vector.wait_ge(ld3[t % 3].h, b_ld3[t % 3] + 16 * (t // 3 + 1))
                        if t >= 2:
                            vector.wait_ge(sq.h, b_sq + t - 1)
                            vector.wait_ge(trr.h, b_trr + DC * (t - 1))
                        vector.bn_stats(stats[pr][:, 0:6], xbuf[t % 3][:, 0:512])
                        _inc(vector.bn_stats(stats[pr][:, 6:12], xbuf[t % 3][:, 512:1024]), vr, 1)
                        vector.wait_ge(vr.h, vr.tot)
                        _inc(vector.bn_aggr(mv[pr][:], stats[pr][:]), agg, 1)
                        vector.wait_ge(agg.h, agg.tot)
                        vector.wait_ge(sq.h, b_sq + t + 1)
                        _inc(vector.reciprocal(rstd[pr][:], sqv[pr][:]), vr, 1)
                        vector.wait_ge(vr.h, vr.tot)
                        ins = vector.tensor_scalar(xn[pr][:], xbuf[t % 3][:], mv[pr][:, 0:1],
                                                   rstd[pr][:], op0=ALU.subtract, op1=ALU.mult)
                        _inc(ins, xq, 1)
                    ln(0)
                    for t in range(8):
                        if t + 1 < 8:
                            ln(t + 1)
                        for k in range(DC):
                            g = t * DC + k
                            vector.wait_ge(trr.h, b_trr + g + 1)
                            ins = vector.tensor_copy(lnT[:, k, t * P:(t + 1) * P], trsl[g % 4][:])
                            _inc(ins, tev, 1)

                @block.scalar
                def _(scalar):
                    for t in range(8):
                        pr = t % 2
                        scalar.wait_ge(agg.h, b_agg + t + 1)
                        if t >= 2:
                            scalar.wait_ge(xq.h, b_xq + t - 1)
                        _inc(scalar.activation(sqv[pr][:], mv[pr][:, 1:2], AF.Sqrt,
                                               bias=consts["eps"][:]), sq, 1)

                @block.tensor
                def _(tensor):
                    for t in range(8):
                        pr = t % 2
                        tensor.wait_ge(xq.h, b_xq + t + 1)
                        for k in range(DC):
                            g = t * DC + k
                            if g >= 4:
                                tensor.wait_ge(tev.h, b_tev + g - 3)
                            ins = tensor.transpose(trsl[g % 4][:],
                                                   xn[pr][:, k * P:(k + 1) * P], ident[:])
                            _inc(ins, trr, 1)

            mqk = ctx.ch(f"mqk{tag}")
            eqk = ctx.ch(f"lmv{tag}")
            b_mqk, b_eqk = mqk.base(), eqk.base()
            with nc.Block(no_gpsimd_drain=True) as block:
                @block.tensor
                def _(tensor):
                    g = 0
                    for (WT,) in ((WqT,), (WkT,)):
                        for ec in range(DC):
                            for tg in range(2):
                                if g >= 2:
                                    tensor.wait_ge(eqk.h, b_eqk + g - 1)
                                for k in range(DC):
                                    ins = tensor.matmul(psqk[g % 2][:],
                                                        WT[:, k, ec * P:(ec + 1) * P],
                                                        lnT[:, k, tg * 512:(tg + 1) * 512],
                                                        start=(k == 0), stop=(k == DC - 1))
                                _inc(ins, mqk, 1)
                                g += 1
                    for t in range(8):
                        for nb in range(2):
                            if g >= 2:
                                tensor.wait_ge(eqk.h, b_eqk + g - 1)
                            for k in range(DC):
                                ins = tensor.matmul(psqk[g % 2][:],
                                                    lnT[:, k, t * P:(t + 1) * P],
                                                    WvT[:, k, nb * 512:(nb + 1) * 512],
                                                    start=(k == 0), stop=(k == DC - 1))
                            _inc(ins, mqk, 1)
                            g += 1

                @block.vector
                def _(vector):
                    g = 0
                    for wi, (dstT, bias_sb) in enumerate(((qT, bq_sb), (kT, bk_sb))):
                        for ec in range(DC):
                            for tg in range(2):
                                vector.wait_ge(mqk.h, b_mqk + g + 1)
                                ins = vector.tensor_scalar(
                                    dstT[:, ec, tg * 512:(tg + 1) * 512], psqk[g % 2][:],
                                    bias_sb[:, ec:ec + 1], None, op0=ALU.add)
                                _inc(ins, eqk, 1)
                                g += 1
                    for t in range(8):
                        for nb in range(2):
                            vector.wait_ge(mqk.h, b_mqk + g + 1)
                            ins = vector.tensor_tensor(v_sb[:, t, nb * 512:(nb + 1) * 512],
                                                       psqk[g % 2][:],
                                                       bv_b[:, nb * 512:(nb + 1) * 512],
                                                       op=ALU.add)
                            _inc(ins, eqk, 1)
                            g += 1

            ldx2 = ctx.chn(f"ldx{tag}", 2)
            msc = ctx.ch(f"mq2{tag}")
            vrc = ctx.ch(f"mkv{tag}")
            expd = ctx.ch(f"expd{tag}")
            ptr = ctx.ch(f"ptr{tag}")
            ptev = ctx.ch(f"ptev{tag}")
            mpv = ctx.ch(f"mpv{tag}")
            epo = ctx.ch(f"ropd{tag}")
            cbf = ctx.ch(f"vev{tag}")
            sto2 = ctx.chn("qssqw", 2)
            b_ldx2 = [c.base() for c in ldx2]
            b_msc, b_vrc, b_expd = msc.base(), vrc.base(), expd.base()
            b_ptr, b_ptev, b_mpv, b_epo = ptr.base(), ptev.base(), mpv.base(), epo.base()
            b_cbf = cbf.base()
            b_sto2 = [c.base() for c in sto2]
            with nc.Block() as block:
                @block.sync
                def _(sync):
                    for qt in range(8):
                        if qt >= 2:
                            sync.wait_ge(epo.h, b_epo + 2 * (qt - 1))
                        _inc(sync.dma_start(out=xb2[qt % 2][:],
                                            in_=src_ap[base + qt * P:base + (qt + 1) * P, :]),
                             ldx2[qt % 2], 16)

                @block.tensor
                def _(tensor):
                    for qt in range(8):
                        if qt >= 1:
                            tensor.wait_ge(expd.h, b_expd + 2 * qt)
                        for nb in range(2):
                            for k in range(DC):
                                ins = tensor.matmul(pss[nb][:],
                                                    qT[:, k, qt * P:(qt + 1) * P],
                                                    kT[:, k, nb * 512:(nb + 1) * 512],
                                                    start=(k == 0), stop=(k == DC - 1))
                            _inc(ins, msc, 1)
                        tensor.wait_ge(expd.h, b_expd + 2 * (qt + 1))
                        for k in range(DC):
                            g = qt * DC + k
                            if g >= 4:
                                tensor.wait_ge(ptev.h, b_ptev + g - 3)
                            ins = tensor.transpose(trsl[g % 4][:],
                                                   p_sb[qt % 2][:, k * P:(k + 1) * P], ident[:])
                            _inc(ins, ptr, 1)
                        tensor.wait_ge(ptev.h, b_ptev + DC * (qt + 1))
                        if qt >= 1:
                            tensor.wait_ge(epo.h, b_epo + 2 * qt)
                        for nb in range(2):
                            for k in range(DC):
                                ins = tensor.matmul(pso[nb][:],
                                                    pt_sb[:, k, :],
                                                    v_sb[:, k, nb * 512:(nb + 1) * 512],
                                                    start=(k == 0), stop=(k == DC - 1))
                            _inc(ins, mpv, 1)

                @block.vector
                def _(vector):
                    for qt in range(8):
                        if qt >= 1:
                            vector.wait_ge(epo.h, b_epo + 2 * qt)
                        vector.wait_ge(msc.h, b_msc + 2 * qt + 1)
                        _inc(vector.tensor_reduce(nmt[0][:], pss[0][:], axis=AX.X, op=ALU.max,
                                                  negate=True), vrc, 1)
                        vector.wait_ge(msc.h, b_msc + 2 * qt + 2)
                        _inc(vector.tensor_reduce(nmt[1][:], pss[1][:], axis=AX.X, op=ALU.max,
                                                  negate=True), vrc, 1)
                        vector.wait_ge(vrc.h, vrc.tot)
                        _inc(vector.tensor_tensor(nmt[2][:], nmt[0][:], nmt[1][:], op=ALU.min),
                             vrc, 1)
                        vector.wait_ge(expd.h, b_expd + 2 * (qt + 1))
                        _inc(vector.tensor_tensor(ssm[2][:], ssm[0][:], ssm[1][:], op=ALU.add),
                             vrc, 1)
                        vector.wait_ge(vrc.h, vrc.tot)
                        _inc(vector.reciprocal(rr[:], ssm[2][:]), vrc, 1)
                        if qt >= 1:
                            vector.wait_ge(mpv.h, b_mpv + 2 * qt)
                        for k in range(DC):
                            g = qt * DC + k
                            vector.wait_ge(ptr.h, b_ptr + g + 1)
                            ins = vector.tensor_copy(pt_sb[:, k, :], trsl[g % 4][:])
                            _inc(ins, ptev, 1)
                        vector.wait_ge(ldx2[qt % 2].h, b_ldx2[qt % 2] + 16 * (qt // 2 + 1))
                        if qt >= 2:
                            vector.wait_ge(sto2[qt % 2].h, b_sto2[qt % 2] + 16 * (qt // 2))
                        vector.wait_ge(vrc.h, vrc.tot)
                        for nb in range(2):
                            vector.wait_ge(mpv.h, b_mpv + 2 * qt + nb + 1)
                            sl = slice(nb * 512, (nb + 1) * 512)
                            ins = vector.scalar_tensor_tensor(obuf[qt % 2][:, sl], pso[nb][:],
                                                              rr[:], xb2[qt % 2][:, sl],
                                                              op0=ALU.mult, op1=ALU.add)
                            _inc(ins, epo, 1)

                @block.scalar
                def _(scalar):
                    for qt in range(8):
                        pr = qt % 2
                        scalar.wait_ge(vrc.h, b_vrc + 5 * qt + 3)
                        if qt >= 2:
                            scalar.wait_ge(ptr.h, b_ptr + DC * (qt - 1))
                        _inc(scalar.activation(p_sb[pr][:, 0:512], pss[0][:], AF.Exp,
                                               bias=nmt[2][:], accum_out=ssm[0][:]), expd, 1)
                        _inc(scalar.activation(p_sb[pr][:, 512:1024], pss[1][:], AF.Exp,
                                               bias=nmt[2][:], accum_out=ssm[1][:]), expd, 1)

                @block.gpsimd
                def _(gpsimd):
                    for qt in range(8):
                        gpsimd.wait_ge(epo.h, b_epo + 2 * (qt + 1))
                        _inc(gpsimd.dma_start(out=dst_ap[base + qt * P:base + (qt + 1) * P, :],
                                              in_=obuf[qt % 2][:]), sto2[qt % 2], 16)
                    for q in range(2):
                        gpsimd.wait_ge(sto2[q].h, sto2[q].tot)


def emit_ident(ctx, ident_h):
    nc = ctx.nc
    ident = ctx.sb("identsb", [P, P], BF16)
    ch = ctx.ch("identld")
    b = ch.base()
    with nc.Block(no_gpsimd_drain=True) as block:
        @block.sync
        def _(sync):
            _inc(sync.dma_start(out=ident[:], in_=ident_h[:, :]), ch, 16)
            sync.wait_ge(ch.h, b + 16)
    return ident


_PROG_CACHE = {}


def build_ffn0(parts=("conv", "passA", "passB")):
    ctx = Ctx()
    nc = ctx.nc
    H = 3072
    xin = nc.dram_tensor("xin", [R, D], F32, kind="ExternalInput")
    wg = nc.dram_tensor("wg", [H, D], I32, kind="ExternalInput")
    wu = nc.dram_tensor("wu", [H, D], I32, kind="ExternalInput")
    wd = nc.dram_tensor("wd", [D, H], I32, kind="ExternalInput")
    ident_h = nc.dram_tensor("ident", [P, P], BF16, kind="ExternalInput")
    xout = nc.dram_tensor("xout", [R, D], F32, kind="ExternalOutput")
    hscr = nc.dram_tensor("hscr", [R, H], BF16)
    consts = emit_consts(ctx)
    consts["ident"] = emit_ident(ctx, ident_h)
    wgT = ctx.sb("wgT", [P, DC, H], BF16)
    wuT = ctx.sb("wuT", [P, DC, H], BF16)
    wdT = ctx.sb("wdT", [P, H // P, D], BF16)
    if "conv" in parts:
        emit_conv_int_weight(ctx, wg, H, D, wgT, consts["ident"], "g0")
        emit_conv_int_weight(ctx, wu, H, D, wuT, consts["ident"], "u0")
        emit_conv_int_weight(ctx, wd, D, H, wdT, consts["ident"], "d0")
    tiles = [xin.ap()[i * P:(i + 1) * P, :] for i in range(NT)]
    xd = [(xin.ap()[i * P:(i + 1) * P, :], xout.ap()[i * P:(i + 1) * P, :]) for i in range(NT)]
    if "passA" in parts:
        emit_ffn_passA(ctx, tiles, hscr.ap(), wgT, wuT, H, consts, False, None, "f0")
    if "passB" in parts:
        emit_ffn_passB(ctx, xd, hscr.ap(), wdT, H, consts, False, None, "f0")
    else:
        chz = ctx.ch("dbgst")
        bz = chz.base()
        with nc.Block(no_gpsimd_drain=True) as block:
            @block.gpsimd
            def _(gpsimd):
                _inc(gpsimd.dma_start(out=xout[0:P, :], in_=wgT[:, 0, 0:D]), chz, 16)
                gpsimd.wait_ge(chz.h, bz + 16)
    ctx.close()
    return nc


def build_ffn1():
    ctx = Ctx()
    nc = ctx.nc
    H = 4096
    xin = nc.dram_tensor("xin", [R, D], F32, kind="ExternalInput")
    wg = nc.dram_tensor("wg", [H, D], F32, kind="ExternalInput")
    wu = nc.dram_tensor("wu", [H, D], F32, kind="ExternalInput")
    wd = nc.dram_tensor("wd", [D, H], F32, kind="ExternalInput")
    ident_h = nc.dram_tensor("ident", [P, P], BF16, kind="ExternalInput")
    xout = nc.dram_tensor("xout", [R, D], F32, kind="ExternalOutput")
    hscr = nc.dram_tensor("hscr", [R, H], BF16)
    wdT_s = nc.dram_tensor("wdTs", [P, H // P, D], BF16)
    consts = emit_consts(ctx)
    consts["ident"] = emit_ident(ctx, ident_h)
    qd = {
        "m_g": ctx.sb("m_g", [P, 1], F32),
        "m_u": ctx.sb("m_u", [P, 1], F32),
        "m_d": ctx.sb("m_d", [P, 1], F32),
        "inv_sh_all": ctx.sb("inv_sh_all", [P, NT], F32),
    }
    emit_quant_l1_weight(ctx, wd, D, H, None, consts["ident"], consts, qd["m_d"], "qd",
                         stash_ap=wdT_s.ap())
    es_gu = contextlib.ExitStack()
    ternGT = es_gu.enter_context(nc.sbuf_tensor("ternGT", [P, DC, H], BF16))
    ternUT = es_gu.enter_context(nc.sbuf_tensor("ternUT", [P, DC, H], BF16))
    emit_quant_l1_weight(ctx, wg, H, D, ternGT, consts["ident"], consts, qd["m_g"], "qg")
    emit_quant_l1_weight(ctx, wu, H, D, ternUT, consts["ident"], consts, qd["m_u"], "qu")
    tiles = [xin.ap()[i * P:(i + 1) * P, :] for i in range(NT)]
    emit_ffn_passA(ctx, tiles, hscr.ap(), ternGT, ternUT, H, consts, True, qd, "f1")
    es_gu.close()
    ternDT = ctx.sb("ternDT", [P, H // P, D], BF16)
    ch = ctx.ch("ldDT")
    bldt = ch.base()
    with ctx.nc.Block(no_gpsimd_drain=True) as block:
        @block.sync
        def _(sync):
            _inc(sync.dma_start(out=ternDT[:], in_=wdT_s[:, :, :]), ch, 16)
            sync.wait_ge(ch.h, bldt + 16)
    xd = [(xin.ap()[i * P:(i + 1) * P, :], xout.ap()[i * P:(i + 1) * P, :]) for i in range(NT)]
    emit_ffn_passB(ctx, xd, hscr.ap(), ternDT, H, consts, True, qd, "f1")
    ctx.close()
    return nc


def _ident_np():
    return np.eye(P, dtype=ml_dtypes.bfloat16)


def _x_shards(x):
    out = []
    for c in range(N_CORES):
        b, s = c // 4, (c % 4) * R
        out.append(np.ascontiguousarray(x[b, s:s + R, :], dtype=np.float32))
    return out


def _launch(name, builder, in_maps):
    if name not in _PROG_CACHE:
        _PROG_CACHE[name] = builder()
    nc = _PROG_CACHE[name]
    return run_bass_kernel_spmd(nc, in_maps, core_ids=list(range(N_CORES)))


def run_ffn0(x_shards, wg, wu, wd):
    ident = _ident_np()
    maps = [{"xin": xs, "wg": wg, "wu": wu, "wd": wd, "ident": ident} for xs in x_shards]
    res = _launch("ffn0", build_ffn0, maps)
    return [res.results[c]["xout"] for c in range(N_CORES)]


def run_ffn1(x_shards, wg, wu, wd):
    ident = _ident_np()
    wg = np.ascontiguousarray(wg, dtype=np.float32)
    wu = np.ascontiguousarray(wu, dtype=np.float32)
    wd = np.ascontiguousarray(wd, dtype=np.float32)
    maps = [{"xin": xs, "wg": wg, "wu": wu, "wd": wd, "ident": ident} for xs in x_shards]
    res = _launch("ffn1", build_ffn1, maps)
    return [res.results[c]["xout"] for c in range(N_CORES)]


def build_attn1():
    ctx = Ctx()
    nc = ctx.nc
    xin = nc.dram_tensor("xin", [R, D], F32, kind="ExternalInput")
    Wq = nc.dram_tensor("Wq", [D, D], F32, kind="ExternalInput")
    bq = nc.dram_tensor("bq", [D], F32, kind="ExternalInput")
    Wk = nc.dram_tensor("Wk", [D, D], F32, kind="ExternalInput")
    bk = nc.dram_tensor("bk", [D], F32, kind="ExternalInput")
    Wv = nc.dram_tensor("Wv", [D, D], F32, kind="ExternalInput")
    bv = nc.dram_tensor("bv", [D], F32, kind="ExternalInput")
    ident_h = nc.dram_tensor("ident", [P, P], BF16, kind="ExternalInput")
    xout = nc.dram_tensor("xout", [R, D], F32, kind="ExternalOutput")
    consts = emit_consts(ctx)
    consts["ident"] = emit_ident(ctx, ident_h)
    emit_attn1(ctx, xin.ap(), xout.ap(), Wq, bq, Wk, bk, Wv, bv, consts)
    ctx.close()
    return nc


def run_attn1(x_shards, Wq, bq, Wk, bk, Wv, bv):
    ident = _ident_np()
    maps = [{"xin": xs, "Wq": Wq, "bq": bq, "Wk": Wk, "bk": bk, "Wv": Wv, "bv": bv,
             "ident": ident} for xs in x_shards]
    res = _launch("attn1", build_attn1, maps)
    return [res.results[c]["xout"] for c in range(N_CORES)]


def build_attn0():
    ctx = Ctx()
    nc = ctx.nc
    xin = nc.dram_tensor("xin", [R, D], F32, kind="ExternalInput")
    halo = nc.dram_tensor("halo", [HALO, D], F32, kind="ExternalInput")
    cosr = nc.dram_tensor("cosr", [HALO, 512], F32, kind="ExternalInput")
    sinr = nc.dram_tensor("sinr", [HALO, 512], F32, kind="ExternalInput")
    coskr = nc.dram_tensor("coskr", [HALO, 128], F32, kind="ExternalInput")
    sinkr = nc.dram_tensor("sinkr", [HALO, 128], F32, kind="ExternalInput")
    Wq1 = nc.dram_tensor("Wq1", [D, D], F32, kind="ExternalInput")
    Wq2 = nc.dram_tensor("Wq2", [D, D], F32, kind="ExternalInput")
    Wk1 = nc.dram_tensor("Wk1", [D, 128], F32, kind="ExternalInput")
    Wk2 = nc.dram_tensor("Wk2", [D, 128], F32, kind="ExternalInput")
    Wv = nc.dram_tensor("Wv", [D, 128], F32, kind="ExternalInput")
    Wout = nc.dram_tensor("Wout", [D, D], F32, kind="ExternalInput")
    bout = nc.dram_tensor("bout", [D], F32, kind="ExternalInput")
    lq1 = nc.dram_tensor("lq1", [64], F32, kind="ExternalInput")
    lk1 = nc.dram_tensor("lk1", [64], F32, kind="ExternalInput")
    lq2 = nc.dram_tensor("lq2", [64], F32, kind="ExternalInput")
    lk2 = nc.dram_tensor("lk2", [64], F32, kind="ExternalInput")
    ident_h = nc.dram_tensor("ident", [P, P], BF16, kind="ExternalInput")
    xout = nc.dram_tensor("xout", [R, D], F32, kind="ExternalOutput")
    consts = emit_consts(ctx)
    consts["ident"] = emit_ident(ctx, ident_h)
    emit_attn0(ctx, xin.ap(), xout.ap(), halo.ap(), F32, cosr.ap(), sinr.ap(),
               coskr.ap(), sinkr.ap(),
               Wq1, Wq2, Wk1, Wk2, Wv, Wout, bout, lq1, lk1, lq2, lk2, consts)
    ctx.close()
    return nc


def _rope_tables(cp):
    pos = (2048 * cp + np.arange(HALO, dtype=np.float64))[:, None]
    div = np.exp(np.arange(0, 64, 2, dtype=np.float64) * (-np.log(10000.0) / 64.0))
    cos = np.cos(pos * div).astype(np.float32)
    sin = np.sin(pos * div).astype(np.float32)
    cosq = (np.tile(cos, (1, 16)) * np.float32(0.125)).copy()
    sinq = (np.tile(sin, (1, 16)) * np.float32(0.125)).copy()
    cosk = np.tile(cos, (1, 4)).copy()
    sink = np.tile(sin, (1, 4)).copy()
    return cosq, sinq, cosk, sink


def run_attn0(x_shards, x_full_per_batch, ins):
    ident = _ident_np()
    maps = []
    for c in range(N_CORES):
        b, cp = c // 4, c % 4
        halo = np.ascontiguousarray(x_full_per_batch[b][2048 * cp:2048 * cp + HALO, :],
                                    dtype=np.float32)
        cosr, sinr, coskr, sinkr = _rope_tables(cp)
        maps.append({"xin": x_shards[c], "halo": halo, "cosr": cosr, "sinr": sinr,
                     "coskr": coskr, "sinkr": sinkr,
                     "Wq1": ins["L0_attn_Wq1"], "Wq2": ins["L0_attn_Wq2"],
                     "Wk1": ins["L0_attn_Wk1"], "Wk2": ins["L0_attn_Wk2"],
                     "Wv": ins["L0_attn_Wv"], "Wout": ins["L0_attn_Wout"],
                     "bout": ins["L0_attn_bout"], "lq1": ins["L0_attn_lq1"],
                     "lk1": ins["L0_attn_lk1"], "lq2": ins["L0_attn_lq2"],
                     "lk2": ins["L0_attn_lk2"], "ident": ident})
    res = _launch("attn0", build_attn0, maps)
    return [res.results[c]["xout"] for c in range(N_CORES)]


def kernel(**inputs):
    if not _UNFUSED:
        return kernel_fused(inputs)
    return _kernel_unfused(**inputs)


_UNFUSED = False


def _kernel_unfused(**inputs):
    x = np.ascontiguousarray(np.asarray(inputs["L0_attn_x"]), dtype=np.float32)
    shards = _x_shards(x)
    wg0 = np.ascontiguousarray(np.asarray(inputs["L0_ffn_w_gate"]), dtype=np.int32)
    wu0 = np.ascontiguousarray(np.asarray(inputs["L0_ffn_w_up"]), dtype=np.int32)
    wd0 = np.ascontiguousarray(np.asarray(inputs["L0_ffn_w_down"]), dtype=np.int32)

    s1 = run_ffn0(shards, wg0, wu0, wd0)
    x1b = [np.concatenate(s1[0:4], 0), np.concatenate(s1[4:8], 0)]
    ins_np = {k: np.ascontiguousarray(np.asarray(v)) for k, v in inputs.items()
              if k.startswith("L0_attn_")}
    s2 = run_attn0(s1, x1b, ins_np)
    s3 = run_ffn0(s2, wg0, wu0, wd0)
    s4 = run_ffn1(s3, np.asarray(inputs["L1_ffn_w_gate"]),
                  np.asarray(inputs["L1_ffn_w_up"]), np.asarray(inputs["L1_ffn_w_down"]))
    s5 = run_attn1(s4, np.ascontiguousarray(np.asarray(inputs["L1_attn_Wq"]), dtype=np.float32),
                   np.ascontiguousarray(np.asarray(inputs["L1_attn_bq"]), dtype=np.float32),
                   np.ascontiguousarray(np.asarray(inputs["L1_attn_Wk"]), dtype=np.float32),
                   np.ascontiguousarray(np.asarray(inputs["L1_attn_bk"]), dtype=np.float32),
                   np.ascontiguousarray(np.asarray(inputs["L1_attn_Wv"]), dtype=np.float32),
                   np.ascontiguousarray(np.asarray(inputs["L1_attn_bv"]), dtype=np.float32))
    s6 = run_ffn1(s5, np.asarray(inputs["L1_ffn_w_gate"]),
                  np.asarray(inputs["L1_ffn_w_up"]), np.asarray(inputs["L1_ffn_w_down"]))
    out = np.stack([np.concatenate(s6[0:4], 0), np.concatenate(s6[4:8], 0)], 0)
    return out


def emit_load_sbuf(ctx, dst_sb, src_ap, tag):
    ch = ctx.ch(f"lds{tag}")
    b = ch.base()
    with ctx.nc.Block(no_gpsimd_drain=True) as block:
        @block.sync
        def _(sync):
            _inc(sync.dma_start(out=dst_sb[:], in_=src_ap), ch, 16)
            sync.wait_ge(ch.h, b + 16)


def build_fused():
    ctx = Ctx()
    nc = ctx.nc
    H0, H1 = 3072, 4096
    ein = lambda n, s, d: nc.dram_tensor(n, s, d, kind="ExternalInput")
    xin = ein("xin", [R, D], F32)
    xhalo = ein("xhalo", [HALO, D], F32)
    cosr = ein("cosr", [HALO, 512], F32)
    sinr = ein("sinr", [HALO, 512], F32)
    coskr = ein("coskr", [HALO, 128], F32)
    sinkr = ein("sinkr", [HALO, 128], F32)
    ident_h = ein("ident", [P, P], BF16)
    wg0 = ein("wg0", [H0, D], I32)
    wu0 = ein("wu0", [H0, D], I32)
    wd0 = ein("wd0", [D, H0], I32)
    Wq1 = ein("Wq1", [D, D], F32)
    Wq2 = ein("Wq2", [D, D], F32)
    Wk1 = ein("Wk1", [D, 128], F32)
    Wk2 = ein("Wk2", [D, 128], F32)
    Wv0 = ein("Wv0", [D, 128], F32)
    Wout = ein("Wout", [D, D], F32)
    bout = ein("bout", [D], F32)
    lq1 = ein("lq1", [64], F32)
    lk1 = ein("lk1", [64], F32)
    lq2 = ein("lq2", [64], F32)
    lk2 = ein("lk2", [64], F32)
    wg1 = ein("wg1", [H1, D], F32)
    wu1 = ein("wu1", [H1, D], F32)
    wd1 = ein("wd1", [D, H1], F32)
    Wq = ein("Wq", [D, D], F32)
    bq = ein("bq", [D], F32)
    Wk = ein("Wk", [D, D], F32)
    bk = ein("bk", [D], F32)
    Wv = ein("Wv", [D, D], F32)
    bv = ein("bv", [D], F32)
    xout = nc.dram_tensor("xout", [R, D], F32, kind="ExternalOutput")

    resid = nc.dram_tensor("resid", [R, D], F32)
    halo_scr = nc.dram_tensor("haloscr", [HALO, D], F32)
    hscr0 = nc.dram_tensor("hscr0", [R + HALO, H0], BF16)
    hscr1 = nc.dram_tensor("hscr1", [R, H1], BF16)
    wgT_s = nc.dram_tensor("wgTs", [P, DC, H0], BF16)
    wuT_s = nc.dram_tensor("wuTs", [P, DC, H0], BF16)
    wdT_s = nc.dram_tensor("wdTs", [P, H0 // P, D], BF16)
    tG_s = nc.dram_tensor("tGs", [P, DC, H1], BF16)
    tU_s = nc.dram_tensor("tUs", [P, DC, H1], BF16)
    tD_s = nc.dram_tensor("tDs", [P, H1 // P, D], BF16)

    consts = emit_consts(ctx)
    consts["ident"] = emit_ident(ctx, ident_h)
    qd = {
        "m_g": ctx.sb("m_g", [P, 1], F32),
        "m_u": ctx.sb("m_u", [P, 1], F32),
        "m_d": ctx.sb("m_d", [P, 1], F32),
        "inv_sh_all": ctx.sb("inv_sh_all", [P, NT], F32),
    }

    def own_tiles(t_ap):
        return [t_ap[i * P:(i + 1) * P, :] for i in range(NT)]

    def halo_tiles(t_ap):
        return [t_ap[i * P:(i + 1) * P, :] for i in range(NHT)]

    with contextlib.ExitStack() as es1:
        wgT = es1.enter_context(nc.sbuf_tensor("wgT1", [P, DC, H0], BF16))
        wuT = es1.enter_context(nc.sbuf_tensor("wuT1", [P, DC, H0], BF16))
        wdT = es1.enter_context(nc.sbuf_tensor("wdT1", [P, H0 // P, D], BF16))
        emit_conv_int_weight(ctx, wg0, H0, D, wgT, consts["ident"], "cv", stash_ap=wgT_s.ap())
        emit_conv_int_weight(ctx, wu0, H0, D, wuT, consts["ident"], "cv", stash_ap=wuT_s.ap())
        emit_conv_int_weight(ctx, wd0, D, H0, wdT, consts["ident"], "cv", stash_ap=wdT_s.ap())
        src1 = own_tiles(xin.ap()) + halo_tiles(xhalo.ap())
        emit_ffn_passA(ctx, src1, hscr0.ap(), wgT, wuT, H0, consts, False, None, "f")
        xd1 = ([(xin.ap()[i * P:(i + 1) * P, :], resid.ap()[i * P:(i + 1) * P, :])
                for i in range(NT)] +
               [(xhalo.ap()[i * P:(i + 1) * P, :], halo_scr.ap()[i * P:(i + 1) * P, :])
                for i in range(NHT)])
        emit_ffn_passB(ctx, xd1, hscr0.ap(), wdT, H0, consts, False, None, "f")

    emit_attn0(ctx, resid.ap(), resid.ap(), halo_scr.ap(), F32, cosr.ap(), sinr.ap(),
               coskr.ap(), sinkr.ap(),
               Wq1, Wq2, Wk1, Wk2, Wv0, Wout, bout, lq1, lk1, lq2, lk2, consts, tag="a0")

    with contextlib.ExitStack() as es3:
        wgT = es3.enter_context(nc.sbuf_tensor("wgT3", [P, DC, H0], BF16))
        wuT = es3.enter_context(nc.sbuf_tensor("wuT3", [P, DC, H0], BF16))
        wdT = es3.enter_context(nc.sbuf_tensor("wdT3", [P, H0 // P, D], BF16))
        emit_load_sbuf(ctx, wgT, wgT_s.ap(), "w")
        emit_load_sbuf(ctx, wuT, wuT_s.ap(), "w")
        emit_load_sbuf(ctx, wdT, wdT_s.ap(), "w")
        emit_ffn_passA(ctx, own_tiles(resid.ap()), hscr0.ap(), wgT, wuT, H0, consts,
                       False, None, "f")
        xd3 = [(resid.ap()[i * P:(i + 1) * P, :], resid.ap()[i * P:(i + 1) * P, :])
               for i in range(NT)]
        emit_ffn_passB(ctx, xd3, hscr0.ap(), wdT, H0, consts, False, None, "f")

    emit_quant_l1_weight(ctx, wd1, D, H1, None, consts["ident"], consts, qd["m_d"], "qw",
                         stash_ap=tD_s.ap())
    with contextlib.ExitStack() as es4:
        ternGT = es4.enter_context(nc.sbuf_tensor("tG4", [P, DC, H1], BF16))
        ternUT = es4.enter_context(nc.sbuf_tensor("tU4", [P, DC, H1], BF16))
        emit_quant_l1_weight(ctx, wg1, H1, D, ternGT, consts["ident"], consts, qd["m_g"],
                             "qw", stash_ap=tG_s.ap())
        emit_quant_l1_weight(ctx, wu1, H1, D, ternUT, consts["ident"], consts, qd["m_u"],
                             "qw", stash_ap=tU_s.ap())
        emit_ffn_passA(ctx, own_tiles(resid.ap()), hscr1.ap(), ternGT, ternUT, H1, consts,
                       True, qd, "f")
    with contextlib.ExitStack() as es4b:
        ternDT = es4b.enter_context(nc.sbuf_tensor("tD4", [P, H1 // P, D], BF16))
        emit_load_sbuf(ctx, ternDT, tD_s.ap(), "w")
        xd4 = [(resid.ap()[i * P:(i + 1) * P, :], resid.ap()[i * P:(i + 1) * P, :])
               for i in range(NT)]
        emit_ffn_passB(ctx, xd4, hscr1.ap(), ternDT, H1, consts, True, qd, "f")

    emit_attn1(ctx, resid.ap(), resid.ap(), Wq, bq, Wk, bk, Wv, bv, consts, tag="a0")

    with contextlib.ExitStack() as es6:
        ternGT = es6.enter_context(nc.sbuf_tensor("tG6", [P, DC, H1], BF16))
        ternUT = es6.enter_context(nc.sbuf_tensor("tU6", [P, DC, H1], BF16))
        emit_load_sbuf(ctx, ternGT, tG_s.ap(), "w")
        emit_load_sbuf(ctx, ternUT, tU_s.ap(), "w")
        emit_ffn_passA(ctx, own_tiles(resid.ap()), hscr1.ap(), ternGT, ternUT, H1, consts,
                       True, qd, "f")
    with contextlib.ExitStack() as es6b:
        ternDT = es6b.enter_context(nc.sbuf_tensor("tD6", [P, H1 // P, D], BF16))
        emit_load_sbuf(ctx, ternDT, tD_s.ap(), "w")
        xd6 = [(resid.ap()[i * P:(i + 1) * P, :], xout.ap()[i * P:(i + 1) * P, :])
               for i in range(NT)]
        emit_ffn_passB(ctx, xd6, hscr1.ap(), ternDT, H1, consts, True, qd, "f")
    ctx.close()
    return nc


def kernel_fused(inputs):
    x = np.ascontiguousarray(np.asarray(inputs["L0_attn_x"]), dtype=np.float32)
    shards = _x_shards(x)
    ident = _ident_np()
    f32 = lambda k: np.ascontiguousarray(np.asarray(inputs[k]), dtype=np.float32)
    i32 = lambda k: np.ascontiguousarray(np.asarray(inputs[k]), dtype=np.int32)
    common = {
        "ident": ident,
        "wg0": i32("L0_ffn_w_gate"), "wu0": i32("L0_ffn_w_up"), "wd0": i32("L0_ffn_w_down"),
        "Wq1": f32("L0_attn_Wq1"), "Wq2": f32("L0_attn_Wq2"), "Wk1": f32("L0_attn_Wk1"),
        "Wk2": f32("L0_attn_Wk2"), "Wv0": f32("L0_attn_Wv"), "Wout": f32("L0_attn_Wout"),
        "bout": f32("L0_attn_bout"), "lq1": f32("L0_attn_lq1"), "lk1": f32("L0_attn_lk1"),
        "lq2": f32("L0_attn_lq2"), "lk2": f32("L0_attn_lk2"),
        "wg1": f32("L1_ffn_w_gate"), "wu1": f32("L1_ffn_w_up"), "wd1": f32("L1_ffn_w_down"),
        "Wq": f32("L1_attn_Wq"), "bq": f32("L1_attn_bq"), "Wk": f32("L1_attn_Wk"),
        "bk": f32("L1_attn_bk"), "Wv": f32("L1_attn_Wv"), "bv": f32("L1_attn_bv"),
    }
    maps = []
    for c in range(N_CORES):
        b, cp = c // 4, c % 4
        halo = np.ascontiguousarray(x[b, 2048 * cp:2048 * cp + HALO, :], dtype=np.float32)
        cosr, sinr, coskr, sinkr = _rope_tables(cp)
        m = dict(common)
        m.update({"xin": shards[c], "xhalo": halo, "cosr": cosr, "sinr": sinr,
                  "coskr": coskr, "sinkr": sinkr})
        maps.append(m)
    res = _launch("fused", build_fused, maps)
    outs = [res.results[c]["xout"] for c in range(N_CORES)]
    return np.stack([np.concatenate(outs[0:4], 0), np.concatenate(outs[4:8], 0)], 0)


# version 27 (final)
# speedup vs baseline: 1.0254x; 1.0113x over previous
import contextlib

import numpy as np
import ml_dtypes

import concourse.bass as bass
import concourse.mybir as mybir
from concourse.bass_utils import run_bass_kernel_spmd

F32 = mybir.dt.float32
BF16 = mybir.dt.bfloat16
I32 = mybir.dt.int32
AF = mybir.ActivationFunctionType
ALU = mybir.AluOpType
AX = mybir.AxisListType

P = 128
R = 4096
NT = R // P
D = 1024
DC = D // P
HALO = 2176
NHT = HALO // P

N_CORES = 8


class Ch:

    def __init__(self, handle):
        self.h = handle
        self.tot = 0

    def base(self):
        return self.tot


def _inc(instr, ch, k=1):
    instr.then_inc(ch.h, k)
    ch.tot += k
    return instr


class Ctx:
    def __init__(self):
        import functools
        self.nc = bass.Bass("TRN2", target_bir_lowering=False, debug=False)
        self.nc.sbuf_tensor = functools.partial(self.nc.sbuf_tensor,
                                                allow_name_mangling=True)
        self.nc.psum_tensor = functools.partial(self.nc.psum_tensor,
                                                allow_name_mangling=True)
        self.es = contextlib.ExitStack()
        self._chs = {}

    def sb(self, name, shape, dt):
        return self.es.enter_context(self.nc.sbuf_tensor(name, shape, dt))

    def ch(self, name):
        if name not in self._chs:
            h = self.es.enter_context(self.nc.semaphore(name))
            self._chs[name] = Ch(h)
        return self._chs[name]

    def chn(self, name, n):
        return [self.ch(f"{name}_{j}") for j in range(n)]

    def close(self):
        self.es.close()


def emit_tern_chain(vector, pairs, bufs, scale_ap, ch, final_ch):
    W = bufs["W"]
    y, p, n = bufs["y"], bufs["c1"], bufs["c2"]
    hv = [slice(0, W), slice(W, 2 * W)]
    tok = {}
    n_ops = len(pairs) * 2 * 4
    cnt = [0]

    def op(emit, out_key, deps):
        v = max([tok[d] for d in deps if d in tok], default=0)
        if v:
            vector.wait_ge(ch.h, v)
        ins = emit()
        cnt[0] += 1
        if cnt[0] == n_ops:
            _inc(ins, final_ch, 1)
            tok[out_key] = 0
        else:
            _inc(ins, ch, 1)
            tok[out_key] = ch.tot
        return ins

    for (srcA, dstA), (srcB, dstB) in pairs:
        src_ = [srcA, srcB]
        dst = [dstA, dstB]
        for s in range(2):
            op(lambda s=s: vector.tensor_scalar(y[:, hv[s]], src_[s], scale_ap, None,
                                                op0=ALU.mult),
               ("y", s), [("o", 0), ("o", 1)])
        for s in range(2):
            op(lambda s=s: vector.tensor_scalar(p[:, hv[s]], y[:, hv[s]], 0.5, None,
                                                op0=ALU.is_gt),
               ("p", s), [("y", s)])
        for s in range(2):
            op(lambda s=s: vector.tensor_scalar(n[:, hv[s]], y[:, hv[s]], -0.5, None,
                                                op0=ALU.is_lt),
               ("n", s), [("y", s)])
        for s in range(2):
            op(lambda s=s: vector.tensor_tensor(dst[s], p[:, hv[s]], n[:, hv[s]],
                                                op=ALU.subtract),
               ("o", s), [("p", s), ("n", s), ("o", 1 - s)])
    return


def emit_consts(ctx):
    nc = ctx.nc
    c = {}
    c["eps"] = ctx.sb("c_eps", [P, 1], F32)
    c["ones_col"] = ctx.sb("c_ones_col", [P, 1], F32)
    c["ones_row"] = ctx.sb("c_ones_row", [1, P], F32)
    with nc.Block(no_gpsimd_drain=True) as block:
        @block.vector
        def _(vector):
            vector.memset(c["eps"][:], 1e-5)
            vector.memset(c["ones_col"][:], 1.0)
            vector.memset(c["ones_row"][:], 1.0)
    return c


def emit_conv_int_weight(ctx, w_h, O, I, wT, ident, tag, src_dt=I32, stash_ap=None):
    nc = ctx.nc
    OT = O // P
    KC = I // P
    lw = ctx.chn(f"lw{tag}", 2)
    cw, twr, wev = ctx.ch(f"cw{tag}"), ctx.ch(f"twr{tag}"), ctx.ch(f"wev{tag}")
    b_lw = [c.base() for c in lw]
    b_cw, b_twr, b_wev = cw.base(), twr.base(), wev.base()
    with contextlib.ExitStack() as es2:
        wint = [es2.enter_context(nc.sbuf_tensor(f"wint{tag}{j}", [P, I], src_dt)) for j in range(2)]
        wbf = [es2.enter_context(nc.sbuf_tensor(f"wbf{tag}{j}", [P, I], BF16)) for j in range(2)]
        trsl = [es2.enter_context(nc.psum_tensor(f"pstr{tag}{j}", [P, P], BF16)) for j in range(4)]
        with nc.Block(no_gpsimd_drain=True) as block:
            @block.sync
            def _(sync):
                for j in range(OT):
                    if j >= 2:
                        sync.wait_ge(cw.h, b_cw + j - 1)
                    _inc(sync.dma_start(out=wint[j % 2][:], in_=w_h[j * P:(j + 1) * P, :]),
                         lw[j % 2], 16)

            @block.scalar
            def _(scalar):
                for j in range(OT):
                    scalar.wait_ge(lw[j % 2].h, b_lw[j % 2] + 16 * (j // 2 + 1))
                    if j >= 2:
                        scalar.wait_ge(twr.h, b_twr + (j - 1) * KC)
                    _inc(scalar.activation(wbf[j % 2][:], wint[j % 2][:], AF.Copy), cw, 1)

            @block.tensor
            def _(tensor):
                for j in range(OT):
                    tensor.wait_ge(cw.h, b_cw + j + 1)
                    for k in range(KC):
                        g = j * KC + k
                        if g >= 4:
                            tensor.wait_ge(wev.h, b_wev + g - 3)
                        ins = tensor.transpose(trsl[g % 4][:],
                                               wbf[j % 2][:, k * P:(k + 1) * P], ident[:])
                        _inc(ins, twr, 1)

            @block.vector
            def _(vector):
                for j in range(OT):
                    for k in range(KC):
                        g = j * KC + k
                        vector.wait_ge(twr.h, b_twr + g + 1)
                        ins = vector.tensor_copy(wT[:, k, j * P:(j + 1) * P], trsl[g % 4][:])
                        _inc(ins, wev, 1)

            @block.gpsimd
            def _(gpsimd):
                if stash_ap is not None:
                    sst = ctx.chn("qssqw", 2)[0]
                    gpsimd.wait_ge(wev.h, b_wev + OT * KC)
                    _inc(gpsimd.dma_start(out=stash_ap, in_=wT[:]), sst, 16)
                    gpsimd.wait_ge(sst.h, sst.tot)


def emit_quant_l1_weight(ctx, w_h, O, I, wT, ident, consts, m_b, tag, stash_ap=None):
    nc = ctx.nc
    OT = O // P
    KC = I // P
    QC = I // D
    lw = ctx.chn(f"qlw{tag}", 2)
    ar, cw, twr, wev = (ctx.ch(f"qar{tag}"), ctx.ch(f"qcw{tag}"),
                        ctx.ch(f"qtw{tag}"), ctx.ch(f"qwe{tag}"))
    sst2 = ctx.chn(f"qss{tag}", 2)
    b_lw = [c.base() for c in lw]
    b_ar, b_cw = ar.base(), cw.base()
    b_twr, b_wev = twr.base(), wev.base()
    b_sst2 = [c.base() for c in sst2]
    to_stash = wT is None
    with contextlib.ExitStack() as es2:
        sb = lambda n, s, d: es2.enter_context(nc.sbuf_tensor(n, s, d))
        wfl = [sb(f"qwf{tag}{j}", [P, I], F32) for j in range(2)]
        absr = [sb(f"qab{tag}{j}", [P, 1], F32) for j in range(2)]
        acc = sb(f"qac{tag}", [P, 1], F32)
        tot = sb(f"qto{tag}", [P, 1], F32)
        sw_b = sb(f"qsw{tag}", [P, 1], F32)
        qy = sb(f"qqy{tag}", [P, D], F32)
        qc1 = sb(f"qq1{tag}", [P, D], F32)
        qc2 = sb(f"qq2{tag}", [P, D], F32)
        wbf = [sb(f"qwb{tag}{j}", [P, I], BF16) for j in range(2)]
        if to_stash:
            stg = [sb(f"qsg{tag}{j}", [P, KC, P], BF16) for j in range(2)]
        trsl = [es2.enter_context(nc.psum_tensor(f"qpt{tag}{j}", [P, P], BF16)) for j in range(4)]
        psum1 = es2.enter_context(nc.psum_tensor(f"qp1{tag}", [P, 1], F32))
        psumb = es2.enter_context(nc.psum_tensor(f"qpb{tag}", [P, 1], F32))

        with nc.Block(no_gpsimd_drain=True) as block:
            @block.sync
            def _(sync):
                for j in range(OT):
                    if j >= 2:
                        sync.wait_ge(ar.h, b_ar + j - 1)
                    _inc(sync.dma_start(out=wfl[j % 2][:], in_=w_h[j * P:(j + 1) * P, :]),
                         lw[j % 2], 16)

            @block.vector
            def _(vector):
                vector.memset(acc[:], 0.0)
                for j in range(OT):
                    vector.wait_ge(lw[j % 2].h, b_lw[j % 2] + 16 * (j // 2 + 1))
                    ins = vector.tensor_reduce(absr[j % 2][:], wfl[j % 2][:], axis=AX.X,
                                               op=ALU.add, apply_absolute_value=True)
                    _inc(ins, ar, 1)
                    vector.wait_ge(ar.h, ar.tot)
                    vector.tensor_tensor(acc[:], acc[:], absr[j % 2][:], op=ALU.add)

        with nc.Block(no_gpsimd_drain=True) as block:
            @block.tensor
            def _(tensor):
                _inc(tensor.matmul(psum1[:1, :], acc[:], consts["ones_col"][:],
                                   start=True, stop=True), twr, 1)

            @block.vector
            def _(vector):
                vector.wait_ge(twr.h, b_twr + 1)
                _inc(vector.tensor_scalar(tot[:1, :], psum1[:1, :], 1.0 / float(O * I), None,
                                          op0=ALU.mult), ar, 1)
                vector.wait_ge(ar.h, ar.tot)
                _inc(vector.tensor_scalar(tot[:1, :], tot[:1, :], 1e-5, None, op0=ALU.max),
                     ar, 1)
                vector.wait_ge(ar.h, ar.tot)
                _inc(vector.reciprocal(sw_b[:1, :], tot[:1, :]), ar, 1)

        with nc.Block(no_gpsimd_drain=True) as block:
            @block.tensor
            def _(tensor):
                tensor.matmul(psum1[:], consts["ones_row"][:], tot[:1, :], start=True, stop=True)
                _inc(tensor.matmul(psumb[:], consts["ones_row"][:], sw_b[:1, :],
                                   start=True, stop=True), twr, 1)

            @block.vector
            def _(vector):
                vector.wait_ge(twr.h, b_twr + 2)
                vector.tensor_copy(m_b[:], psum1[:])
                vector.tensor_copy(sw_b[:], psumb[:])

        b_lw2 = [c.base() for c in lw]
        with nc.Block() as block:
            @block.sync
            def _(sync):
                for j in range(OT):
                    if j >= 2:
                        sync.wait_ge(cw.h, b_cw + j - 1)
                    _inc(sync.dma_start(out=wfl[j % 2][:], in_=w_h[j * P:(j + 1) * P, :]),
                         lw[j % 2], 16)

            @block.vector
            def _(vector):
                qbufs = {"W": 512, "y": qy, "c1": qc1, "c2": qc2}
                for j in range(OT):
                    vector.wait_ge(lw[j % 2].h, b_lw2[j % 2] + 16 * (j // 2 + 1))
                    if j >= 2:
                        vector.wait_ge(twr.h, b_twr + 2 + (j - 1) * KC)
                    if j >= 1:
                        vector.wait_ge(cw.h, b_cw + j)
                    pairs = []
                    for cc in range(QC):
                        s0 = slice(cc * D, cc * D + 512)
                        s1 = slice(cc * D + 512, (cc + 1) * D)
                        pairs.append(((wfl[j % 2][:, s0], wbf[j % 2][:, s0]),
                                      (wfl[j % 2][:, s1], wbf[j % 2][:, s1])))
                    emit_tern_chain(vector, pairs, qbufs, sw_b[:], ar, cw)

            @block.tensor
            def _(tensor):
                for j in range(OT):
                    tensor.wait_ge(cw.h, b_cw + j + 1)
                    for k in range(KC):
                        g = j * KC + k
                        if g >= 4:
                            tensor.wait_ge(wev.h, b_wev + g - 3)
                        ins = tensor.transpose(trsl[g % 4][:],
                                               wbf[j % 2][:, k * P:(k + 1) * P], ident[:])
                        _inc(ins, twr, 1)

            @block.scalar
            def _(scalar):
                for j in range(OT):
                    if to_stash and j >= 2:
                        scalar.wait_ge(sst2[j % 2].h, b_sst2[j % 2] + 16 * (j // 2))
                    for k in range(KC):
                        g = j * KC + k
                        scalar.wait_ge(twr.h, b_twr + 2 + g + 1)
                        dst = stg[j % 2][:, k, :] if to_stash else wT[:, k, j * P:(j + 1) * P]
                        ins = scalar.activation(dst, trsl[g % 4][:], AF.Copy)
                        _inc(ins, wev, 1)

            @block.gpsimd
            def _(gpsimd):
                if to_stash:
                    for j in range(OT):
                        gpsimd.wait_ge(wev.h, b_wev + (j + 1) * KC)
                        _inc(gpsimd.dma_start(out=stash_ap[:, :, j * P:(j + 1) * P],
                                              in_=stg[j % 2][:]), sst2[j % 2], 16)
                    for q in range(2):
                        gpsimd.wait_ge(sst2[q].h, sst2[q].tot)
                elif stash_ap is not None:
                    gpsimd.wait_ge(wev.h, b_wev + OT * KC)
                    _inc(gpsimd.dma_start(out=stash_ap, in_=wT[:]), sst2[0], 16)
                    gpsimd.wait_ge(sst2[0].h, sst2[0].tot)


def emit_ffn_passA(ctx, src_tiles, h_scr, wgT, wuT, H, consts, quant, qd, tag):
    ntot = len(src_tiles)
    nc = ctx.nc
    NB = H // 512
    NHC = H // 512
    NCC = 2 + NHC
    ident = consts["ident"]
    ld3 = ctx.chn(f"ldA{tag}", 3)
    xq = ctx.ch(f"xqA{tag}")
    sq = ctx.ch(f"sqA{tag}")
    agg = ctx.ch(f"aggA{tag}")
    trr = ctx.ch(f"trrA{tag}")
    tev = ctx.ch(f"tevA{tag}")
    mg = ctx.ch(f"mgA{tag}")
    mu = ctx.ch(f"muA{tag}")
    sil = ctx.ch(f"silA{tag}")
    hh = ctx.ch(f"hhA{tag}")
    qv = ctx.ch(f"hqA{tag}")
    qa = ctx.ch(f"qaA{tag}")
    qp = ctx.ch(f"qpA{tag}")
    sth2 = ctx.chn(f"sthA{tag}", 2)
    vr = ctx.ch(f"vrA{tag}")
    cs = ctx.ch(f"csA{tag}")
    b_ld3 = [c.base() for c in ld3]
    b_xq, b_sq, b_agg = xq.base(), sq.base(), agg.base()
    b_trr, b_tev, b_mg, b_mu = trr.base(), tev.base(), mg.base(), mu.base()
    b_sil, b_hh = sil.base(), hh.base()
    b_qv, b_qa, b_qp = qv.base(), qa.base(), qp.base()
    b_sth2 = [c.base() for c in sth2]
    b_cs = cs.base()
    HV = [slice(0, 512), slice(512, 1024), slice(1024, 1536)]
    NSL = 3
    qidx = {}
    marks = {}
    cctr = [0]

    with contextlib.ExitStack() as es2:
        sb = lambda n, s, d: es2.enter_context(nc.sbuf_tensor(n, s, d))
        ps = lambda n, s, d: es2.enter_context(nc.psum_tensor(n, s, d))
        xbuf = [sb(f"xA{tag}{j}", [P, D], F32) for j in range(3)]
        stats = [sb(f"stA{tag}{j}", [P, 12], F32) for j in range(2)]
        mv = [sb(f"mvA{tag}{j}", [P, 2], F32) for j in range(2)]
        sqv = [sb(f"sqvA{tag}{j}", [P, 1], F32) for j in range(2)]
        rstd = [sb(f"rsA{tag}{j}", [P, 1], F32) for j in range(2)]
        xop = [sb(f"xoA{tag}{j}", [P, D], BF16) for j in range(2)]
        xT = [sb(f"xTA{tag}{j}", [P, DC, P], BF16) for j in range(2)]
        sg = [sb(f"sgA{tag}{j}", [P, 512], F32) for j in range(2)]
        hsb = [sb(f"hsA{tag}{j}", [P, H], BF16) for j in range(2)]
        if quant:
            xn = [sb(f"xnA{tag}{j}", [P, D], F32) for j in range(2)]
            am = [sb(f"amA{tag}{j}", [P, 1], F32) for j in range(2)]
            sa = [sb(f"saA{tag}{j}", [P, 1], F32) for j in range(2)]
            csg = [sb(f"csgA{tag}{j}", [P, 1], F32) for j in range(2)]
            csu_all = sb(f"csuA{tag}", [P, ntot], F32)
            ham = [sb(f"hamA{tag}{j}", [P, 1], F32) for j in range(2)]
            hsa = [sb(f"hsaA{tag}{j}", [P, 1], F32) for j in range(2)]
            qy = sb(f"qyA{tag}", [P, 3 * 512], F32)
            qyi = sb(f"qyiA{tag}", [P, 3 * 512], I32)
            qc1 = sb(f"qc1A{tag}", [P, 3 * 512], F32)
            qc2 = sb(f"qc2A{tag}", [P, 3 * 512], F32)
        trsl = [ps(f"ptA{tag}{j}", [P, P], BF16) for j in range(4)]
        psg = [ps(f"pgA{tag}{j}", [P, 512], F32) for j in range(2)]
        psu = [ps(f"puA{tag}{j}", [P, 512], F32) for j in range(2)]

        def chain_head(vector, src_half, scale_ap, vrk):
            c = cctr[0]
            cctr[0] += 1
            qidx[(c, 'src')] = src_half
            qidx[(c, 'scl')] = scale_ap
            qidx[(c, 'vrk')] = vrk
            return c

        def chain_r(vector, c):
            h = HV[c % NSL]
            vector.wait_ge(qa.h, b_qa + c + 1)
            _inc(vector.scalar_tensor_tensor(qy[:, h], qidx[(c, 'src')],
                                             qidx[(c, 'scl')], qyi[:, h],
                                             op0=ALU.mult, op1=ALU.subtract), qv, 1)
            qidx[(c, 'r')] = qv.tot

        def chain_ccc(vector, c, dst_half):
            h = HV[c % NSL]
            if c >= NSL:
                vector.wait_ge(qp.h, b_qp + (c - NSL + 1))
            vector.wait_ge(qv.h, qidx[(c, 'r')])
            _inc(vector.tensor_scalar(qc1[:, h], qy[:, h], 0.5, None, op0=ALU.is_gt),
                 qv, 1)
            _inc(vector.tensor_scalar(qc2[:, h], qy[:, h], -0.5, None, op0=ALU.is_lt),
                 qv, 1)
            qidx[(c, 'cc')] = qv.tot
            vector.wait_ge(qv.h, qidx[(c, 'cc')])
            _inc(vector.tensor_tensor(qc1[:, h], qyi[:, h], qc1[:, h], op=ALU.add), qv, 1)
            qidx[(c, 't2a')] = qv.tot
            qidx[(c, 'dst')] = dst_half

        def emit_ln(vector, i):
            pr = i % 2
            vector.wait_ge(ld3[i % 3].h, b_ld3[i % 3] + 16 * (i // 3 + 1))
            if i >= 2:
                vector.wait_ge(sq.h, b_sq + i - 1)
                vector.wait_ge(trr.h, b_trr + DC * (i - 1))
                if quant:
                    vector.wait_ge(sil.h, b_sil + (i - 1) * NB)
            vector.bn_stats(stats[pr][:, 0:6], xbuf[i % 3][:, 0:512])
            _inc(vector.bn_stats(stats[pr][:, 6:12], xbuf[i % 3][:, 512:1024]), vr, 1)
            vector.wait_ge(vr.h, vr.tot)
            _inc(vector.bn_aggr(mv[pr][:], stats[pr][:]), agg, 1)
            vector.wait_ge(agg.h, agg.tot)
            vector.wait_ge(sq.h, b_sq + i + 1)
            _inc(vector.reciprocal(rstd[pr][:], sqv[pr][:]), vr, 1)
            vector.wait_ge(vr.h, vr.tot)
            if not quant:
                ins = vector.tensor_scalar(xop[pr][:], xbuf[i % 3][:], mv[pr][:, 0:1],
                                           rstd[pr][:], op0=ALU.subtract, op1=ALU.mult)
                _inc(ins, xq, 1)
            else:
                if i >= 2:
                    vector.wait_ge(qa.h, b_qa + marks[i - 2]['xc'][1] + 1)
                _inc(vector.tensor_scalar(xn[pr][:], xbuf[i % 3][:], mv[pr][:, 0:1],
                                          rstd[pr][:], op0=ALU.subtract, op1=ALU.mult), vr, 1)
                vector.wait_ge(vr.h, vr.tot)
                _inc(vector.tensor_reduce(am[pr][:], xn[pr][:], axis=AX.X, op=ALU.max,
                                          apply_absolute_value=True), vr, 1)
                vector.wait_ge(vr.h, vr.tot)
                _inc(vector.tensor_scalar(am[pr][:], am[pr][:], 1e-5, None, op0=ALU.max),
                     vr, 1)
                vector.wait_ge(vr.h, vr.tot)
                _inc(vector.reciprocal(sa[pr][:], am[pr][:]), vr, 1)
                vector.wait_ge(vr.h, vr.tot)
                _inc(vector.tensor_scalar(sa[pr][:], sa[pr][:], 127.0, None, op0=ALU.mult),
                     vr, 1)
                _inc(vector.tensor_scalar(csg[pr][:], am[pr][:], 1.0 / 127.0, qd["m_g"][:],
                                          op0=ALU.mult, op1=ALU.mult), cs, 1)
                _inc(vector.tensor_scalar(csu_all[:, i:i + 1], am[pr][:], 1.0 / 127.0,
                                          qd["m_u"][:], op0=ALU.mult, op1=ALU.mult), cs, 1)
                vector.wait_ge(vr.h, vr.tot)
                vrk = vr.tot
                ca = chain_head(vector, xn[pr][:, HV[0]], sa[pr][:], vrk)
                cb = chain_head(vector, xn[pr][:, HV[1]], sa[pr][:], vrk)
                chain_r(vector, ca)
                chain_r(vector, cb)
                chain_ccc(vector, ca, xop[pr][:, HV[0]])
                chain_ccc(vector, cb, xop[pr][:, HV[1]])
                marks[i] = {'xc': [ca, cb], 'hc': []}

        def emit_hquant_prologue(vector, i):
            pr = i % 2
            vector.wait_ge(hh.h, b_hh + (i + 1) * NB)
            if i >= 2:
                vector.wait_ge(qa.h, b_qa + marks[i - 2]['hc'][-1] + 1)
            _inc(vector.tensor_reduce(ham[pr][:], hsb[pr][:], axis=AX.X, op=ALU.max,
                                      apply_absolute_value=True), vr, 1)
            vector.wait_ge(vr.h, vr.tot)
            _inc(vector.tensor_scalar(ham[pr][:], ham[pr][:], 1e-5, None, op0=ALU.max), vr, 1)
            vector.wait_ge(vr.h, vr.tot)
            _inc(vector.reciprocal(hsa[pr][:], ham[pr][:]), vr, 1)
            vector.wait_ge(vr.h, vr.tot)
            _inc(vector.tensor_scalar(hsa[pr][:], hsa[pr][:], 127.0, None, op0=ALU.mult), vr, 1)
            _inc(vector.tensor_scalar(qd["inv_sh_all"][:, i:i + 1], ham[pr][:], 1.0 / 127.0,
                                      csu_all[:, i:i + 1], op0=ALU.mult, op1=ALU.mult), vr, 1)
            vector.wait_ge(vr.h, vr.tot)
            marks[i]['hvrk'] = vr.tot
            marks[i]['hc'] = []

        def emit_hquant_step(vector, i, hc):
            pr = i % 2
            cs_list = marks[i]['hc']
            half = hsb[pr][:, hc * 512:(hc + 1) * 512]
            cs_list.append(chain_head(vector, half, hsa[pr][:], marks[i]['hvrk']))
            if hc >= 1:
                chain_r(vector, cs_list[hc - 1])
            if hc >= 2:
                j = hc - 2
                chain_ccc(vector, cs_list[j], hsb[pr][:, j * 512:(j + 1) * 512])
            if hc == NHC - 1:
                chain_r(vector, cs_list[NHC - 1])
                for j in (NHC - 2, NHC - 1):
                    chain_ccc(vector, cs_list[j], hsb[pr][:, j * 512:(j + 1) * 512])

        with nc.Block() as block:
            @block.vector
            def _(vector):
                emit_ln(vector, 0)
                for i in range(ntot):
                    pr = i % 2
                    if quant and i >= 1:
                        emit_hquant_prologue(vector, i - 1)
                    if i + 1 < ntot:
                        emit_ln(vector, i + 1)
                    if i >= 2:
                        vector.wait_ge(sth2[i % 2].h, b_sth2[i % 2] + 16 * (i // 2))
                    for nb in range(NB):
                        G = i * NB + nb
                        vector.wait_ge(sil.h, b_sil + G + 1)
                        vector.wait_ge(mu.h, b_mu + G + 1)
                        sl = slice(nb * 512, (nb + 1) * 512)
                        ins = vector.tensor_tensor(hsb[pr][:, sl], sg[nb % 2][:],
                                                   psu[nb % 2][:], op=ALU.mult)
                        _inc(ins, hh, 1)
                        if quant and i >= 1 and nb < NHC:
                            emit_hquant_step(vector, i - 1, nb)
                if quant:
                    emit_hquant_prologue(vector, ntot - 1)
                    for hc in range(NHC):
                        emit_hquant_step(vector, ntot - 1, hc)

            @block.sync
            def _(sync):
                for i in range(ntot):
                    if i >= 3:
                        if quant:
                            sync.wait_ge(qp.h, b_qp + marks[i - 3]['xc'][1] + 1)
                        else:
                            sync.wait_ge(xq.h, b_xq + i - 2)
                    _inc(sync.dma_start(out=xbuf[i % 3][:], in_=src_tiles[i]), ld3[i % 3], 16)

            @block.scalar
            def _(scalar):
                def emit_sqrt(i):
                    pr = i % 2
                    scalar.wait_ge(agg.h, b_agg + i + 1)
                    if i >= 2:
                        if quant:
                            scalar.wait_ge(qp.h, b_qp + marks[i - 2]['xc'][1] + 1)
                        else:
                            scalar.wait_ge(xq.h, b_xq + i - 1)
                    _inc(scalar.activation(sqv[pr][:], mv[pr][:, 1:2], AF.Sqrt,
                                           bias=consts["eps"][:]), sq, 1)

                def act_chunk(c):
                    h = HV[c % NSL]
                    scalar.wait_ge(vr.h, qidx[(c, 'vrk')])
                    if c >= NSL:
                        scalar.wait_ge(qv.h, qidx[(c - NSL, 't2a')])
                    _inc(scalar.activation(qyi[:, h], qidx[(c, 'src')], AF.Copy,
                                           scale=qidx[(c, 'scl')]), qa, 1)

                emit_sqrt(0)
                if quant:
                    for c in marks[0]['xc']:
                        act_chunk(c)
                for i in range(ntot):
                    pr = i % 2
                    if i >= 2:
                        scalar.wait_ge(mg.h, b_mg + (i - 1) * NB)
                        scalar.wait_ge(mu.h, b_mu + (i - 1) * NB)
                    for k in range(DC):
                        g = i * DC + k
                        scalar.wait_ge(trr.h, b_trr + g + 1)
                        _inc(scalar.activation(xT[pr][:, k, :], trsl[g % 4][:], AF.Copy),
                             tev, 1)
                    if i + 1 < ntot:
                        emit_sqrt(i + 1)
                        if quant:
                            for c in marks[i + 1]['xc']:
                                act_chunk(c)
                    if quant:
                        scalar.wait_ge(cs.h, b_cs + 2 * i + 1)
                    for nb in range(NB):
                        G = i * NB + nb
                        scalar.wait_ge(mg.h, b_mg + G + 1)
                        if quant:
                            ins = scalar.activation(sg[nb % 2][:], psg[nb % 2][:], AF.Silu,
                                                    scale=csg[pr][:])
                        else:
                            ins = scalar.activation(sg[nb % 2][:], psg[nb % 2][:], AF.Silu,
                                                    scale=1.0 / 127.0)
                        _inc(ins, sil, 1)
                        if quant and i >= 1 and nb < NHC:
                            act_chunk(marks[i - 1]['hc'][nb])
                if quant:
                    for c in marks[ntot - 1]['hc']:
                        act_chunk(c)

            @block.tensor
            def _(tensor):
                for i in range(ntot):
                    pr = i % 2
                    if quant:
                        tensor.wait_ge(qp.h, b_qp + marks[i]['xc'][1] + 1)
                    else:
                        tensor.wait_ge(xq.h, b_xq + i + 1)
                    for k in range(DC):
                        g = i * DC + k
                        if g >= 4:
                            tensor.wait_ge(tev.h, b_tev + g - 3)
                        ins = tensor.transpose(trsl[g % 4][:],
                                               xop[pr][:, k * P:(k + 1) * P], ident[:])
                        _inc(ins, trr, 1)
                    tensor.wait_ge(tev.h, b_tev + DC * (i + 1))
                    for nb in range(NB):
                        G = i * NB + nb
                        if G >= 2:
                            tensor.wait_ge(hh.h, b_hh + G - 1)
                        for k in range(DC):
                            ins = tensor.matmul(psg[nb % 2][:], xT[pr][:, k, :],
                                                wgT[:, k, nb * 512:(nb + 1) * 512],
                                                start=(k == 0), stop=(k == DC - 1))
                            if k == DC - 1:
                                _inc(ins, mg, 1)
                        for k in range(DC):
                            ins = tensor.matmul(psu[nb % 2][:], xT[pr][:, k, :],
                                                wuT[:, k, nb * 512:(nb + 1) * 512],
                                                start=(k == 0), stop=(k == DC - 1))
                            if k == DC - 1:
                                _inc(ins, mu, 1)

            @block.gpsimd
            def _(gpsimd):
                def pool_chunk(c):
                    h = HV[c % NSL]
                    gpsimd.wait_ge(qv.h, qidx[(c, 't2a')])
                    _inc(gpsimd.tensor_tensor(qidx[(c, 'dst')], qc1[:, h], qc2[:, h],
                                              op=ALU.subtract), qp, 1)

                if quant:
                    for c in marks[0]['xc']:
                        pool_chunk(c)

                def store_tile(i):
                    pr = i % 2
                    if quant:
                        gpsimd.wait_ge(qp.h, b_qp + marks[i]['hc'][-1] + 1)
                    gpsimd.wait_ge(hh.h, b_hh + (i + 1) * NB)
                    _inc(gpsimd.dma_start(out=h_scr[i * P:(i + 1) * P, :], in_=hsb[pr][:]),
                         sth2[i % 2], 16)

                for i in range(ntot):
                    if quant and i + 1 < ntot:
                        for c in marks[i + 1]['xc']:
                            pool_chunk(c)
                    if quant and i >= 1:
                        for c in marks[i - 1]['hc']:
                            pool_chunk(c)
                        store_tile(i - 1)
                    if not quant:
                        store_tile(i)
                if quant:
                    for c in marks[ntot - 1]['hc']:
                        pool_chunk(c)
                    store_tile(ntot - 1)
                for q in range(2):
                    gpsimd.wait_ge(sth2[q].h, sth2[q].tot)


def emit_ffn_passB(ctx, xd_tiles, h_scr, wdT, H, consts, quant, qd, tag,
                  bf16_copy_ap=None):
    ntot = len(xd_tiles)
    nc = ctx.nc
    HC = H // P
    NBAT = (HC + 7) // 8
    ident = consts["ident"]
    ldh2 = ctx.chn(f"ldhB{tag}", 2)
    ldx2 = ctx.chn(f"ldxB{tag}", 2)
    trr = ctx.ch(f"trrB{tag}")
    tev = ctx.ch(f"tevB{tag}")
    md = ctx.ch(f"mdB{tag}")
    epi = ctx.ch(f"epiB{tag}")
    cbf = ctx.ch(f"cbfB{tag}")
    st2 = ctx.chn(f"stB{tag}", 2)
    stb2 = ctx.chn(f"stbB{tag}", 2)
    vrB = ctx.ch(f"vrB{tag}")
    b_vrB = vrB.base()
    b_ldh2 = [c.base() for c in ldh2]
    b_ldx2 = [c.base() for c in ldx2]
    b_trr, b_tev, b_md = trr.base(), tev.base(), md.base()
    b_epi, b_cbf = epi.base(), cbf.base()
    b_st2 = [c.base() for c in st2]
    b_stb2 = [c.base() for c in stb2]

    with contextlib.ExitStack() as es2:
        sb = lambda n, s, d: es2.enter_context(nc.sbuf_tensor(n, s, d))
        ps = lambda n, s, d: es2.enter_context(nc.psum_tensor(n, s, d))
        hbuf = [sb(f"hB{tag}{j}", [P, H], BF16) for j in range(2)]
        xbuf = [sb(f"xB{tag}{j}", [P, D], F32) for j in range(2)]
        hT = [sb(f"hTB{tag}{j}", [P, HC, P], BF16) for j in range(2)]
        obuf = [sb(f"oB{tag}{j}", [P, D], F32) for j in range(2)]
        obf = [sb(f"obB{tag}{j}", [P, D], BF16) for j in range(2)] if bf16_copy_ap is not None else None
        csd = [sb(f"csdB{tag}{j}", [P, 1], F32) for j in range(2)] if quant else None
        trsl = [ps(f"ptB{tag}{j}", [P, P], BF16) for j in range(4)]
        psd = [ps(f"pdB{tag}{j}", [P, 512], F32) for j in range(4)]

        with nc.Block() as block:
            @block.sync
            def _(sync):
                for i in range(ntot):
                    if i >= 2:
                        sync.wait_ge(trr.h, b_trr + (i - 1) * HC)
                        sync.wait_ge(epi.h, b_epi + 2 * (i - 1))
                    _inc(sync.dma_start(out=hbuf[i % 2][:], in_=h_scr[i * P:(i + 1) * P, :]),
                         ldh2[i % 2], 16)
                    _inc(sync.dma_start(out=xbuf[i % 2][:], in_=xd_tiles[i][0]),
                         ldx2[i % 2], 16)

            @block.tensor
            def _(tensor):
                for i in range(ntot):
                    pr = i % 2
                    tensor.wait_ge(ldh2[pr].h, b_ldh2[pr] + 16 * (i // 2 + 1))
                    for k in range(HC):
                        g = i * HC + k
                        if g >= 4:
                            tensor.wait_ge(tev.h, b_tev + g - 3)
                        ins = tensor.transpose(trsl[g % 4][:],
                                               hbuf[pr][:, k * P:(k + 1) * P], ident[:])
                        _inc(ins, trr, 1)
                    tensor.wait_ge(tev.h, b_tev + (i + 1) * HC)
                    for nb in range(2):
                        if i >= 2:
                            tensor.wait_ge(epi.h, b_epi + 2 * (i - 1))
                        for k in range(HC):
                            ins = tensor.matmul(psd[2 * pr + nb][:], hT[pr][:, k, :],
                                                wdT[:, k, nb * 512:(nb + 1) * 512],
                                                start=(k == 0), stop=(k == HC - 1))
                            if k == HC - 1:
                                _inc(ins, md, 1)

            @block.vector
            def _(vector):
                for i in range(ntot):
                    pr = i % 2
                    if quant:
                        if i >= 2:
                            vector.wait_ge(epi.h, b_epi + 2 * (i - 1))
                        _inc(vector.tensor_scalar(csd[pr][:], qd["inv_sh_all"][:, i:i + 1], 0.5,
                                                  qd["m_d"][:], op0=ALU.mult, op1=ALU.mult),
                             vrB, 1)
                    if i >= 2:
                        vector.wait_ge(md.h, b_md + 2 * (i - 1))
                    for k in range(HC):
                        g = i * HC + k
                        vector.wait_ge(trr.h, b_trr + g + 1)
                        ins = vector.tensor_copy(hT[pr][:, k, :], trsl[g % 4][:])
                        _inc(ins, tev, 1)
                    if i >= 2:
                        vector.wait_ge(st2[pr].h, b_st2[pr] + 16 * (i // 2))
                    vector.wait_ge(ldx2[pr].h, b_ldx2[pr] + 16 * (i // 2 + 1))
                    if quant:
                        vector.wait_ge(vrB.h, b_vrB + i + 1)
                    for nb in range(2):
                        vector.wait_ge(md.h, b_md + 2 * i + nb + 1)
                        sl = slice(nb * 512, (nb + 1) * 512)
                        if quant:
                            ins = vector.scalar_tensor_tensor(obuf[pr][:, sl],
                                                              psd[2 * pr + nb][:], csd[pr][:],
                                                              xbuf[pr][:, sl],
                                                              op0=ALU.mult, op1=ALU.add)
                        else:
                            ins = vector.scalar_tensor_tensor(obuf[pr][:, sl],
                                                              psd[2 * pr + nb][:],
                                                              0.5 / (127.0 * 127.0),
                                                              xbuf[pr][:, sl],
                                                              op0=ALU.mult, op1=ALU.add)
                        _inc(ins, epi, 1)

            @block.scalar
            def _(scalar):
                if bf16_copy_ap is not None:
                    for i in range(ntot):
                        pr = i % 2
                        scalar.wait_ge(epi.h, b_epi + 2 * (i + 1))
                        if i >= 2:
                            scalar.wait_ge(stb2[pr].h, b_stb2[pr] + 16 * (i // 2))
                        _inc(scalar.activation(obf[pr][:], obuf[pr][:], AF.Copy), cbf, 1)

            @block.gpsimd
            def _(gpsimd):
                for i in range(ntot):
                    pr = i % 2
                    gpsimd.wait_ge(epi.h, b_epi + 2 * (i + 1))
                    _inc(gpsimd.dma_start(out=xd_tiles[i][1], in_=obuf[pr][:]),
                         st2[pr], 16)
                    if bf16_copy_ap is not None:
                        gpsimd.wait_ge(cbf.h, b_cbf + i + 1)
                        _inc(gpsimd.dma_start(out=bf16_copy_ap[i * P:(i + 1) * P, :],
                                              in_=obf[pr][:]), stb2[pr], 16)
                for q in range(2):
                    gpsimd.wait_ge(st2[q].h, st2[q].tot)
                    if bf16_copy_ap is not None:
                        gpsimd.wait_ge(stb2[q].h, stb2[q].tot)




def emit_attn0(ctx, src_ap, dst_ap, halo_ap, halo_dt, cos_h, sin_h, cosk_h, sink_h,
               Wq1_h, Wq2_h, Wk1_h, Wk2_h, Wv_h, Wout_h, bout_h,
               lq1_h, lk1_h, lq2_h, lk2_h, consts, tag="a0"):
    nc = ctx.nc
    ident = consts["ident"]
    NW = 16
    U = NW * 2 * 16
    es0 = contextlib.ExitStack()
    sb = lambda n, s, d: es0.enter_context(nc.sbuf_tensor(n, s, d))

    q1T = sb(f"q1T{tag}", [P, DC, HALO], BF16)
    q2T = sb(f"q2T{tag}", [P, DC, HALO], BF16)
    k1T = sb(f"k1T{tag}", [P, HALO], BF16)
    k2T = sb(f"k2T{tag}", [P, HALO], BF16)
    v_sb = sb(f"v{tag}", [P, NHT, P], BF16)
    neglam = sb(f"nlam{tag}", [P, 1], F32)

    with contextlib.ExitStack() as esw:
        wsb = lambda n, s, d: esw.enter_context(nc.sbuf_tensor(n, s, d))
        wps = lambda n, s, d: esw.enter_context(nc.psum_tensor(n, s, d))
        Wq1 = wsb(f"Wq1{tag}", [P, DC, D], BF16)
        Wq2 = wsb(f"Wq2{tag}", [P, DC, D], BF16)
        Wkv = wsb(f"Wkv{tag}", [P, DC, 384], BF16)
        wf = [wsb(f"wf{tag}{j}", [P, D], F32) for j in range(2)]
        lvec = wsb(f"lv{tag}", [1, 256], F32)
        lsc = wsb(f"lsc{tag}", [1, 8], F32)
        lw2 = ctx.chn(f"lw{tag}", 2)
        cwc = ctx.ch(f"cw{tag}")
        lml = ctx.ch(f"lml{tag}")
        lmv = ctx.ch(f"lmv{tag}")
        lme = ctx.ch(f"lme{tag}")
        b_lw2 = [c.base() for c in lw2]
        b_cw, b_lml = cwc.base(), lml.base()
        b_lmv, b_lme = lmv.base(), lme.base()
        with nc.Block(no_gpsimd_drain=True) as block:
            @block.sync
            def _(sync):
                j = 0
                for src_h in (Wq1_h, Wq2_h):
                    for k in range(DC):
                        if j >= 2:
                            sync.wait_ge(cwc.h, b_cw + j - 1)
                        _inc(sync.dma_start(out=wf[j % 2][:],
                                            in_=src_h[k * P:(k + 1) * P, :]), lw2[j % 2], 16)
                        j += 1
                for k in range(DC):
                    if j >= 2:
                        sync.wait_ge(cwc.h, b_cw + j - 1)
                    _inc(sync.dma_start(out=wf[j % 2][:, 0:128],
                                        in_=Wk1_h[k * P:(k + 1) * P, :]), lw2[j % 2], 16)
                    _inc(sync.dma_start(out=wf[j % 2][:, 128:256],
                                        in_=Wk2_h[k * P:(k + 1) * P, :]), lw2[j % 2], 16)
                    _inc(sync.dma_start(out=wf[j % 2][:, 256:384],
                                        in_=Wv_h[k * P:(k + 1) * P, :]), lw2[j % 2], 16)
                    j += 1
                for li, lh in enumerate((lq1_h, lk1_h, lq2_h, lk2_h)):
                    _inc(sync.dma_start(out=lvec[:, li * 64:(li + 1) * 64],
                                        in_=lh.ap().rearrange("(one d) -> one d", one=1)),
                         lml, 16)

            @block.scalar
            def _(scalar):
                j = 0
                nld = [0, 0]
                for dst_T in (Wq1, Wq2):
                    for k in range(DC):
                        nld[j % 2] += 16
                        scalar.wait_ge(lw2[j % 2].h, b_lw2[j % 2] + nld[j % 2])
                        if j >= 4:
                            pass
                        _inc(scalar.activation(dst_T[:, k, :], wf[j % 2][:], AF.Copy), cwc, 1)
                        j += 1
                for k in range(DC):
                    nld[j % 2] += 48
                    scalar.wait_ge(lw2[j % 2].h, b_lw2[j % 2] + nld[j % 2])
                    _inc(scalar.activation(Wkv[:, k, :], wf[j % 2][:, 0:384], AF.Copy), cwc, 1)
                    j += 1
                scalar.wait_ge(lmv.h, b_lmv + 4)
                _inc(scalar.activation(lsc[:, 2:3], lsc[:, 0:1], AF.Exp), lme, 1)
                _inc(scalar.activation(lsc[:, 3:4], lsc[:, 1:2], AF.Exp), lme, 1)

            @block.vector
            def _(vector):
                vector.wait_ge(lml.h, b_lml + 64)
                _inc(vector.tensor_tensor(lvec[:, 0:64], lvec[:, 0:64], lvec[:, 64:128],
                                          op=ALU.mult), lmv, 1)
                vector.wait_ge(lmv.h, lmv.tot)
                _inc(vector.tensor_reduce(lsc[:, 0:1], lvec[:, 0:64], axis=AX.X, op=ALU.add),
                     lmv, 1)
                vector.wait_ge(lmv.h, lmv.tot)
                _inc(vector.tensor_tensor(lvec[:, 128:192], lvec[:, 128:192],
                                          lvec[:, 192:256], op=ALU.mult), lmv, 1)
                vector.wait_ge(lmv.h, lmv.tot)
                _inc(vector.tensor_reduce(lsc[:, 1:2], lvec[:, 128:192], axis=AX.X,
                                          op=ALU.add), lmv, 1)
                vector.wait_ge(lme.h, b_lme + 2)
                _inc(vector.tensor_tensor(lsc[:, 4:5], lsc[:, 2:3], lsc[:, 3:4],
                                          op=ALU.subtract), lmv, 1)
                vector.wait_ge(lmv.h, lmv.tot)
                _inc(vector.tensor_scalar(lsc[:, 4:5], lsc[:, 4:5], 0.8, None, op0=ALU.add),
                     lmv, 1)
                vector.wait_ge(lmv.h, lmv.tot)
                _inc(vector.tensor_scalar(lsc[:, 4:5], lsc[:, 4:5], 0.9, None, op0=ALU.min),
                     lmv, 1)
                vector.wait_ge(lmv.h, lmv.tot)
                _inc(vector.tensor_scalar(lsc[:, 4:5], lsc[:, 4:5], 0.1, None, op0=ALU.max),
                     lmv, 1)
                vector.wait_ge(lmv.h, lmv.tot)
                _inc(vector.tensor_scalar(lsc[:, 5:6], lsc[:, 4:5], -1.0, None,
                                          op0=ALU.mult), lmv, 1)

        with contextlib.ExitStack() as esl:
            psl = esl.enter_context(nc.psum_tensor(f"psl{tag}", [P, 1], F32))
            with nc.Block(no_gpsimd_drain=True) as block:
                @block.tensor
                def _(tensor):
                    _inc(tensor.matmul(psl[:], consts["ones_row"][:], lsc[:, 5:6],
                                       start=True, stop=True), lme, 1)

                @block.vector
                def _(vector):
                    vector.wait_ge(lme.h, b_lme + 3)
                    vector.tensor_copy(neglam[:], psl[:])

        xhb = [wsb(f"xh{tag}{j}", [P, D], halo_dt) for j in range(3)]
        cosb = [wsb(f"cos{tag}{j}", [P, 512], F32) for j in range(2)]
        sinb = [wsb(f"sin{tag}{j}", [P, 512], F32) for j in range(2)]
        coskb = [wsb(f"cosk{tag}{j}", [P, 128], F32) for j in range(2)]
        sinkb = [wsb(f"sink{tag}{j}", [P, 128], F32) for j in range(2)]
        stats = [wsb(f"st{tag}{j}", [P, 12], F32) for j in range(2)]
        mvt = [wsb(f"mv{tag}{j}", [P, 2], F32) for j in range(2)]
        sqvt = [wsb(f"sqv{tag}{j}", [P, 1], F32) for j in range(2)]
        rstd = [wsb(f"rstd{tag}{j}", [P, 1], F32) for j in range(2)]
        xnb = [wsb(f"xn{tag}{j}", [P, D], BF16) for j in range(2)]
        lnT = [wsb(f"lnT{tag}{j}", [P, DC, P], BF16) for j in range(2)]
        q1r = [wsb(f"q1r{tag}{j}", [P, D], BF16) for j in range(2)]
        q2r = [wsb(f"q2r{tag}{j}", [P, D], BF16) for j in range(2)]
        krb = [wsb(f"kr{tag}{j}", [P, 256], BF16) for j in range(2)]
        rt1 = wsb(f"rt1{tag}", [P, 512], F32)
        rt2 = wsb(f"rt2{tag}", [P, 512], F32)
        trsl = [wps(f"tr{tag}{j}", [P, P], BF16) for j in range(3)]
        psq1 = [wps(f"pq1{tag}{j}", [P, 512], F32) for j in range(2)]
        psq2 = [wps(f"pq2{tag}{j}", [P, 512], F32) for j in range(2)]
        pskv = wps(f"pkv{tag}", [P, 384], F32)

        ldx3 = ctx.chn(f"ldx{tag}", 3)
        ldc2 = ctx.chn(f"ldc{tag}", 2)
        sq = ctx.ch(f"sq{tag}")
        agg = ctx.ch(f"agg{tag}")
        vr = ctx.ch(f"vr{tag}")
        xq = ctx.ch(f"xq{tag}")
        trr = ctx.ch(f"trr{tag}")
        tev = ctx.ch(f"tev{tag}")
        rte = ctx.ch(f"rte{tag}")
        mq1 = ctx.ch(f"mq1{tag}")
        mq2 = ctx.ch(f"mq2{tag}")
        mkv = ctx.ch(f"mkv{tag}")
        ropd = ctx.ch(f"ropd{tag}")
        vev = ctx.ch(f"vev{tag}")
        b_ldx3 = [c.base() for c in ldx3]
        b_ldc2 = [c.base() for c in ldc2]
        b_sq, b_agg, b_xq = sq.base(), agg.base(), xq.base()
        b_trr, b_tev, b_rte = trr.base(), tev.base(), rte.base()
        b_mq1, b_mq2, b_mkv = mq1.base(), mq2.base(), mkv.base()
        b_ropd, b_vev = ropd.base(), vev.base()

        def slot_guard(tensor_eng, t, j):
            gt = t * 26 + j
            if gt < 3:
                return
            pt, pj = (gt - 3) // 26, (gt - 3) % 26
            if pj < 8:
                tensor_eng.wait_ge(tev.h, b_tev + pt * 8 + pj + 1)
            else:
                tensor_eng.wait_ge(rte.h, b_rte + pt * 18 + (pj - 8) + 1)

        with nc.Block() as block:
            @block.sync
            def _(sync):
                for t in range(NHT):
                    if t >= 3:
                        sync.wait_ge(xq.h, b_xq + t - 2)
                    _inc(sync.dma_start(out=xhb[t % 3][:],
                                        in_=halo_ap[t * P:(t + 1) * P, :]), ldx3[t % 3], 16)
                    if t >= 2:
                        sync.wait_ge(ropd.h, b_ropd + 5 * (t - 1))
                    _inc(sync.dma_start(out=cosb[t % 2][:],
                                        in_=cos_h[t * P:(t + 1) * P, :]), ldc2[t % 2], 16)
                    _inc(sync.dma_start(out=sinb[t % 2][:],
                                        in_=sin_h[t * P:(t + 1) * P, :]), ldc2[t % 2], 16)
                    _inc(sync.dma_start(out=coskb[t % 2][:],
                                        in_=cosk_h[t * P:(t + 1) * P, :]), ldc2[t % 2], 16)
                    _inc(sync.dma_start(out=sinkb[t % 2][:],
                                        in_=sink_h[t * P:(t + 1) * P, :]), ldc2[t % 2], 16)

            @block.vector
            def _(vector):
                def ln(t):
                    pr = t % 2
                    vector.wait_ge(ldx3[t % 3].h, b_ldx3[t % 3] + 16 * (t // 3 + 1))
                    if t >= 2:
                        vector.wait_ge(sq.h, b_sq + t - 1)
                        vector.wait_ge(trr.h, b_trr + 26 * (t - 2) + 8)
                    vector.bn_stats(stats[pr][:, 0:6], xhb[t % 3][:, 0:512])
                    _inc(vector.bn_stats(stats[pr][:, 6:12], xhb[t % 3][:, 512:1024]), vr, 1)
                    vector.wait_ge(vr.h, vr.tot)
                    _inc(vector.bn_aggr(mvt[pr][:], stats[pr][:]), agg, 1)
                    vector.wait_ge(agg.h, agg.tot)
                    vector.wait_ge(sq.h, b_sq + t + 1)
                    _inc(vector.reciprocal(rstd[pr][:], sqvt[pr][:]), vr, 1)
                    vector.wait_ge(vr.h, vr.tot)
                    ins = vector.tensor_scalar(xnb[pr][:], xhb[t % 3][:], mvt[pr][:, 0:1],
                                               rstd[pr][:], op0=ALU.subtract, op1=ALU.mult)
                    _inc(ins, xq, 1)

                def rope(t):
                    pr = t % 2
                    vector.wait_ge(ldc2[t % 2].h, b_ldc2[t % 2] + 64 * (t // 2 + 1))
                    if t >= 2:
                        vector.wait_ge(trr.h, b_trr + 26 * (t - 1))
                    jobs = [(psq1[0], q1r[pr][:, 0:512], 8, mq1, b_mq1, 2, 1),
                            (psq1[1], q1r[pr][:, 512:1024], 8, mq1, b_mq1, 2, 2),
                            (psq2[0], q2r[pr][:, 0:512], 8, mq2, b_mq2, 2, 1),
                            (psq2[1], q2r[pr][:, 512:1024], 8, mq2, b_mq2, 2, 2),
                            (pskv[:, 0:256], krb[pr][:, 0:256], 4, mkv, b_mkv, 1, 1)]
                    for ji, (bnk, dstslice, nh, mch, mbase, per, mcnt) in enumerate(jobs):
                        vector.wait_ge(mch.h, mbase + per * t + mcnt)
                        vector.wait_ge(ropd.h, b_ropd + 5 * t + ji)
                        W2 = nh * 32
                        bv = bnk if ji == 4 else bnk[:]
                        src_e = bv.rearrange("p (h d two) -> p h d two", h=nh, d=32,
                                             two=2)[:, :, :, 0:1]
                        src_o = bv.rearrange("p (h d two) -> p h d two", h=nh, d=32,
                                             two=2)[:, :, :, 1:2]
                        dst_h0 = dstslice.rearrange("p (h two d) -> p h two d", h=nh, two=2,
                                                    d=32)[:, :, 0:1, :]
                        dst_h1 = dstslice.rearrange("p (h two d) -> p h two d", h=nh, two=2,
                                                    d=32)[:, :, 1:2, :]
                        if mch is mkv:
                            cosv = coskb[pr][:, 0:W2]
                            sinv = sinkb[pr][:, 0:W2]
                        else:
                            cosv = cosb[pr][:, 0:W2]
                            sinv = sinb[pr][:, 0:W2]
                        _inc(vector.tensor_tensor(rt1[:, 0:W2], src_e, cosv, op=ALU.mult),
                             vr, 1)
                        _inc(vector.tensor_tensor(rt2[:, 0:W2], src_o, sinv, op=ALU.mult),
                             vr, 1)
                        vector.wait_ge(vr.h, vr.tot)
                        _inc(vector.tensor_tensor(dst_h0, rt1[:, 0:W2], rt2[:, 0:W2],
                                                  op=ALU.subtract), vr, 1)
                        vector.wait_ge(vr.h, vr.tot)
                        _inc(vector.tensor_tensor(rt1[:, 0:W2], src_e, sinv, op=ALU.mult),
                             vr, 1)
                        _inc(vector.tensor_tensor(rt2[:, 0:W2], src_o, cosv, op=ALU.mult),
                             vr, 1)
                        vector.wait_ge(vr.h, vr.tot)
                        _inc(vector.tensor_tensor(dst_h1, rt1[:, 0:W2], rt2[:, 0:W2],
                                                  op=ALU.add), ropd, 1)

                ln(0)
                for t in range(NHT):
                    if t + 1 < NHT:
                        ln(t + 1)
                    pr = t % 2
                    if t >= 2:
                        vector.wait_ge(mkv.h, b_mkv + t - 1)
                    for k in range(DC):
                        vector.wait_ge(trr.h, b_trr + t * 26 + k + 1)
                        ins = vector.tensor_copy(lnT[pr][:, k, :], trsl[(t * 26 + k) % 3][:])
                        _inc(ins, tev, 1)
                    rope(t)

            @block.scalar
            def _(scalar):
                def sqrt_t(t):
                    pr = t % 2
                    scalar.wait_ge(agg.h, b_agg + t + 1)
                    if t >= 2:
                        scalar.wait_ge(xq.h, b_xq + t - 1)
                    _inc(scalar.activation(sqvt[pr][:], mvt[pr][:, 1:2], AF.Sqrt,
                                           bias=consts["eps"][:]), sq, 1)
                sqrt_t(0)
                for t in range(NHT):
                    if t + 1 < NHT:
                        sqrt_t(t + 1)
                    scalar.wait_ge(mkv.h, b_mkv + t + 1)
                    if t >= 1:
                        scalar.wait_ge(ropd.h, b_ropd + 5 * t)
                    _inc(scalar.activation(v_sb[:, t, :], pskv[:, 256:384], AF.Copy), vev, 1)
                    for j in range(18):
                        gt = t * 26 + 8 + j
                        scalar.wait_ge(trr.h, b_trr + gt + 1)
                        if j < 8:
                            dst = q1T[:, j, t * P:(t + 1) * P]
                        elif j < 16:
                            dst = q2T[:, j - 8, t * P:(t + 1) * P]
                        elif j == 16:
                            dst = k1T[:, t * P:(t + 1) * P]
                        else:
                            dst = k2T[:, t * P:(t + 1) * P]
                        _inc(scalar.activation(dst, trsl[gt % 3][:], AF.Copy), rte, 1)

            @block.tensor
            def _(tensor):
                for t in range(NHT):
                    pr = t % 2
                    tensor.wait_ge(xq.h, b_xq + t + 1)
                    for k in range(DC):
                        slot_guard(tensor, t, k)
                        ins = tensor.transpose(trsl[(t * 26 + k) % 3][:],
                                               xnb[pr][:, k * P:(k + 1) * P], ident[:])
                        _inc(ins, trr, 1)
                    tensor.wait_ge(tev.h, b_tev + 8 * (t + 1))
                    if t >= 1:
                        tensor.wait_ge(ropd.h, b_ropd + 5 * t)
                        tensor.wait_ge(vev.h, b_vev + t)
                    for bq in range(2):
                        for k in range(DC):
                            ins = tensor.matmul(psq1[bq][:], lnT[pr][:, k, :],
                                                Wq1[:, k, bq * 512:(bq + 1) * 512],
                                                start=(k == 0), stop=(k == DC - 1))
                        _inc(ins, mq1, 1)
                    for bq in range(2):
                        for k in range(DC):
                            ins = tensor.matmul(psq2[bq][:], lnT[pr][:, k, :],
                                                Wq2[:, k, bq * 512:(bq + 1) * 512],
                                                start=(k == 0), stop=(k == DC - 1))
                        _inc(ins, mq2, 1)
                    for k in range(DC):
                        ins = tensor.matmul(pskv[:], lnT[pr][:, k, :], Wkv[:, k, :],
                                            start=(k == 0), stop=(k == DC - 1))
                    _inc(ins, mkv, 1)
                    tensor.wait_ge(ropd.h, b_ropd + 5 * (t + 1))
                    for j in range(18):
                        slot_guard(tensor, t, 8 + j)
                        if j < 8:
                            srcv = q1r[pr][:, j * P:(j + 1) * P]
                        elif j < 16:
                            srcv = q2r[pr][:, (j - 8) * P:(j - 7) * P]
                        elif j == 16:
                            srcv = krb[pr][:, 0:128]
                        else:
                            srcv = krb[pr][:, 128:256]
                        ins = tensor.transpose(trsl[(t * 26 + 8 + j) % 3][:], srcv, ident[:])
                        _inc(ins, trr, 1)

    A_sb = sb(f"A{tag}", [P, 32, D], BF16)
    with contextlib.ExitStack() as esb:
        bsb = lambda n, s, d: esb.enter_context(nc.sbuf_tensor(n, s, d))
        bps = lambda n, s, d: esb.enter_context(nc.psum_tensor(n, s, d))
        p_sb2 = [bsb(f"p2{tag}{j}", [P, 2, 256], BF16) for j in range(2)]
        pt_sb = bsb(f"pt{tag}", [P, 4, P], BF16)
        nm1 = [bsb(f"nm1{tag}{j}", [P, 1], F32) for j in range(2)]
        nm2 = [bsb(f"nm2{tag}{j}", [P, 1], F32) for j in range(2)]
        bb1 = [bsb(f"bb1{tag}{j}", [P, 1], F32) for j in range(2)]
        bb2 = [bsb(f"bb2{tag}{j}", [P, 1], F32) for j in range(2)]
        ss1 = [bsb(f"ss1{tag}{j}", [P, 1], F32) for j in range(2)]
        ss2 = [bsb(f"ss2{tag}{j}", [P, 1], F32) for j in range(2)]
        rr1 = [bsb(f"rr1{tag}{j}", [P, 1], F32) for j in range(2)]
        rr2 = [bsb(f"rr2{tag}{j}", [P, 1], F32) for j in range(2)]
        tmpa = bsb(f"tmpa{tag}", [P, 64], F32)
        tmpb = bsb(f"tmpb{tag}", [P, 64], F32)
        pss1 = [bps(f"ps1{tag}{j}", [P, 256], F32) for j in range(2)]
        pss2 = [bps(f"ps2{tag}{j}", [P, 256], F32) for j in range(2)]
        pts = [bps(f"pts{tag}{j}", [P, P], BF16) for j in range(2)]
        psa1 = bps(f"pa1{tag}", [P, 64], F32)
        psa2 = bps(f"pa2{tag}", [P, 64], F32)

        mqk = ctx.ch(f"mqk{tag}")
        nmb = ctx.ch(f"nmb{tag}")
        expd = ctx.ch(f"expd{tag}")
        rch = ctx.ch(f"rch{tag}")
        ptrc = ctx.ch(f"ptr{tag}")
        ptev = ctx.ch(f"ptev{tag}")
        mpv = ctx.ch(f"mpv{tag}")
        cmb = ctx.ch(f"cmb{tag}")
        b_mqk, b_nmb, b_expd, b_rch = mqk.base(), nmb.base(), expd.base(), rch.base()
        b_ptr, b_ptev, b_mpv, b_cmb = ptrc.base(), ptev.base(), mpv.base(), cmb.base()

        def unit(u):
            a, h = u // 16, u % 16
            w, qh = a // 2, a % 2
            return w, qh, h, a

        with nc.Block() as block:
            @block.tensor
            def _(tensor):
                for s in range(U + 2):
                    if s < U:
                        u = s
                        w, qh, h, a = unit(u)
                        g = h % 2
                        qcol = 128 * (w + qh)
                        kcol = 128 * w
                        if u >= 2:
                            tensor.wait_ge(expd.h, b_expd + 2 * (u - 1))
                        ins = tensor.matmul(pss1[u % 2][:],
                                            q1T[64 * g:64 * g + 64, h // 2, qcol:qcol + 128],
                                            k1T[64 * g:64 * g + 64, kcol:kcol + 256],
                                            start=True, stop=True)
                        _inc(ins, mqk, 1)
                        ins = tensor.matmul(pss2[u % 2][:],
                                            q2T[64 * g:64 * g + 64, h // 2, qcol:qcol + 128],
                                            k2T[64 * g:64 * g + 64, kcol:kcol + 256],
                                            start=True, stop=True)
                        _inc(ins, mqk, 1)
                    if 2 <= s:
                        u = s - 2
                        w, qh, h, a = unit(u)
                        g = h % 2
                        tensor.wait_ge(ptev.h, b_ptev + 4 * (u + 1))
                        if u >= 1:
                            tensor.wait_ge(cmb.h, b_cmb + 3 * u)
                        for br, dstp in ((0, psa1), (1, psa2)):
                            for j in range(2):
                                rv = v_sb[:, w + j, 64 * g:64 * g + 64]
                                ins = tensor.matmul(dstp[:], pt_sb[:, 2 * br + j, :], rv,
                                                    start=(j == 0), stop=(j == 1))
                            _inc(ins, mpv, 1)
                    if 1 <= s <= U:
                        u = s - 1
                        tensor.wait_ge(expd.h, b_expd + 2 * (u + 1))
                        for j in range(4):
                            gj = 4 * u + j
                            if gj >= 2:
                                tensor.wait_ge(ptev.h, b_ptev + gj - 1)
                            br, half = j // 2, j % 2
                            srcv = p_sb2[u % 2][:, br, half * 128:(half + 1) * 128]
                            ins = tensor.transpose(pts[gj % 2][:], srcv, ident[:])
                            _inc(ins, ptrc, 1)

            @block.vector
            def _(vector):
                for s in range(U + 2):
                    if s < U:
                        u = s
                        pr = u % 2
                        if u >= 2:
                            vector.wait_ge(expd.h, b_expd + 2 * (u - 1))
                        vector.wait_ge(mqk.h, b_mqk + 2 * u + 1)
                        _inc(vector.tensor_reduce(nm1[pr][:], pss1[pr][:], axis=AX.X,
                                                  op=ALU.max, negate=True), nmb, 1)
                        vector.wait_ge(mqk.h, b_mqk + 2 * u + 2)
                        _inc(vector.tensor_reduce(nm2[pr][:], pss2[pr][:], axis=AX.X,
                                                  op=ALU.max, negate=True), nmb, 1)
                    if 1 <= s <= U:
                        u = s - 1
                        pr = u % 2
                        vector.wait_ge(expd.h, b_expd + 2 * (u + 1))
                        if u >= 2:
                            vector.wait_ge(cmb.h, b_cmb + 3 * (u - 1))
                        _inc(vector.reciprocal(rr1[pr][:], ss1[pr][:]), rch, 1)
                        _inc(vector.reciprocal(rr2[pr][:], ss2[pr][:]), rch, 1)
                    if 2 <= s:
                        u = s - 2
                        pr = u % 2
                        w, qh, h, a = unit(u)
                        vector.wait_ge(mpv.h, b_mpv + 2 * (u + 1))
                        vector.wait_ge(rch.h, b_rch + 2 * (u + 1))
                        _inc(vector.tensor_scalar(tmpa[:], psa1[:], rr1[pr][:], None,
                                                  op0=ALU.mult), cmb, 1)
                        _inc(vector.tensor_scalar(tmpb[:], psa2[:], rr2[pr][:], neglam[:],
                                                  op0=ALU.mult, op1=ALU.mult), cmb, 1)
                        vector.wait_ge(cmb.h, cmb.tot)
                        ins = vector.tensor_tensor(A_sb[:, a, 64 * h:64 * h + 64],
                                                   tmpa[:], tmpb[:], op=ALU.add)
                        _inc(ins, cmb, 1)

            @block.scalar
            def _(scalar):
                for u in range(U):
                    pr = u % 2
                    scalar.wait_ge(nmb.h, b_nmb + 2 * u + 1)
                    if u >= 2:
                        scalar.wait_ge(ptrc.h, b_ptr + 4 * (u - 1))
                        scalar.wait_ge(rch.h, b_rch + 2 * (u - 1))
                    _inc(scalar.activation(p_sb2[pr][:, 0, :], pss1[pr][:], AF.Exp,
                                           bias=nm1[pr][:],
                                           accum_out=ss1[pr][:]), expd, 1)
                    scalar.wait_ge(nmb.h, b_nmb + 2 * u + 2)
                    _inc(scalar.activation(p_sb2[pr][:, 1, :], pss2[pr][:], AF.Exp,
                                           bias=nm2[pr][:],
                                           accum_out=ss2[pr][:]), expd, 1)
                    if u >= 1:
                        v = u - 1
                        scalar.wait_ge(mpv.h, b_mpv + 2 * v)
                        for j in range(4):
                            gj = 4 * v + j
                            scalar.wait_ge(ptrc.h, b_ptr + gj + 1)
                            _inc(scalar.activation(pt_sb[:, j, :], pts[gj % 2][:], AF.Copy),
                                 ptev, 1)
                for v in (U - 1,):
                    scalar.wait_ge(mpv.h, b_mpv + 2 * v)
                    for j in range(4):
                        gj = 4 * v + j
                        scalar.wait_ge(ptrc.h, b_ptr + gj + 1)
                        _inc(scalar.activation(pt_sb[:, j, :], pts[gj % 2][:], AF.Copy),
                             ptev, 1)

    with contextlib.ExitStack() as esc:
        csb = lambda n, s, d: esc.enter_context(nc.sbuf_tensor(n, s, d))
        cps = lambda n, s, d: esc.enter_context(nc.psum_tensor(n, s, d))
        Wout = csb(f"Wout{tag}", [P, DC, D], BF16)
        bout_row = csb(f"bor{tag}", [1, D], F32)
        bout_b = csb(f"bob{tag}", [P, D], F32)
        wf2 = [csb(f"wf2{tag}{j}", [P, D], F32) for j in range(2)]
        lwo2 = ctx.chn(f"lw{tag}", 2)
        cwo = ctx.ch(f"cw{tag}")
        mbo = ctx.ch(f"lme{tag}")
        ldbo = ctx.chn(f"ldc{tag}", 2)[0]
        b_lwo2 = [c.base() for c in lwo2]
        b_cwo, b_mbo = cwo.base(), mbo.base()
        b_ldbo = ldbo.base()
        poC = [cps(f"po{tag}{j}", [P, 512], F32) for j in range(2)]
        with nc.Block(no_gpsimd_drain=True) as block:
            @block.sync
            def _(sync):
                for k in range(DC):
                    if k >= 2:
                        sync.wait_ge(cwo.h, b_cwo + k - 1)
                    _inc(sync.dma_start(out=wf2[k % 2][:],
                                        in_=Wout_h[k * P:(k + 1) * P, :]), lwo2[k % 2], 16)
                _inc(sync.dma_start(out=bout_row[:],
                                    in_=bout_h.ap().rearrange("(one d) -> one d", one=1)),
                     ldbo, 16)

            @block.scalar
            def _(scalar):
                for k in range(DC):
                    scalar.wait_ge(lwo2[k % 2].h, b_lwo2[k % 2] + 16 * (k // 2 + 1))
                    _inc(scalar.activation(Wout[:, k, :], wf2[k % 2][:], AF.Copy), cwo, 1)

            @block.tensor
            def _(tensor):
                tensor.wait_ge(ldbo.h, b_ldbo + 16)
                _inc(tensor.matmul(poC[0][:], consts["ones_row"][:], bout_row[:, 0:512],
                                   start=True, stop=True), mbo, 1)
                _inc(tensor.matmul(poC[1][:], consts["ones_row"][:], bout_row[:, 512:1024],
                                   start=True, stop=True), mbo, 1)

            @block.vector
            def _(vector):
                vector.wait_ge(mbo.h, b_mbo + 2)
                vector.tensor_copy(bout_b[:, 0:512], poC[0][:])
                vector.tensor_copy(bout_b[:, 512:1024], poC[1][:])

        xbC = [csb(f"xbC{tag}{j}", [P, D], F32) for j in range(2)]
        AT = [csb(f"AT{tag}{j}", [P, DC, P], BF16) for j in range(2)]
        obuf = [csb(f"obC{tag}{j}", [P, D], F32) for j in range(2)]
        trslC = [cps(f"trC{tag}{j}", [P, P], BF16) for j in range(4)]
        ldxC = ctx.chn(f"ldx{tag}", 2)
        trrC = ctx.ch(f"trr{tag}")
        tevC = ctx.ch(f"tev{tag}")
        moC = ctx.ch(f"mq1{tag}")
        vrC = ctx.ch(f"vr{tag}")
        epoC = ctx.ch(f"xq{tag}")
        stoC = ctx.chn("qssqw", 2)
        b_ldxC = [c.base() for c in ldxC]
        b_trrC, b_tevC, b_moC = trrC.base(), tevC.base(), moC.base()
        b_vrC, b_epoC = vrC.base(), epoC.base()
        b_stoC = [c.base() for c in stoC]
        with nc.Block() as block:
            @block.sync
            def _(sync):
                for a in range(32):
                    if a >= 2:
                        sync.wait_ge(epoC.h, b_epoC + 2 * (a - 1))
                    _inc(sync.dma_start(out=xbC[a % 2][:],
                                        in_=src_ap[a * P:(a + 1) * P, :]), ldxC[a % 2], 16)

            @block.tensor
            def _(tensor):
                for a in range(32):
                    pr = a % 2
                    for k in range(DC):
                        g = a * DC + k
                        if g >= 4:
                            tensor.wait_ge(tevC.h, b_tevC + g - 3)
                        ins = tensor.transpose(trslC[g % 4][:], A_sb[:, a, k * P:(k + 1) * P],
                                               ident[:])
                        _inc(ins, trrC, 1)
                    tensor.wait_ge(tevC.h, b_tevC + DC * (a + 1))
                    if a >= 1:
                        tensor.wait_ge(epoC.h, b_epoC + 2 * a)
                    for nb in range(2):
                        for k in range(DC):
                            ins = tensor.matmul(poC[nb][:], AT[pr][:, k, :],
                                                Wout[:, k, nb * 512:(nb + 1) * 512],
                                                start=(k == 0), stop=(k == DC - 1))
                        _inc(ins, moC, 1)

            @block.vector
            def _(vector):
                for a in range(32):
                    pr = a % 2
                    if a >= 2:
                        vector.wait_ge(moC.h, b_moC + 2 * (a - 1))
                    for k in range(DC):
                        g = a * DC + k
                        vector.wait_ge(trrC.h, b_trrC + g + 1)
                        ins = vector.tensor_copy(AT[pr][:, k, :], trslC[g % 4][:])
                        _inc(ins, tevC, 1)
                    vector.wait_ge(ldxC[pr].h, b_ldxC[pr] + 16 * (a // 2 + 1))
                    if a >= 2:
                        vector.wait_ge(stoC[pr].h, b_stoC[pr] + 16 * (a // 2))
                    for nb in range(2):
                        vector.wait_ge(moC.h, b_moC + 2 * a + nb + 1)
                        sl = slice(nb * 512, (nb + 1) * 512)
                        _inc(vector.tensor_tensor(obuf[pr][:, sl], poC[nb][:],
                                                  xbC[pr][:, sl], op=ALU.add), vrC, 1)
                        vector.wait_ge(vrC.h, vrC.tot)
                        _inc(vector.tensor_tensor(obuf[pr][:, sl], obuf[pr][:, sl],
                                                  bout_b[:, sl], op=ALU.add), epoC, 1)

            @block.gpsimd
            def _(gpsimd):
                for a in range(32):
                    pr = a % 2
                    gpsimd.wait_ge(epoC.h, b_epoC + 2 * (a + 1))
                    _inc(gpsimd.dma_start(out=dst_ap[a * P:(a + 1) * P, :], in_=obuf[pr][:]),
                         stoC[pr], 16)
                for q in range(2):
                    gpsimd.wait_ge(stoC[q].h, stoC[q].tot)
    es0.close()


def emit_attn1(ctx, src_ap, dst_ap, Wq_h, bq_h, Wk_h, bk_h, Wv_h, bv_h, consts,
               tag="a1"):
    nc = ctx.nc
    ident = consts["ident"]
    with contextlib.ExitStack() as es2:
        sb = lambda n, s, d: es2.enter_context(nc.sbuf_tensor(n, s, d))
        ps = lambda n, s, d: es2.enter_context(nc.psum_tensor(n, s, d))
        WqT = sb(f"WqT{tag}", [P, DC, D], BF16)
        WkT = sb(f"WkT{tag}", [P, DC, D], BF16)
        WvT = sb(f"WvT{tag}", [P, DC, D], BF16)
        emit_conv_int_weight(ctx, Wq_h, D, D, WqT, ident, "cv", src_dt=F32)
        emit_conv_int_weight(ctx, Wk_h, D, D, WkT, ident, "cv", src_dt=F32)
        emit_conv_int_weight(ctx, Wv_h, D, D, WvT, ident, "cv", src_dt=F32)
        bq_sb = sb(f"bq{tag}", [P, DC], F32)
        bk_sb = sb(f"bk{tag}", [P, DC], F32)
        bv_row = sb(f"bvr{tag}", [1, D], F32)
        bv_b = sb(f"bvb{tag}", [P, D], F32)
        es_b = contextlib.ExitStack()
        psb = [es_b.enter_context(nc.psum_tensor(f"psb{tag}{j}", [P, 512], F32))
               for j in range(2)]
        ldb = ctx.chn(f"ldx{tag}", 3)[0]
        mmb = ctx.ch(f"lme{tag}")
        b_ldb, b_mmb = ldb.base(), mmb.base()
        with nc.Block(no_gpsimd_drain=True) as block:
            @block.sync
            def _(sync):
                with nc.allow_non_contiguous_dma(reason="tiny bias rearrange"):
                    _inc(sync.dma_start(out=bq_sb[:], in_=bq_h.ap().rearrange("(k p) -> p k", p=P),
                                        allow_slow_non_contiguous=True), ldb, 16)
                    _inc(sync.dma_start(out=bk_sb[:], in_=bk_h.ap().rearrange("(k p) -> p k", p=P),
                                        allow_slow_non_contiguous=True), ldb, 16)
                _inc(sync.dma_start(out=bv_row[:], in_=bv_h.ap().rearrange("(one d) -> one d", one=1)),
                     ldb, 16)
                sync.wait_ge(ldb.h, b_ldb + 48)

            @block.tensor
            def _(tensor):
                tensor.wait_ge(ldb.h, b_ldb + 48)
                _inc(tensor.matmul(psb[0][:], consts["ones_row"][:], bv_row[:, 0:512],
                                   start=True, stop=True), mmb, 1)
                _inc(tensor.matmul(psb[1][:], consts["ones_row"][:], bv_row[:, 512:1024],
                                   start=True, stop=True), mmb, 1)

            @block.vector
            def _(vector):
                vector.wait_ge(mmb.h, b_mmb + 2)
                vector.tensor_copy(bv_b[:, 0:512], psb[0][:])
                vector.tensor_copy(bv_b[:, 512:1024], psb[1][:])
        es_b.close()

        lnT = sb(f"lnT{tag}", [P, DC, 1024], BF16)
        qT = sb(f"qT{tag}", [P, DC, 1024], BF16)
        kT = sb(f"kT{tag}", [P, DC, 1024], BF16)
        v_sb = sb(f"v{tag}", [P, DC, 1024], BF16)
        p_sb = [sb(f"p{tag}{j}", [P, 1024], BF16) for j in range(2)]
        pt_sb = sb(f"pt{tag}", [P, DC, P], BF16)
        xbuf = [sb(f"xa{tag}{j}", [P, D], F32) for j in range(3)]
        xn = [sb(f"xn{tag}{j}", [P, D], BF16) for j in range(2)]
        stats = [sb(f"st{tag}{j}", [P, 12], F32) for j in range(2)]
        mv = [sb(f"mv{tag}{j}", [P, 2], F32) for j in range(2)]
        sqv = [sb(f"sq{tag}{j}", [P, 1], F32) for j in range(2)]
        rstd = [sb(f"rs{tag}{j}", [P, 1], F32) for j in range(2)]
        nmt = [sb(f"nm{tag}{j}", [P, 1], F32) for j in range(3)]
        ssm = [sb(f"ss{tag}{j}", [P, 1], F32) for j in range(3)]
        rr = sb(f"rr{tag}", [P, 1], F32)
        xb2 = [sb(f"xb2{tag}{j}", [P, D], F32) for j in range(2)]
        obuf = [sb(f"ob{tag}{j}", [P, D], F32) for j in range(2)]
        trsl = [ps(f"tr{tag}{j}", [P, P], BF16) for j in range(4)]
        psqk = [ps(f"pqk{tag}{j}", [P, 512], F32) for j in range(2)]
        psv = psqk
        pss = [ps(f"pss{tag}{j}", [P, 512], F32) for j in range(2)]
        pso = psqk

        for chn in range(4):
            base = chn * 1024
            ld3 = ctx.chn(f"ldx{tag}", 3)
            sq = ctx.ch(f"sq{tag}")
            agg = ctx.ch(f"agg{tag}")
            vr = ctx.ch(f"vr{tag}")
            xq = ctx.ch(f"xq{tag}")
            trr = ctx.ch(f"trr{tag}")
            tev = ctx.ch(f"tev{tag}")
            b_ld3 = [c.base() for c in ld3]
            b_sq, b_agg, b_xq = sq.base(), agg.base(), xq.base()
            b_trr, b_tev = trr.base(), tev.base()
            with nc.Block() as block:
                @block.sync
                def _(sync):
                    for t in range(8):
                        if t >= 3:
                            sync.wait_ge(xq.h, b_xq + t - 2)
                        _inc(sync.dma_start(out=xbuf[t % 3][:],
                                            in_=src_ap[base + t * P:base + (t + 1) * P, :]),
                             ld3[t % 3], 16)

                @block.vector
                def _(vector):
                    def ln(t):
                        pr = t % 2
                        vector.wait_ge(ld3[t % 3].h, b_ld3[t % 3] + 16 * (t // 3 + 1))
                        if t >= 2:
                            vector.wait_ge(sq.h, b_sq + t - 1)
                            vector.wait_ge(trr.h, b_trr + DC * (t - 1))
                        vector.bn_stats(stats[pr][:, 0:6], xbuf[t % 3][:, 0:512])
                        _inc(vector.bn_stats(stats[pr][:, 6:12], xbuf[t % 3][:, 512:1024]), vr, 1)
                        vector.wait_ge(vr.h, vr.tot)
                        _inc(vector.bn_aggr(mv[pr][:], stats[pr][:]), agg, 1)
                        vector.wait_ge(agg.h, agg.tot)
                        vector.wait_ge(sq.h, b_sq + t + 1)
                        _inc(vector.reciprocal(rstd[pr][:], sqv[pr][:]), vr, 1)
                        vector.wait_ge(vr.h, vr.tot)
                        ins = vector.tensor_scalar(xn[pr][:], xbuf[t % 3][:], mv[pr][:, 0:1],
                                                   rstd[pr][:], op0=ALU.subtract, op1=ALU.mult)
                        _inc(ins, xq, 1)
                    ln(0)
                    for t in range(8):
                        if t + 1 < 8:
                            ln(t + 1)
                        for k in range(DC):
                            g = t * DC + k
                            vector.wait_ge(trr.h, b_trr + g + 1)
                            ins = vector.tensor_copy(lnT[:, k, t * P:(t + 1) * P], trsl[g % 4][:])
                            _inc(ins, tev, 1)

                @block.scalar
                def _(scalar):
                    for t in range(8):
                        pr = t % 2
                        scalar.wait_ge(agg.h, b_agg + t + 1)
                        if t >= 2:
                            scalar.wait_ge(xq.h, b_xq + t - 1)
                        _inc(scalar.activation(sqv[pr][:], mv[pr][:, 1:2], AF.Sqrt,
                                               bias=consts["eps"][:]), sq, 1)

                @block.tensor
                def _(tensor):
                    for t in range(8):
                        pr = t % 2
                        tensor.wait_ge(xq.h, b_xq + t + 1)
                        for k in range(DC):
                            g = t * DC + k
                            if g >= 4:
                                tensor.wait_ge(tev.h, b_tev + g - 3)
                            ins = tensor.transpose(trsl[g % 4][:],
                                                   xn[pr][:, k * P:(k + 1) * P], ident[:])
                            _inc(ins, trr, 1)

            mqk = ctx.ch(f"mqk{tag}")
            eqk = ctx.ch(f"lmv{tag}")
            b_mqk, b_eqk = mqk.base(), eqk.base()
            with nc.Block() as block:
                @block.tensor
                def _(tensor):
                    g = 0
                    for (WT,) in ((WqT,), (WkT,)):
                        for ec in range(DC):
                            for tg in range(2):
                                if g >= 2:
                                    tensor.wait_ge(eqk.h, b_eqk + g - 1)
                                for k in range(DC):
                                    ins = tensor.matmul(psqk[g % 2][:],
                                                        WT[:, k, ec * P:(ec + 1) * P],
                                                        lnT[:, k, tg * 512:(tg + 1) * 512],
                                                        start=(k == 0), stop=(k == DC - 1))
                                _inc(ins, mqk, 1)
                                g += 1
                    for t in range(8):
                        for nb in range(2):
                            if g >= 2:
                                tensor.wait_ge(eqk.h, b_eqk + g - 1)
                            for k in range(DC):
                                ins = tensor.matmul(psqk[g % 2][:],
                                                    lnT[:, k, t * P:(t + 1) * P],
                                                    WvT[:, k, nb * 512:(nb + 1) * 512],
                                                    start=(k == 0), stop=(k == DC - 1))
                            _inc(ins, mqk, 1)
                            g += 1

                @block.vector
                def _(vector):
                    g = 0
                    for wi, (dstT, bias_sb) in enumerate(((qT, bq_sb), (kT, bk_sb))):
                        for ec in range(DC):
                            for tg in range(2):
                                vector.wait_ge(mqk.h, b_mqk + g + 1)
                                ins = vector.tensor_scalar(
                                    dstT[:, ec, tg * 512:(tg + 1) * 512], psqk[g % 2][:],
                                    bias_sb[:, ec:ec + 1], None, op0=ALU.add)
                                _inc(ins, eqk, 1)
                                g += 1
                    for t in range(8):
                        for nb in range(2):
                            vector.wait_ge(mqk.h, b_mqk + g + 1)
                            ins = vector.tensor_tensor(v_sb[:, t, nb * 512:(nb + 1) * 512],
                                                       psqk[g % 2][:],
                                                       bv_b[:, nb * 512:(nb + 1) * 512],
                                                       op=ALU.add)
                            _inc(ins, eqk, 1)
                            g += 1

            ldx2 = ctx.chn(f"ldx{tag}", 2)
            msc = ctx.ch(f"mq2{tag}")
            vrc = ctx.ch(f"mkv{tag}")
            expd = ctx.ch(f"expd{tag}")
            ptr = ctx.ch(f"ptr{tag}")
            ptev = ctx.ch(f"ptev{tag}")
            mpv = ctx.ch(f"mpv{tag}")
            epo = ctx.ch(f"ropd{tag}")
            cbf = ctx.ch(f"vev{tag}")
            sto2 = ctx.chn("qssqw", 2)
            b_ldx2 = [c.base() for c in ldx2]
            b_msc, b_vrc, b_expd = msc.base(), vrc.base(), expd.base()
            b_ptr, b_ptev, b_mpv, b_epo = ptr.base(), ptev.base(), mpv.base(), epo.base()
            b_cbf = cbf.base()
            b_sto2 = [c.base() for c in sto2]
            with nc.Block() as block:
                @block.sync
                def _(sync):
                    for qt in range(8):
                        if qt >= 2:
                            sync.wait_ge(epo.h, b_epo + 2 * (qt - 1))
                        _inc(sync.dma_start(out=xb2[qt % 2][:],
                                            in_=src_ap[base + qt * P:base + (qt + 1) * P, :]),
                             ldx2[qt % 2], 16)

                @block.tensor
                def _(tensor):
                    for qt in range(8):
                        if qt >= 1:
                            tensor.wait_ge(expd.h, b_expd + 2 * qt)
                        for nb in range(2):
                            for k in range(DC):
                                ins = tensor.matmul(pss[nb][:],
                                                    qT[:, k, qt * P:(qt + 1) * P],
                                                    kT[:, k, nb * 512:(nb + 1) * 512],
                                                    start=(k == 0), stop=(k == DC - 1))
                            _inc(ins, msc, 1)
                        tensor.wait_ge(expd.h, b_expd + 2 * (qt + 1))
                        for k in range(DC):
                            g = qt * DC + k
                            if g >= 4:
                                tensor.wait_ge(ptev.h, b_ptev + g - 3)
                            ins = tensor.transpose(trsl[g % 4][:],
                                                   p_sb[qt % 2][:, k * P:(k + 1) * P], ident[:])
                            _inc(ins, ptr, 1)
                        tensor.wait_ge(ptev.h, b_ptev + DC * (qt + 1))
                        if qt >= 1:
                            tensor.wait_ge(epo.h, b_epo + 2 * qt)
                        for nb in range(2):
                            for k in range(DC):
                                ins = tensor.matmul(pso[nb][:],
                                                    pt_sb[:, k, :],
                                                    v_sb[:, k, nb * 512:(nb + 1) * 512],
                                                    start=(k == 0), stop=(k == DC - 1))
                            _inc(ins, mpv, 1)

                @block.vector
                def _(vector):
                    for qt in range(8):
                        if qt >= 1:
                            vector.wait_ge(epo.h, b_epo + 2 * qt)
                        vector.wait_ge(msc.h, b_msc + 2 * qt + 1)
                        _inc(vector.tensor_reduce(nmt[0][:], pss[0][:], axis=AX.X, op=ALU.max,
                                                  negate=True), vrc, 1)
                        vector.wait_ge(msc.h, b_msc + 2 * qt + 2)
                        _inc(vector.tensor_reduce(nmt[1][:], pss[1][:], axis=AX.X, op=ALU.max,
                                                  negate=True), vrc, 1)
                        vector.wait_ge(vrc.h, vrc.tot)
                        _inc(vector.tensor_tensor(nmt[2][:], nmt[0][:], nmt[1][:], op=ALU.min),
                             vrc, 1)
                        vector.wait_ge(expd.h, b_expd + 2 * (qt + 1))
                        _inc(vector.tensor_tensor(ssm[2][:], ssm[0][:], ssm[1][:], op=ALU.add),
                             vrc, 1)
                        vector.wait_ge(vrc.h, vrc.tot)
                        _inc(vector.reciprocal(rr[:], ssm[2][:]), vrc, 1)
                        if qt >= 1:
                            vector.wait_ge(mpv.h, b_mpv + 2 * qt)
                        for k in range(DC):
                            g = qt * DC + k
                            vector.wait_ge(ptr.h, b_ptr + g + 1)
                            ins = vector.tensor_copy(pt_sb[:, k, :], trsl[g % 4][:])
                            _inc(ins, ptev, 1)
                        vector.wait_ge(ldx2[qt % 2].h, b_ldx2[qt % 2] + 16 * (qt // 2 + 1))
                        if qt >= 2:
                            vector.wait_ge(sto2[qt % 2].h, b_sto2[qt % 2] + 16 * (qt // 2))
                        vector.wait_ge(vrc.h, vrc.tot)
                        for nb in range(2):
                            vector.wait_ge(mpv.h, b_mpv + 2 * qt + nb + 1)
                            sl = slice(nb * 512, (nb + 1) * 512)
                            ins = vector.scalar_tensor_tensor(obuf[qt % 2][:, sl], pso[nb][:],
                                                              rr[:], xb2[qt % 2][:, sl],
                                                              op0=ALU.mult, op1=ALU.add)
                            _inc(ins, epo, 1)

                @block.scalar
                def _(scalar):
                    for qt in range(8):
                        pr = qt % 2
                        scalar.wait_ge(vrc.h, b_vrc + 5 * qt + 3)
                        if qt >= 2:
                            scalar.wait_ge(ptr.h, b_ptr + DC * (qt - 1))
                        _inc(scalar.activation(p_sb[pr][:, 0:512], pss[0][:], AF.Exp,
                                               bias=nmt[2][:], accum_out=ssm[0][:]), expd, 1)
                        _inc(scalar.activation(p_sb[pr][:, 512:1024], pss[1][:], AF.Exp,
                                               bias=nmt[2][:], accum_out=ssm[1][:]), expd, 1)

                @block.gpsimd
                def _(gpsimd):
                    for qt in range(8):
                        gpsimd.wait_ge(epo.h, b_epo + 2 * (qt + 1))
                        _inc(gpsimd.dma_start(out=dst_ap[base + qt * P:base + (qt + 1) * P, :],
                                              in_=obuf[qt % 2][:]), sto2[qt % 2], 16)
                    for q in range(2):
                        gpsimd.wait_ge(sto2[q].h, sto2[q].tot)


def emit_ident(ctx, ident_h):
    nc = ctx.nc
    ident = ctx.sb("identsb", [P, P], BF16)
    ch = ctx.ch("identld")
    b = ch.base()
    with nc.Block(no_gpsimd_drain=True) as block:
        @block.sync
        def _(sync):
            _inc(sync.dma_start(out=ident[:], in_=ident_h[:, :]), ch, 16)
            sync.wait_ge(ch.h, b + 16)
    return ident


_PROG_CACHE = {}


def build_ffn0(parts=("conv", "passA", "passB")):
    ctx = Ctx()
    nc = ctx.nc
    H = 3072
    xin = nc.dram_tensor("xin", [R, D], F32, kind="ExternalInput")
    wg = nc.dram_tensor("wg", [H, D], I32, kind="ExternalInput")
    wu = nc.dram_tensor("wu", [H, D], I32, kind="ExternalInput")
    wd = nc.dram_tensor("wd", [D, H], I32, kind="ExternalInput")
    ident_h = nc.dram_tensor("ident", [P, P], BF16, kind="ExternalInput")
    xout = nc.dram_tensor("xout", [R, D], F32, kind="ExternalOutput")
    hscr = nc.dram_tensor("hscr", [R, H], BF16)
    consts = emit_consts(ctx)
    consts["ident"] = emit_ident(ctx, ident_h)
    wgT = ctx.sb("wgT", [P, DC, H], BF16)
    wuT = ctx.sb("wuT", [P, DC, H], BF16)
    wdT = ctx.sb("wdT", [P, H // P, D], BF16)
    if "conv" in parts:
        emit_conv_int_weight(ctx, wg, H, D, wgT, consts["ident"], "g0")
        emit_conv_int_weight(ctx, wu, H, D, wuT, consts["ident"], "u0")
        emit_conv_int_weight(ctx, wd, D, H, wdT, consts["ident"], "d0")
    tiles = [xin.ap()[i * P:(i + 1) * P, :] for i in range(NT)]
    xd = [(xin.ap()[i * P:(i + 1) * P, :], xout.ap()[i * P:(i + 1) * P, :]) for i in range(NT)]
    if "passA" in parts:
        emit_ffn_passA(ctx, tiles, hscr.ap(), wgT, wuT, H, consts, False, None, "f0")
    if "passB" in parts:
        emit_ffn_passB(ctx, xd, hscr.ap(), wdT, H, consts, False, None, "f0")
    else:
        chz = ctx.ch("dbgst")
        bz = chz.base()
        with nc.Block(no_gpsimd_drain=True) as block:
            @block.gpsimd
            def _(gpsimd):
                _inc(gpsimd.dma_start(out=xout[0:P, :], in_=wgT[:, 0, 0:D]), chz, 16)
                gpsimd.wait_ge(chz.h, bz + 16)
    ctx.close()
    return nc


def build_ffn1():
    ctx = Ctx()
    nc = ctx.nc
    H = 4096
    xin = nc.dram_tensor("xin", [R, D], F32, kind="ExternalInput")
    wg = nc.dram_tensor("wg", [H, D], F32, kind="ExternalInput")
    wu = nc.dram_tensor("wu", [H, D], F32, kind="ExternalInput")
    wd = nc.dram_tensor("wd", [D, H], F32, kind="ExternalInput")
    ident_h = nc.dram_tensor("ident", [P, P], BF16, kind="ExternalInput")
    xout = nc.dram_tensor("xout", [R, D], F32, kind="ExternalOutput")
    hscr = nc.dram_tensor("hscr", [R, H], BF16)
    wdT_s = nc.dram_tensor("wdTs", [P, H // P, D], BF16)
    consts = emit_consts(ctx)
    consts["ident"] = emit_ident(ctx, ident_h)
    qd = {
        "m_g": ctx.sb("m_g", [P, 1], F32),
        "m_u": ctx.sb("m_u", [P, 1], F32),
        "m_d": ctx.sb("m_d", [P, 1], F32),
        "inv_sh_all": ctx.sb("inv_sh_all", [P, NT], F32),
    }
    emit_quant_l1_weight(ctx, wd, D, H, None, consts["ident"], consts, qd["m_d"], "qd",
                         stash_ap=wdT_s.ap())
    es_gu = contextlib.ExitStack()
    ternGT = es_gu.enter_context(nc.sbuf_tensor("ternGT", [P, DC, H], BF16))
    ternUT = es_gu.enter_context(nc.sbuf_tensor("ternUT", [P, DC, H], BF16))
    emit_quant_l1_weight(ctx, wg, H, D, ternGT, consts["ident"], consts, qd["m_g"], "qg")
    emit_quant_l1_weight(ctx, wu, H, D, ternUT, consts["ident"], consts, qd["m_u"], "qu")
    tiles = [xin.ap()[i * P:(i + 1) * P, :] for i in range(NT)]
    emit_ffn_passA(ctx, tiles, hscr.ap(), ternGT, ternUT, H, consts, True, qd, "f1")
    es_gu.close()
    ternDT = ctx.sb("ternDT", [P, H // P, D], BF16)
    ch = ctx.ch("ldDT")
    bldt = ch.base()
    with ctx.nc.Block(no_gpsimd_drain=True) as block:
        @block.sync
        def _(sync):
            _inc(sync.dma_start(out=ternDT[:], in_=wdT_s[:, :, :]), ch, 16)
            sync.wait_ge(ch.h, bldt + 16)
    xd = [(xin.ap()[i * P:(i + 1) * P, :], xout.ap()[i * P:(i + 1) * P, :]) for i in range(NT)]
    emit_ffn_passB(ctx, xd, hscr.ap(), ternDT, H, consts, True, qd, "f1")
    ctx.close()
    return nc


def _ident_np():
    return np.eye(P, dtype=ml_dtypes.bfloat16)


def _x_shards(x):
    out = []
    for c in range(N_CORES):
        b, s = c // 4, (c % 4) * R
        out.append(np.ascontiguousarray(x[b, s:s + R, :], dtype=np.float32))
    return out


def _launch(name, builder, in_maps):
    if name not in _PROG_CACHE:
        _PROG_CACHE[name] = builder()
    nc = _PROG_CACHE[name]
    return run_bass_kernel_spmd(nc, in_maps, core_ids=list(range(N_CORES)))


def run_ffn0(x_shards, wg, wu, wd):
    ident = _ident_np()
    maps = [{"xin": xs, "wg": wg, "wu": wu, "wd": wd, "ident": ident} for xs in x_shards]
    res = _launch("ffn0", build_ffn0, maps)
    return [res.results[c]["xout"] for c in range(N_CORES)]


def run_ffn1(x_shards, wg, wu, wd):
    ident = _ident_np()
    wg = np.ascontiguousarray(wg, dtype=np.float32)
    wu = np.ascontiguousarray(wu, dtype=np.float32)
    wd = np.ascontiguousarray(wd, dtype=np.float32)
    maps = [{"xin": xs, "wg": wg, "wu": wu, "wd": wd, "ident": ident} for xs in x_shards]
    res = _launch("ffn1", build_ffn1, maps)
    return [res.results[c]["xout"] for c in range(N_CORES)]


def build_attn1():
    ctx = Ctx()
    nc = ctx.nc
    xin = nc.dram_tensor("xin", [R, D], F32, kind="ExternalInput")
    Wq = nc.dram_tensor("Wq", [D, D], F32, kind="ExternalInput")
    bq = nc.dram_tensor("bq", [D], F32, kind="ExternalInput")
    Wk = nc.dram_tensor("Wk", [D, D], F32, kind="ExternalInput")
    bk = nc.dram_tensor("bk", [D], F32, kind="ExternalInput")
    Wv = nc.dram_tensor("Wv", [D, D], F32, kind="ExternalInput")
    bv = nc.dram_tensor("bv", [D], F32, kind="ExternalInput")
    ident_h = nc.dram_tensor("ident", [P, P], BF16, kind="ExternalInput")
    xout = nc.dram_tensor("xout", [R, D], F32, kind="ExternalOutput")
    consts = emit_consts(ctx)
    consts["ident"] = emit_ident(ctx, ident_h)
    emit_attn1(ctx, xin.ap(), xout.ap(), Wq, bq, Wk, bk, Wv, bv, consts)
    ctx.close()
    return nc


def run_attn1(x_shards, Wq, bq, Wk, bk, Wv, bv):
    ident = _ident_np()
    maps = [{"xin": xs, "Wq": Wq, "bq": bq, "Wk": Wk, "bk": bk, "Wv": Wv, "bv": bv,
             "ident": ident} for xs in x_shards]
    res = _launch("attn1", build_attn1, maps)
    return [res.results[c]["xout"] for c in range(N_CORES)]


def build_attn0():
    ctx = Ctx()
    nc = ctx.nc
    xin = nc.dram_tensor("xin", [R, D], F32, kind="ExternalInput")
    halo = nc.dram_tensor("halo", [HALO, D], F32, kind="ExternalInput")
    cosr = nc.dram_tensor("cosr", [HALO, 512], F32, kind="ExternalInput")
    sinr = nc.dram_tensor("sinr", [HALO, 512], F32, kind="ExternalInput")
    coskr = nc.dram_tensor("coskr", [HALO, 128], F32, kind="ExternalInput")
    sinkr = nc.dram_tensor("sinkr", [HALO, 128], F32, kind="ExternalInput")
    Wq1 = nc.dram_tensor("Wq1", [D, D], F32, kind="ExternalInput")
    Wq2 = nc.dram_tensor("Wq2", [D, D], F32, kind="ExternalInput")
    Wk1 = nc.dram_tensor("Wk1", [D, 128], F32, kind="ExternalInput")
    Wk2 = nc.dram_tensor("Wk2", [D, 128], F32, kind="ExternalInput")
    Wv = nc.dram_tensor("Wv", [D, 128], F32, kind="ExternalInput")
    Wout = nc.dram_tensor("Wout", [D, D], F32, kind="ExternalInput")
    bout = nc.dram_tensor("bout", [D], F32, kind="ExternalInput")
    lq1 = nc.dram_tensor("lq1", [64], F32, kind="ExternalInput")
    lk1 = nc.dram_tensor("lk1", [64], F32, kind="ExternalInput")
    lq2 = nc.dram_tensor("lq2", [64], F32, kind="ExternalInput")
    lk2 = nc.dram_tensor("lk2", [64], F32, kind="ExternalInput")
    ident_h = nc.dram_tensor("ident", [P, P], BF16, kind="ExternalInput")
    xout = nc.dram_tensor("xout", [R, D], F32, kind="ExternalOutput")
    consts = emit_consts(ctx)
    consts["ident"] = emit_ident(ctx, ident_h)
    emit_attn0(ctx, xin.ap(), xout.ap(), halo.ap(), F32, cosr.ap(), sinr.ap(),
               coskr.ap(), sinkr.ap(),
               Wq1, Wq2, Wk1, Wk2, Wv, Wout, bout, lq1, lk1, lq2, lk2, consts)
    ctx.close()
    return nc


def _rope_tables(cp):
    pos = (2048 * cp + np.arange(HALO, dtype=np.float64))[:, None]
    div = np.exp(np.arange(0, 64, 2, dtype=np.float64) * (-np.log(10000.0) / 64.0))
    cos = np.cos(pos * div).astype(np.float32)
    sin = np.sin(pos * div).astype(np.float32)
    cosq = (np.tile(cos, (1, 16)) * np.float32(0.125)).copy()
    sinq = (np.tile(sin, (1, 16)) * np.float32(0.125)).copy()
    cosk = np.tile(cos, (1, 4)).copy()
    sink = np.tile(sin, (1, 4)).copy()
    return cosq, sinq, cosk, sink


def run_attn0(x_shards, x_full_per_batch, ins):
    ident = _ident_np()
    maps = []
    for c in range(N_CORES):
        b, cp = c // 4, c % 4
        halo = np.ascontiguousarray(x_full_per_batch[b][2048 * cp:2048 * cp + HALO, :],
                                    dtype=np.float32)
        cosr, sinr, coskr, sinkr = _rope_tables(cp)
        maps.append({"xin": x_shards[c], "halo": halo, "cosr": cosr, "sinr": sinr,
                     "coskr": coskr, "sinkr": sinkr,
                     "Wq1": ins["L0_attn_Wq1"], "Wq2": ins["L0_attn_Wq2"],
                     "Wk1": ins["L0_attn_Wk1"], "Wk2": ins["L0_attn_Wk2"],
                     "Wv": ins["L0_attn_Wv"], "Wout": ins["L0_attn_Wout"],
                     "bout": ins["L0_attn_bout"], "lq1": ins["L0_attn_lq1"],
                     "lk1": ins["L0_attn_lk1"], "lq2": ins["L0_attn_lq2"],
                     "lk2": ins["L0_attn_lk2"], "ident": ident})
    res = _launch("attn0", build_attn0, maps)
    return [res.results[c]["xout"] for c in range(N_CORES)]


def kernel(**inputs):
    if not _UNFUSED:
        return kernel_fused(inputs)
    return _kernel_unfused(**inputs)


_UNFUSED = False


def _kernel_unfused(**inputs):
    x = np.ascontiguousarray(np.asarray(inputs["L0_attn_x"]), dtype=np.float32)
    shards = _x_shards(x)
    wg0 = np.ascontiguousarray(np.asarray(inputs["L0_ffn_w_gate"]), dtype=np.int32)
    wu0 = np.ascontiguousarray(np.asarray(inputs["L0_ffn_w_up"]), dtype=np.int32)
    wd0 = np.ascontiguousarray(np.asarray(inputs["L0_ffn_w_down"]), dtype=np.int32)

    s1 = run_ffn0(shards, wg0, wu0, wd0)
    x1b = [np.concatenate(s1[0:4], 0), np.concatenate(s1[4:8], 0)]
    ins_np = {k: np.ascontiguousarray(np.asarray(v)) for k, v in inputs.items()
              if k.startswith("L0_attn_")}
    s2 = run_attn0(s1, x1b, ins_np)
    s3 = run_ffn0(s2, wg0, wu0, wd0)
    s4 = run_ffn1(s3, np.asarray(inputs["L1_ffn_w_gate"]),
                  np.asarray(inputs["L1_ffn_w_up"]), np.asarray(inputs["L1_ffn_w_down"]))
    s5 = run_attn1(s4, np.ascontiguousarray(np.asarray(inputs["L1_attn_Wq"]), dtype=np.float32),
                   np.ascontiguousarray(np.asarray(inputs["L1_attn_bq"]), dtype=np.float32),
                   np.ascontiguousarray(np.asarray(inputs["L1_attn_Wk"]), dtype=np.float32),
                   np.ascontiguousarray(np.asarray(inputs["L1_attn_bk"]), dtype=np.float32),
                   np.ascontiguousarray(np.asarray(inputs["L1_attn_Wv"]), dtype=np.float32),
                   np.ascontiguousarray(np.asarray(inputs["L1_attn_bv"]), dtype=np.float32))
    s6 = run_ffn1(s5, np.asarray(inputs["L1_ffn_w_gate"]),
                  np.asarray(inputs["L1_ffn_w_up"]), np.asarray(inputs["L1_ffn_w_down"]))
    out = np.stack([np.concatenate(s6[0:4], 0), np.concatenate(s6[4:8], 0)], 0)
    return out


def emit_load_sbuf(ctx, dst_sb, src_ap, tag):
    ch = ctx.ch(f"lds{tag}")
    b = ch.base()
    with ctx.nc.Block(no_gpsimd_drain=True) as block:
        @block.sync
        def _(sync):
            _inc(sync.dma_start(out=dst_sb[:], in_=src_ap), ch, 16)
            sync.wait_ge(ch.h, b + 16)


def build_fused():
    ctx = Ctx()
    nc = ctx.nc
    H0, H1 = 3072, 4096
    ein = lambda n, s, d: nc.dram_tensor(n, s, d, kind="ExternalInput")
    xin = ein("xin", [R, D], F32)
    xhalo = ein("xhalo", [HALO, D], F32)
    cosr = ein("cosr", [HALO, 512], F32)
    sinr = ein("sinr", [HALO, 512], F32)
    coskr = ein("coskr", [HALO, 128], F32)
    sinkr = ein("sinkr", [HALO, 128], F32)
    ident_h = ein("ident", [P, P], BF16)
    wg0 = ein("wg0", [H0, D], I32)
    wu0 = ein("wu0", [H0, D], I32)
    wd0 = ein("wd0", [D, H0], I32)
    Wq1 = ein("Wq1", [D, D], F32)
    Wq2 = ein("Wq2", [D, D], F32)
    Wk1 = ein("Wk1", [D, 128], F32)
    Wk2 = ein("Wk2", [D, 128], F32)
    Wv0 = ein("Wv0", [D, 128], F32)
    Wout = ein("Wout", [D, D], F32)
    bout = ein("bout", [D], F32)
    lq1 = ein("lq1", [64], F32)
    lk1 = ein("lk1", [64], F32)
    lq2 = ein("lq2", [64], F32)
    lk2 = ein("lk2", [64], F32)
    wg1 = ein("wg1", [H1, D], F32)
    wu1 = ein("wu1", [H1, D], F32)
    wd1 = ein("wd1", [D, H1], F32)
    Wq = ein("Wq", [D, D], F32)
    bq = ein("bq", [D], F32)
    Wk = ein("Wk", [D, D], F32)
    bk = ein("bk", [D], F32)
    Wv = ein("Wv", [D, D], F32)
    bv = ein("bv", [D], F32)
    xout = nc.dram_tensor("xout", [R, D], F32, kind="ExternalOutput")

    resid = nc.dram_tensor("resid", [R, D], F32)
    halo_scr = nc.dram_tensor("haloscr", [HALO, D], F32)
    hscr0 = nc.dram_tensor("hscr0", [R + HALO, H0], BF16)
    hscr1 = nc.dram_tensor("hscr1", [R, H1], BF16)
    wgT_s = nc.dram_tensor("wgTs", [P, DC, H0], BF16)
    wuT_s = nc.dram_tensor("wuTs", [P, DC, H0], BF16)
    wdT_s = nc.dram_tensor("wdTs", [P, H0 // P, D], BF16)
    tG_s = nc.dram_tensor("tGs", [P, DC, H1], BF16)
    tU_s = nc.dram_tensor("tUs", [P, DC, H1], BF16)
    tD_s = nc.dram_tensor("tDs", [P, H1 // P, D], BF16)

    consts = emit_consts(ctx)
    consts["ident"] = emit_ident(ctx, ident_h)
    qd = {
        "m_g": ctx.sb("m_g", [P, 1], F32),
        "m_u": ctx.sb("m_u", [P, 1], F32),
        "m_d": ctx.sb("m_d", [P, 1], F32),
        "inv_sh_all": ctx.sb("inv_sh_all", [P, NT], F32),
    }

    def own_tiles(t_ap):
        return [t_ap[i * P:(i + 1) * P, :] for i in range(NT)]

    def halo_tiles(t_ap):
        return [t_ap[i * P:(i + 1) * P, :] for i in range(NHT)]

    with contextlib.ExitStack() as es1:
        wgT = es1.enter_context(nc.sbuf_tensor("wgT1", [P, DC, H0], BF16))
        wuT = es1.enter_context(nc.sbuf_tensor("wuT1", [P, DC, H0], BF16))
        wdT = es1.enter_context(nc.sbuf_tensor("wdT1", [P, H0 // P, D], BF16))
        emit_conv_int_weight(ctx, wg0, H0, D, wgT, consts["ident"], "cv", stash_ap=wgT_s.ap())
        emit_conv_int_weight(ctx, wu0, H0, D, wuT, consts["ident"], "cv", stash_ap=wuT_s.ap())
        emit_conv_int_weight(ctx, wd0, D, H0, wdT, consts["ident"], "cv", stash_ap=wdT_s.ap())
        src1 = own_tiles(xin.ap()) + halo_tiles(xhalo.ap())
        emit_ffn_passA(ctx, src1, hscr0.ap(), wgT, wuT, H0, consts, False, None, "f")
        xd1 = ([(xin.ap()[i * P:(i + 1) * P, :], resid.ap()[i * P:(i + 1) * P, :])
                for i in range(NT)] +
               [(xhalo.ap()[i * P:(i + 1) * P, :], halo_scr.ap()[i * P:(i + 1) * P, :])
                for i in range(NHT)])
        emit_ffn_passB(ctx, xd1, hscr0.ap(), wdT, H0, consts, False, None, "f")

    emit_attn0(ctx, resid.ap(), resid.ap(), halo_scr.ap(), F32, cosr.ap(), sinr.ap(),
               coskr.ap(), sinkr.ap(),
               Wq1, Wq2, Wk1, Wk2, Wv0, Wout, bout, lq1, lk1, lq2, lk2, consts, tag="a0")

    with contextlib.ExitStack() as es3:
        wgT = es3.enter_context(nc.sbuf_tensor("wgT3", [P, DC, H0], BF16))
        wuT = es3.enter_context(nc.sbuf_tensor("wuT3", [P, DC, H0], BF16))
        wdT = es3.enter_context(nc.sbuf_tensor("wdT3", [P, H0 // P, D], BF16))
        emit_load_sbuf(ctx, wgT, wgT_s.ap(), "w")
        emit_load_sbuf(ctx, wuT, wuT_s.ap(), "w")
        emit_load_sbuf(ctx, wdT, wdT_s.ap(), "w")
        emit_ffn_passA(ctx, own_tiles(resid.ap()), hscr0.ap(), wgT, wuT, H0, consts,
                       False, None, "f")
        xd3 = [(resid.ap()[i * P:(i + 1) * P, :], resid.ap()[i * P:(i + 1) * P, :])
               for i in range(NT)]
        emit_ffn_passB(ctx, xd3, hscr0.ap(), wdT, H0, consts, False, None, "f")

    emit_quant_l1_weight(ctx, wd1, D, H1, None, consts["ident"], consts, qd["m_d"], "qw",
                         stash_ap=tD_s.ap())
    with contextlib.ExitStack() as es4:
        ternGT = es4.enter_context(nc.sbuf_tensor("tG4", [P, DC, H1], BF16))
        ternUT = es4.enter_context(nc.sbuf_tensor("tU4", [P, DC, H1], BF16))
        emit_quant_l1_weight(ctx, wg1, H1, D, ternGT, consts["ident"], consts, qd["m_g"],
                             "qw", stash_ap=tG_s.ap())
        emit_quant_l1_weight(ctx, wu1, H1, D, ternUT, consts["ident"], consts, qd["m_u"],
                             "qw", stash_ap=tU_s.ap())
        emit_ffn_passA(ctx, own_tiles(resid.ap()), hscr1.ap(), ternGT, ternUT, H1, consts,
                       True, qd, "f")
    with contextlib.ExitStack() as es4b:
        ternDT = es4b.enter_context(nc.sbuf_tensor("tD4", [P, H1 // P, D], BF16))
        emit_load_sbuf(ctx, ternDT, tD_s.ap(), "w")
        xd4 = [(resid.ap()[i * P:(i + 1) * P, :], resid.ap()[i * P:(i + 1) * P, :])
               for i in range(NT)]
        emit_ffn_passB(ctx, xd4, hscr1.ap(), ternDT, H1, consts, True, qd, "f")

    emit_attn1(ctx, resid.ap(), resid.ap(), Wq, bq, Wk, bk, Wv, bv, consts, tag="a0")

    with contextlib.ExitStack() as es6:
        ternGT = es6.enter_context(nc.sbuf_tensor("tG6", [P, DC, H1], BF16))
        ternUT = es6.enter_context(nc.sbuf_tensor("tU6", [P, DC, H1], BF16))
        emit_load_sbuf(ctx, ternGT, tG_s.ap(), "w")
        emit_load_sbuf(ctx, ternUT, tU_s.ap(), "w")
        emit_ffn_passA(ctx, own_tiles(resid.ap()), hscr1.ap(), ternGT, ternUT, H1, consts,
                       True, qd, "f")
    with contextlib.ExitStack() as es6b:
        ternDT = es6b.enter_context(nc.sbuf_tensor("tD6", [P, H1 // P, D], BF16))
        emit_load_sbuf(ctx, ternDT, tD_s.ap(), "w")
        xd6 = [(resid.ap()[i * P:(i + 1) * P, :], xout.ap()[i * P:(i + 1) * P, :])
               for i in range(NT)]
        emit_ffn_passB(ctx, xd6, hscr1.ap(), ternDT, H1, consts, True, qd, "f")
    ctx.close()
    return nc


def kernel_fused(inputs):
    x = np.ascontiguousarray(np.asarray(inputs["L0_attn_x"]), dtype=np.float32)
    shards = _x_shards(x)
    ident = _ident_np()
    f32 = lambda k: np.ascontiguousarray(np.asarray(inputs[k]), dtype=np.float32)
    i32 = lambda k: np.ascontiguousarray(np.asarray(inputs[k]), dtype=np.int32)
    common = {
        "ident": ident,
        "wg0": i32("L0_ffn_w_gate"), "wu0": i32("L0_ffn_w_up"), "wd0": i32("L0_ffn_w_down"),
        "Wq1": f32("L0_attn_Wq1"), "Wq2": f32("L0_attn_Wq2"), "Wk1": f32("L0_attn_Wk1"),
        "Wk2": f32("L0_attn_Wk2"), "Wv0": f32("L0_attn_Wv"), "Wout": f32("L0_attn_Wout"),
        "bout": f32("L0_attn_bout"), "lq1": f32("L0_attn_lq1"), "lk1": f32("L0_attn_lk1"),
        "lq2": f32("L0_attn_lq2"), "lk2": f32("L0_attn_lk2"),
        "wg1": f32("L1_ffn_w_gate"), "wu1": f32("L1_ffn_w_up"), "wd1": f32("L1_ffn_w_down"),
        "Wq": f32("L1_attn_Wq"), "bq": f32("L1_attn_bq"), "Wk": f32("L1_attn_Wk"),
        "bk": f32("L1_attn_bk"), "Wv": f32("L1_attn_Wv"), "bv": f32("L1_attn_bv"),
    }
    maps = []
    for c in range(N_CORES):
        b, cp = c // 4, c % 4
        halo = np.ascontiguousarray(x[b, 2048 * cp:2048 * cp + HALO, :], dtype=np.float32)
        cosr, sinr, coskr, sinkr = _rope_tables(cp)
        m = dict(common)
        m.update({"xin": shards[c], "xhalo": halo, "cosr": cosr, "sinr": sinr,
                  "coskr": coskr, "sinkr": sinkr})
        maps.append(m)
    res = _launch("fused", build_fused, maps)
    outs = [res.results[c]["xout"] for c in range(N_CORES)]
    return np.stack([np.concatenate(outs[0:4], 0), np.concatenate(outs[4:8], 0)], 0)
